# version 02
# baseline (speedup 1.0000x reference)
import contextlib
import numpy as np

import concourse.bass as bass
import concourse.mybir as mybir
from concourse.bass_utils import run_bass_kernel_spmd

F32 = mybir.dt.float32
BF16 = mybir.dt.bfloat16
U32 = mybir.dt.uint32
I32 = mybir.dt.int32
AF = mybir.ActivationFunctionType
ALU = mybir.AluOpType
AX = mybir.AxisListType

D = 2048
NCH = 16
B = 2
S = 4096
NTOK = 1024
NCORE = 8
HALF = 512
EPS_LN = 1e-5
EPS_RMS = 1e-6
NH = 16
KVH = 4
HD = 128
HG = 12
RK = 32
SYM = 4
E = 8
CAP = 1024
HMOE = 2816
BIG = 1.0e7

ENGINES = ("sync", "vector", "scalar", "tensor", "gpsimd")


class Prog:

    def __init__(self, nc, es):
        self.nc = nc
        self.es = es
        self.steps = {e: [] for e in ENGINES}
        self.sem = {e: es.enter_context(nc.semaphore(f"ms_{e}")) for e in ENGINES}
        self.cnt = {e: 0 for e in ENGINES}

    def step(self, eng, waits=None, incs=0):
        w = dict(waits or {})
        def deco(fn):
            self.steps[eng].append((fn, w))
            self.cnt[eng] += incs
            return None
        return deco

    def now(self, eng):
        return self.cnt[eng]

    def run(self):
        block = self.es.enter_context(self.nc.Block())
        for eng in ENGINES:
            steps = self.steps[eng]
            sems = self.sem

            def make(eng, steps):
                def body(h):
                    mysem = sems[eng]
                    for fn, waits in steps:
                        for se, v in waits.items():
                            h.wait_ge(sems[se], v)
                        fn(h, mysem)
                return body
            getattr(block, eng)(make(eng, steps))


def sb(nc, es, name, shape, dtype):
    return es.enter_context(nc.sbuf_tensor(name, shape, dtype))


def ps(nc, es, name, shape, dtype=F32):
    return es.enter_context(nc.psum_tensor(name, shape, dtype))



def emit_dma_batch(P, eng, pairs, waits=None):
    n = len(pairs)

    @P.step(eng, waits=waits, incs=16 * n)
    def _(h, sem):
        for out, in_ in pairs:
            h.dma_start(out=out, in_=in_).then_inc(sem, 16)
    return P.now(eng)


def emit_ln(P, T, x_sb, out_sb, consts, scratch, gamma_beta=None, out2_sb=None,
            eps=EPS_LN):
    nc = P.nc
    sq2 = scratch["sq2"]
    p_sum, p_sq = scratch["p_sum"], scratch["p_sq"]
    p_mb, p_rb = scratch["p_mb"], scratch["p_rb"]
    r_mean, r_e2, r_m2, r_var, r_sd, r_rstd = (scratch[k] for k in
        ("r_mean", "r_e2", "r_m2", "r_var", "r_sd", "r_rstd"))
    ones128_f = consts["ones128_f"]
    ones1_f = consts["ones1_f"]

    for half in range(2):
        cols = slice(half * HALF, (half + 1) * HALF)
        for k in range(NCH):
            kb = k % 2
            if k == 0:
                swaits = {"vector": P.now("vector")}
            elif k >= 2:
                swaits = {"tensor": P.now("tensor")}
            else:
                swaits = {}

            @P.step("scalar", waits=swaits, incs=1)
            def _(h, sem, k=k, kb=kb, cols=cols):
                h.activation(sq2[kb][:], x_sb[:, k, cols], AF.Square,
                             bias=0.0, scale=1.0).then_inc(sem, 1)

            @P.step("tensor", waits={"scalar": P.now("scalar")}, incs=1)
            def _(h, sem, k=k, kb=kb, cols=cols):
                h.matmul(p_sum[:], ones128_f[:], x_sb[:, k, cols],
                         start=(k == 0), stop=(k == NCH - 1))
                h.matmul(p_sq[:], ones128_f[:], sq2[kb][:],
                         start=(k == 0), stop=(k == NCH - 1)).then_inc(sem, 1)
        @P.step("vector", waits={"tensor": P.now("tensor")}, incs=1)
        def _(h, sem, cols=cols):
            h.tensor_scalar(r_mean[:], p_sum[:], 1.0 / D, None, ALU.mult)
            h.tensor_scalar(r_e2[:], p_sq[:], 1.0 / D, None, ALU.mult)
            h.drain()
            h.tensor_tensor(r_m2[:], r_mean[:], r_mean[:], ALU.mult)
            h.drain()
            h.tensor_tensor(r_var[:], r_e2[:], r_m2[:], ALU.subtract)
            h.drain()
            h.tensor_scalar(r_m2[:], r_var[:], eps, None, ALU.add)
            h.drain().then_inc(sem, 1)

        @P.step("scalar", waits={"vector": P.now("vector")}, incs=1)
        def _(h, sem):
            h.activation(r_sd[:], r_m2[:], AF.Sqrt, bias=0.0, scale=1.0).then_inc(sem, 1)

        @P.step("vector", waits={"scalar": P.now("scalar")}, incs=1)
        def _(h, sem):
            h.reciprocal(r_rstd[:], r_sd[:]).then_inc(sem, 1)

        @P.step("tensor", waits={"vector": P.now("vector")}, incs=1)
        def _(h, sem):
            h.matmul(p_mb[:], ones1_f[:], r_mean[:], start=True, stop=True)
            h.matmul(p_rb[:], ones1_f[:], r_rstd[:], start=True, stop=True).then_inc(sem, 1)

        tmb = P.now("tensor")
        for k in range(NCH):
            @P.step("vector", waits={"tensor": tmb}, incs=1)
            def _(h, sem, k=k, cols=cols):
                tmp = scratch["tmp"]
                h.tensor_tensor(tmp[:], x_sb[:, k, cols], p_mb[:], ALU.subtract)
                if gamma_beta is None:
                    h.tensor_tensor(out_sb[:, k, cols], tmp[:], p_rb[:],
                                    ALU.mult).then_inc(sem, 1)
                else:
                    g_col, b_col = gamma_beta
                    h.tensor_tensor(tmp[:], tmp[:], p_rb[:], ALU.mult)
                    h.tensor_scalar(out_sb[:, k, cols], tmp[:],
                                    g_col[:, k:k+1], b_col[:, k:k+1],
                                    ALU.mult, ALU.add).then_inc(sem, 1)
        vnow = P.now("vector")

        @P.step("tensor", waits={"vector": vnow}, incs=0)
        def _(h, sem):
            pass



def build_p1():
    nc = bass.Bass("TRN2", target_bir_lowering=False)
    es = contextlib.ExitStack()
    ap = {}
    def din(name, shape, dtype=F32):
        t = nc.dram_tensor(name, shape, dtype, kind="ExternalInput").ap()
        ap[name] = t
        return t
    def dout(name, shape, dtype=F32):
        t = nc.dram_tensor(name, shape, dtype, kind="ExternalOutput").ap()
        ap[name] = t
        return t

    x_t = din("x_t", [D, NTOK])
    gamma_t = din("gamma_t", [128, NCH])
    beta_t = din("beta_t", [128, NCH])
    w_in = din("w_in", [D, 8])
    b_in = din("b_in", [8, 1])
    c_ones128_f = din("c_ones128_f", [128, 1])
    c_ones1_f = din("c_ones1_f", [1, 128])

    u_out = dout("u_out", [8, NTOK])
    xln1_out = dout("xln1_out", [D, NTOK])
    xn_out = dout("xn_out", [D, NTOK], BF16)

    with es:
        xs = sb(nc, es, "xs", [128, NCH, NTOK], F32)
        xln1 = sb(nc, es, "xln1", [128, NCH, NTOK], F32)
        xnb = sb(nc, es, "xnb", [128, NCH, NTOK], BF16)
        g_col = sb(nc, es, "g_col", [128, NCH], F32)
        b_col = sb(nc, es, "b_col", [128, NCH], F32)
        winb = sb(nc, es, "winb", [128, NCH, 8], BF16)
        binv = sb(nc, es, "binv", [8, 1], F32)
        ones128_f = sb(nc, es, "ones128_f", [128, 1], F32)
        ones1_f = sb(nc, es, "ones1_f", [1, 128], F32)
        u_sb = sb(nc, es, "u_sb", [8, NTOK], F32)
        scratch = dict(
            sq2=[sb(nc, es, f"sq{i}", [128, HALF], F32) for i in range(2)],
            tmp=sb(nc, es, "tmp", [128, HALF], F32),
            p_sum=ps(nc, es, "p_sum", [1, HALF]),
            p_sq=ps(nc, es, "p_sq", [1, HALF]),
            p_mb=ps(nc, es, "p_mb", [128, HALF]),
            p_rb=ps(nc, es, "p_rb", [128, HALF]),
        )
        for nm in ("r_mean", "r_e2", "r_m2", "r_var", "r_sd", "r_rstd"):
            scratch[nm] = sb(nc, es, nm, [1, HALF], F32)
        p_u = ps(nc, es, "p_u", [8, HALF])

        P = Prog(nc, es)
        consts = dict(ones128_f=ones128_f, ones1_f=ones1_f)

        ld = emit_dma_batch(P, "sync", [
            (xs[:], x_t[:, :].rearrange("(k p) t -> p k t", p=128)),
            (g_col[:], gamma_t[:, :]),
            (b_col[:], beta_t[:, :]),
            (binv[:], b_in[:, :]),
            (ones128_f[:], c_ones128_f[:, :]),
            (ones1_f[:], c_ones1_f[:, :]),
        ])
        emit_dma_batch(P, "gpsimd", [
            (winb[:], w_in[:, :].rearrange("(k p) r -> p k r", p=128)),
        ])
        gw = P.now("gpsimd")

        for e in ("scalar", "tensor", "vector"):
            @P.step(e, waits={"sync": ld, "gpsimd": gw}, incs=0)
            def _(h, sem):
                pass

        emit_ln(P, None, xs, xln1, consts, scratch, gamma_beta=None)
        emit_ln(P, None, xln1, xnb, consts, scratch, gamma_beta=(g_col, b_col))

        for half in range(2):
            cols = slice(half * HALF, (half + 1) * HALF)
            for k in range(NCH):
                @P.step("tensor", waits={"vector": P.now("vector")}, incs=(1 if k == NCH - 1 else 0))
                def _(h, sem, k=k, cols=cols):
                    ins = h.matmul(p_u[:], winb[:, k, :], xnb[:, k, cols],
                                   start=(k == 0), stop=(k == NCH - 1))
                    if k == NCH - 1:
                        ins.then_inc(sem, 1)

            @P.step("vector", waits={"tensor": P.now("tensor")}, incs=1)
            def _(h, sem, cols=cols):
                h.tensor_scalar(u_sb[:, cols], p_u[:], binv[:], None,
                                ALU.add).then_inc(sem, 1)

        vs = P.now("vector")
        st = emit_dma_batch(P, "gpsimd", [
            (u_out[:, :], u_sb[:]),
            (xln1_out[:, :].rearrange("(k p) t -> p k t", p=128), xln1[:]),
            (xn_out[:, :].rearrange("(k p) t -> p k t", p=128), xnb[:]),
        ], waits={"vector": vs})

        @P.step("gpsimd", waits={"gpsimd": st}, incs=0)
        def _(h, sem):
            pass

        P.run()
    return nc



def emit_ln_dram(P, x_dram, out_sb, consts, scratch, ld2, gamma_beta=None,
                  eps=EPS_LN):
    nc = P.nc
    sq2 = scratch["sq2"]
    p_sum, p_sq = scratch["p_sum"], scratch["p_sq"]
    p_mb, p_rb = scratch["p_mb"], scratch["p_rb"]
    r_mean, r_m2, r_e2, r_var, r_sd, r_rstd = (scratch[k] for k in
        ("r_mean", "r_m2", "r_e2", "r_var", "r_sd", "r_rstd"))
    ones128_f = consts["ones128_f"]
    ones1_f = consts["ones1_f"]
    src = x_dram[:, :].rearrange("(k p) t -> p k t", p=128)
    stash_m, stash_r = scratch["stash_m"], scratch["stash_r"]

    for half in range(2):
        cols = slice(half * HALF, (half + 1) * HALF)
        for k in range(NCH):
            kb = k % 2
            i = half * NCH + k
            wconsume = {"tensor": P.now("tensor")} if i >= 2 else {"vector": P.now("vector")}
            @P.step("sync", waits=wconsume, incs=16)
            def _(h, sem, kb=kb, k=k, cols=cols):
                h.dma_start(out=ld2[kb][:], in_=src[:, k, cols]).then_inc(sem, 16)
            lv = P.now("sync")

            @P.step("scalar", waits={"sync": lv, "tensor": P.now("tensor")}, incs=1)
            def _(h, sem, kb=kb):
                h.activation(sq2[kb][:], ld2[kb][:], AF.Square,
                             bias=0.0, scale=1.0).then_inc(sem, 1)

            @P.step("tensor", waits={"scalar": P.now("scalar")}, incs=1)
            def _(h, sem, kb=kb, k=k):
                h.matmul(p_sum[:], ones128_f[:], ld2[kb][:],
                         start=(k == 0), stop=(k == NCH - 1))
                h.matmul(p_sq[:], ones128_f[:], sq2[kb][:],
                         start=(k == 0), stop=(k == NCH - 1)).then_inc(sem, 1)

        @P.step("vector", waits={"tensor": P.now("tensor")}, incs=1)
        def _(h, sem, half=half):
            h.tensor_scalar(r_mean[:], p_sum[:], 1.0 / D, None, ALU.mult)
            h.tensor_scalar(r_e2[:], p_sq[:], 1.0 / D, None, ALU.mult)
            h.drain()
            h.tensor_tensor(r_m2[:], r_mean[:], r_mean[:], ALU.mult)
            h.drain()
            h.tensor_tensor(r_var[:], r_e2[:], r_m2[:], ALU.subtract)
            h.drain()
            h.tensor_scalar(r_m2[:], r_var[:], eps, None, ALU.add)
            h.drain().then_inc(sem, 1)

        @P.step("scalar", waits={"vector": P.now("vector")}, incs=1)
        def _(h, sem):
            h.activation(r_sd[:], r_m2[:], AF.Sqrt, bias=0.0, scale=1.0).then_inc(sem, 1)

        @P.step("vector", waits={"scalar": P.now("scalar")}, incs=1)
        def _(h, sem, half=half):
            h.reciprocal(r_rstd[:], r_sd[:])
            h.tensor_copy(stash_m[0:1, half*HALF:(half+1)*HALF], r_mean[:]).then_inc(sem, 1)

        @P.step("vector", waits={"vector": P.now("vector")}, incs=1)
        def _(h, sem, half=half):
            h.tensor_copy(stash_r[0:1, half*HALF:(half+1)*HALF], r_rstd[:]).then_inc(sem, 1)

    for half in range(2):
        cols = slice(half * HALF, (half + 1) * HALF)
        @P.step("tensor", waits={"vector": P.now("vector")}, incs=1)
        def _(h, sem, half=half):
            h.matmul(p_mb[:], ones1_f[:], stash_m[0:1, half*HALF:(half+1)*HALF],
                     start=True, stop=True)
            h.matmul(p_rb[:], ones1_f[:], stash_r[0:1, half*HALF:(half+1)*HALF],
                     start=True, stop=True).then_inc(sem, 1)
        tb = P.now("tensor")
        for k in range(NCH):
            kb = k % 2
            i = half * NCH + k
            wconsume = {"vector": P.now("vector")} if i >= 2 else {}
            wconsume["tensor"] = tb
            @P.step("sync", waits=wconsume, incs=16)
            def _(h, sem, kb=kb, k=k, cols=cols):
                h.dma_start(out=ld2[kb][:], in_=src[:, k, cols]).then_inc(sem, 16)
            lv = P.now("sync")

            @P.step("vector", waits={"sync": lv, "tensor": tb}, incs=1)
            def _(h, sem, kb=kb, k=k, cols=cols):
                tmp = scratch["tmp"]
                h.tensor_tensor(tmp[:], ld2[kb][:], p_mb[:], ALU.subtract)
                if gamma_beta is None:
                    h.tensor_tensor(out_sb[:, k, cols], tmp[:], p_rb[:],
                                    ALU.mult).then_inc(sem, 1)
                else:
                    g_col, b_col = gamma_beta
                    h.tensor_tensor(tmp[:], tmp[:], p_rb[:], ALU.mult)
                    h.tensor_scalar(out_sb[:, k, cols], tmp[:],
                                    g_col[:, k:k+1], b_col[:, k:k+1],
                                    ALU.mult, ALU.add).then_inc(sem, 1)


def build_p2():
    nc = bass.Bass("TRN2", target_bir_lowering=False)
    es = contextlib.ExitStack()
    def din(name, shape, dtype=F32):
        return nc.dram_tensor(name, shape, dtype, kind="ExternalInput").ap()
    def dout(name, shape, dtype=F32):
        return nc.dram_tensor(name, shape, dtype, kind="ExternalOutput").ap()

    x_t = din("x_t", [D, NTOK])
    xln1_in = din("xln1_in", [D, NTOK])
    xn_in = din("xn_in", [D, NTOK], BF16)
    u_all = din("u_all", [8, S])
    dlogit = din("dlogit", [8, 1])
    w_out = din("w_out", [8, D])
    b_out_t = din("b_out_t", [128, NCH])
    d_skip_t = din("d_skip_t", [128, NCH])
    gamma_t = din("gamma_t", [128, NCH])
    beta_t = din("beta_t", [128, NCH])
    w_down = din("w_down", [D, 32])
    b_down = din("b_down", [32, 1])
    w_up = din("w_up", [32, D])
    b_up_t = din("b_up_t", [128, NCH])
    a_t = din("a_t", [D, 512])
    bw_t = din("bw_t", [512, D])
    wg0_t = din("wg0_t", [D, 8])
    wq_t = din("wq_t", [D, D])
    wk_t = din("wk_t", [D, 512])
    wv_t = din("wv_t", [D, 512])
    wrq_t = din("wrq_t", [D, 384])
    wrk_t = din("wrk_t", [D, 384])
    qnw = din("qnw", [128, 1])
    knw = din("knw", [128, 1])
    cosq = din("cosq", [128, NTOK])
    sinq = din("sinq", [128, NTOK])
    c_ones128_f = din("c_ones128_f", [128, 1])
    c_ones1_f = din("c_ones1_f", [1, 128])
    c_ones128x8 = din("c_ones128x8", [128, 8])
    c_ident_b = din("c_ident_b", [128, 128], BF16)
    c_ind4 = din("c_ind4", [128, 512], BF16)
    c_msel = din("c_msel", [8, 512], BF16)
    c_hgsel = din("c_hgsel", [128, 4])

    x2_out = dout("x2_out", [D, NTOK])
    q_out = dout("q_out", [D, NTOK], BF16)
    k_out = dout("k_out", [512, NTOK], BF16)
    v_out = dout("v_out", [NTOK, 512], BF16)
    rq_out = dout("rq_out", [384, NTOK], BF16)
    rk_out = dout("rk_out", [384, NTOK], BF16)
    qn2_out = dout("qn2_out", [NH, NTOK])
    kn2_out = dout("kn2_out", [KVH, NTOK])
    rqn2_out = dout("rqn2_out", [12, NTOK])
    rkn2_out = dout("rkn2_out", [12, NTOK])
    xi_dram = nc.dram_tensor("xi_scr", [D, NTOK], F32, kind="Internal").ap()
    x1_dram = nc.dram_tensor("x1_scr", [D, NTOK], F32, kind="Internal").ap()

    with es:
        bufC = sb(nc, es, "bufC", [128, NCH, NTOK], BF16)
        wqs2 = [sb(nc, es, f"wqs{i}", [128, NCH, 128], BF16) for i in range(2)]
        woutb = sb(nc, es, "woutb", [8, NCH, 128], BF16)
        wdownb = sb(nc, es, "wdownb", [128, NCH, 32], BF16)
        wupb = sb(nc, es, "wupb", [32, NCH, 128], BF16)
        wg0b = sb(nc, es, "wg0b", [128, NCH, 8], BF16)
        bout_c = sb(nc, es, "bout_c", [128, NCH], F32)
        dskip_c = sb(nc, es, "dskip_c", [128, NCH], F32)
        g_col = sb(nc, es, "g_col", [128, NCH], F32)
        b_col = sb(nc, es, "b_col", [128, NCH], F32)
        bdown_v = sb(nc, es, "bdown_v", [32, 1], F32)
        bup_c = sb(nc, es, "bup_c", [128, NCH], F32)
        qnw_v = sb(nc, es, "qnw_v", [128, 1], F32)
        knw_v = sb(nc, es, "knw_v", [128, 1], F32)
        wqsc = sb(nc, es, "wqsc", [128, 1], F32)
        cos_sb = sb(nc, es, "cos_sb", [128, NTOK], F32)
        sin_sb = sb(nc, es, "sin_sb", [128, NTOK], F32)
        ones128_f = sb(nc, es, "ones128_f", [128, 1], F32)
        ones1_f = sb(nc, es, "ones1_f", [1, 128], F32)
        ones128x8 = sb(nc, es, "ones128x8", [128, 8], F32)
        ident_b = sb(nc, es, "ident_b", [128, 128], BF16)
        ind4 = sb(nc, es, "ind4", [128, 512], BF16)
        msel = sb(nc, es, "msel", [8, 512], BF16)
        hgsel = sb(nc, es, "hgsel", [128, 4], F32)
        u_seg = sb(nc, es, "u_seg", [8, NTOK], F32)
        dec_v = sb(nc, es, "dec_v", [8, 1], F32)
        dec_s = sb(nc, es, "dec_s", [8, 1], F32)
        decbc1 = sb(nc, es, "decbc1", [8, NTOK], F32)
        h_seg = sb(nc, es, "h_seg", [8, NTOK], F32)
        h_init = sb(nc, es, "h_init", [8, 1], F32)
        h_bfs = sb(nc, es, "h_bfs", [8, NTOK], BF16)
        h_tm = sb(nc, es, "h_tm", [128, 32, 8], BF16)
        sel_tm = sb(nc, es, "sel_tm", [128, 8], BF16)
        h_loc = sb(nc, es, "h_loc", [8, NTOK], BF16)
        ld2 = [sb(nc, es, f"ld2_{i}", [128, HALF], F32) for i in range(2)]
        ld2b = [sb(nc, es, f"ld2b_{i}", [128, HALF], F32) for i in range(2)]
        stg2 = [sb(nc, es, f"stg2_{i}", [128, HALF], F32) for i in range(2)]
        bstg2 = [sb(nc, es, f"bstg2_{i}", [128, HALF], BF16) for i in range(2)]
        hfb = sb(nc, es, "hfb", [32, NTOK], BF16)
        h0m_sb = sb(nc, es, "h0m", [128, 4, NTOK], BF16)
        expd = sb(nc, es, "expd", [128, 8], F32)
        mx8 = sb(nc, es, "mx8", [128, 1], F32)
        sh8 = sb(nc, es, "sh8", [128, 8], F32)
        se8 = sb(nc, es, "se8", [128, 1], F32)
        se8b = sb(nc, es, "se8b", [128, 1], F32)
        rc8 = sb(nc, es, "rc8", [128, 1], F32)
        g4 = sb(nc, es, "g4", [128, 1], F32)
        ohd = sb(nc, es, "ohd", [128, 8], F32)
        cs8 = sb(nc, es, "cs8", [128, 8], F32)
        ohf = sb(nc, es, "ohf", [128, 8], BF16)
        ohf2 = sb(nc, es, "ohf2", [128, 8], F32)
        ohTg = sb(nc, es, "ohTg", [8, NTOK], BF16)
        sq_a = sb(nc, es, "sq_a", [128, HALF], F32)
        rb_sb = sb(nc, es, "rb_sb", [128, HALF], F32)
        qa_t = sb(nc, es, "qa_t", [128, HALF], F32)
        qn_t = sb(nc, es, "qn_t", [128, HALF], F32)
        rot_t = sb(nc, es, "rot_t", [128, HALF], F32)
        t1_t = sb(nc, es, "t1_t", [128, HALF], F32)
        t2_t = sb(nc, es, "t2_t", [128, HALF], F32)
        qf_bf = sb(nc, es, "qf_bf", [128, HALF], BF16)
        r_a = sb(nc, es, "r_a", [1, HALF], F32)
        r_b = sb(nc, es, "r_b", [1, HALF], F32)
        n2row = sb(nc, es, "n2row", [1, HALF], F32)
        rn_sb = sb(nc, es, "rn_sb", [4, HALF], F32)
        pA = ps(nc, es, "pA", [128, HALF])
        pB = ps(nc, es, "pB", [128, HALF])
        pC = ps(nc, es, "pC", [128, HALF])
        pD = ps(nc, es, "pD", [128, HALF])
        pE = ps(nc, es, "pE", [1, HALF])
        pF = ps(nc, es, "pF", [4, HALF])
        pT = ps(nc, es, "pT", [128, 128], BF16)
        p_mm = [pA, pB]
        scratch = dict(
            sq2=[sb(nc, es, f"lnsq{i}", [128, HALF], F32) for i in range(2)],
            tmp=sb(nc, es, "lntmp", [128, HALF], F32),
            p_sum=pE, p_sq=pF[0:1, :], p_mb=pC, p_rb=pD,
            stash_m=sb(nc, es, "stash_m", [1, NTOK], F32),
            stash_r=sb(nc, es, "stash_r", [1, NTOK], F32),
        )
        for nm in ("r_mean", "r_e2", "r_m2", "r_var", "r_sd", "r_rstd"):
            scratch[nm] = sb(nc, es, nm, [1, HALF], F32)
        p_g = pC[:, 0:8]
        p_row = pE
        p_rn = pF
        p_hf = pC[0:32, :]
        p_msk = pC
        p_loc = pC[:, 0:8]

        P = Prog(nc, es)
        consts = dict(ones128_f=ones128_f, ones1_f=ones1_f)

        ld_f = emit_dma_batch(P, "sync", [
            (dec_v[:], dlogit[:, :]),
            (bout_c[:], b_out_t[:, :]),
            (dskip_c[:], d_skip_t[:, :]),
            (g_col[:], gamma_t[:, :]),
            (b_col[:], beta_t[:, :]),
            (bdown_v[:], b_down[:, :]),
            (bup_c[:], b_up_t[:, :]),
            (qnw_v[:], qnw[:, :]),
            (knw_v[:], knw[:, :]),
            (cos_sb[:], cosq[:, :]),
            (sin_sb[:], sinq[:, :]),
            (ones128_f[:], c_ones128_f[:, :]),
            (ones1_f[:], c_ones1_f[:, :]),
            (ones128x8[:], c_ones128x8[:, :]),
            (ident_b[:], c_ident_b[:, :]),
            (ind4[:], c_ind4[:, :]),
            (msel[:], c_msel[:, :]),
            (hgsel[:], c_hgsel[:, :]),
        ])
        ld_w1 = emit_dma_batch(P, "gpsimd", [
            (woutb[:], w_out[:, :].rearrange("r (k m) -> r k m", m=128)),
            (wdownb[:], w_down[:, :].rearrange("(k p) r -> p k r", p=128)),
            (wupb[:], w_up[:, :].rearrange("r (k m) -> r k m", m=128)),
            (wg0b[:], wg0_t[:, :].rearrange("(k p) r -> p k r", p=128)),
        ])
        ld_xn = emit_dma_batch(P, "gpsimd", [
            (bufC[:], xn_in[:, :].rearrange("(k p) t -> p k t", p=128)),
        ])

        for e in ("scalar", "tensor", "vector"):
            @P.step(e, waits={"sync": ld_f}, incs=0)
            def _(h, sem):
                pass

        @P.step("scalar", waits={}, incs=1)
        def _(h, sem):
            h.activation(dec_s[:], dec_v[:], AF.Sigmoid, bias=0.0, scale=1.0).then_inc(sem, 1)

        @P.step("vector", waits={"scalar": P.now("scalar")}, incs=1)
        def _(h, sem):
            h.memset(h_init[:], 0.0).then_inc(sem, 1)

        @P.step("vector", waits={"vector": P.now("vector")}, incs=0)
        def _(h, sem):
            pass
        for seg in range(4):
            lu = emit_dma_batch(P, "sync", [
                (u_seg[:], u_all[:, seg*NTOK:(seg+1)*NTOK]),
            ], waits={"vector": P.now("vector")})

            @P.step("vector", waits={"sync": lu}, incs=1)
            def _(h, sem, seg=seg):
                if seg == 0:
                    h.tensor_scalar(decbc1[:], u_seg[:], 0.0, dec_s[:], ALU.mult, ALU.add)
                h.tensor_tensor_scan(h_seg[:], decbc1[:], u_seg[:], h_init[:],
                                     ALU.mult, ALU.add)
                h.drain()
                h.tensor_copy(h_bfs[:], h_seg[:])
                h.tensor_copy(h_init[:], h_seg[:, NTOK-1:NTOK])
                h.drain().then_inc(sem, 1)

            v_h = P.now("vector")
            for g in range(8):
                @P.step("tensor", waits={"vector": v_h if g == 0 else P.now("vector")}, incs=1)
                def _(h, sem, g=g):
                    h.transpose(pT[:, 0:8], h_bfs[:, g*128:(g+1)*128], ident_b[0:8, 0:8]).then_inc(sem, 1)

                @P.step("vector", waits={"tensor": P.now("tensor")}, incs=1)
                def _(h, sem, g=g, seg=seg):
                    h.tensor_copy(h_tm[:, seg*8+g, :], pT[:, 0:8]).then_inc(sem, 1)

        for m in range(8):
            @P.step("tensor", waits={"vector": P.now("vector")}, incs=1)
            def _(h, sem, m=m):
                for j in range(4):
                    ins = h.matmul(p_loc[:], ind4[:, j*128:(j+1)*128], h_tm[:, 4*m+j, :],
                                   start=(j == 0), stop=(j == 3))
                ins.then_inc(sem, 1)

            @P.step("vector", waits={"tensor": P.now("tensor")}, incs=1)
            def _(h, sem):
                h.tensor_copy(sel_tm[:], p_loc[:]).then_inc(sem, 1)

            @P.step("tensor", waits={"vector": P.now("vector")}, incs=1)
            def _(h, sem):
                h.transpose(pT[0:8, :], sel_tm[:], ident_b[:]).then_inc(sem, 1)

            @P.step("vector", waits={"tensor": P.now("tensor")}, incs=1)
            def _(h, sem, m=m):
                h.tensor_copy(h_loc[:, m*128:(m+1)*128], pT[0:8, :]).then_inc(sem, 1)
        v_hloc = P.now("vector")

        @P.step("tensor", waits={"vector": v_hloc, "gpsimd": ld_w1}, incs=0)
        def _(h, sem):
            pass
        xln1_src = xln1_in[:, :].rearrange("(k p) t -> p k t", p=128)
        x_src = x_t[:, :].rearrange("(k p) t -> p k t", p=128)
        xi_dst = xi_dram[:, :].rearrange("(k p) t -> p k t", p=128)
        xi_src = xi_dst
        x1_dst = x1_dram[:, :].rearrange("(k p) t -> p k t", p=128)
        x1_src = x1_dst
        x2_dst = x2_out[:, :].rearrange("(k p) t -> p k t", p=128)
        for half in range(2):
            cols = slice(half * HALF, (half + 1) * HALF)
            for oc in range(NCH):
                i = half * NCH + oc
                pb = i % 2
                @P.step("sync", waits={"vector": P.now("vector")} if i >= 2 else {}, incs=16)
                def _(h, sem, pb=pb, oc=oc, cols=cols):
                    h.dma_start(out=ld2[pb][:], in_=xln1_src[:, oc, cols]).then_inc(sem, 16)

                @P.step("tensor", waits={"vector": P.now("vector")} if i >= 2 else {}, incs=1)
                def _(h, sem, pb=pb, oc=oc, cols=cols):
                    h.matmul(p_mm[pb][:], woutb[:, oc, :], h_loc[:, cols],
                             start=True, stop=True).then_inc(sem, 1)

                @P.step("vector", waits={"tensor": P.now("tensor"), "sync": P.now("sync"),
                                         "gpsimd": ld_xn}, incs=1)
                def _(h, sem, pb=pb, oc=oc, cols=cols):
                    tmp = scratch["tmp"]
                    h.scalar_tensor_tensor(tmp[:], bufC[:, oc, cols], dskip_c[:, oc:oc+1],
                                           p_mm[pb][:], ALU.mult, ALU.add)
                    h.scalar_tensor_tensor(stg2[pb][:], tmp[:], bout_c[:, oc:oc+1],
                                           ld2[pb][:], ALU.add, ALU.add).then_inc(sem, 1)

                @P.step("sync", waits={"vector": P.now("vector")}, incs=16)
                def _(h, sem, pb=pb, oc=oc, cols=cols):
                    h.dma_start(out=xi_dst[:, oc, cols], in_=stg2[pb][:]).then_inc(sem, 16)

        @P.step("sync", waits={"sync": P.now("sync")}, incs=0)
        def _(h, sem):
            pass

        emit_ln_dram(P, xi_dram, bufC, consts, scratch, ld2b, gamma_beta=(g_col, b_col))

        for half in range(2):
            cols = slice(half * HALF, (half + 1) * HALF)
            @P.step("tensor", waits={"vector": P.now("vector"),
                                     "scalar": P.now("scalar")}, incs=1)
            def _(h, sem, cols=cols):
                for k in range(NCH):
                    ins = h.matmul(p_hf[:], wdownb[:, k, :], bufC[:, k, cols],
                                   start=(k == 0), stop=(k == NCH - 1))
                ins.then_inc(sem, 1)

            @P.step("scalar", waits={"tensor": P.now("tensor")}, incs=1)
            def _(h, sem, cols=cols):
                h.activation(hfb[:, cols], p_hf[:], AF.Relu, bias=bdown_v[:],
                             scale=1.0).then_inc(sem, 1)
        s_hf = P.now("scalar")
        for half in range(2):
            cols = slice(half * HALF, (half + 1) * HALF)
            for oc in range(NCH):
                i = half * NCH + oc
                pb = i % 2
                @P.step("sync", waits={"vector": P.now("vector")} if i >= 2 else {}, incs=32)
                def _(h, sem, pb=pb, oc=oc, cols=cols):
                    h.dma_start(out=ld2[pb][:], in_=x_src[:, oc, cols]).then_inc(sem, 16)
                    h.dma_start(out=ld2b[pb][:], in_=xi_src[:, oc, cols]).then_inc(sem, 16)

                @P.step("tensor", waits={"scalar": s_hf} if i < 2 else
                        {"vector": P.now("vector")}, incs=1)
                def _(h, sem, pb=pb, oc=oc, cols=cols):
                    h.matmul(p_mm[pb][:], wupb[:, oc, :], hfb[:, cols],
                             start=True, stop=True).then_inc(sem, 1)

                @P.step("vector", waits={"tensor": P.now("tensor"), "sync": P.now("sync")},
                        incs=1)
                def _(h, sem, pb=pb, oc=oc, cols=cols):
                    tmp = scratch["tmp"]
                    h.scalar_tensor_tensor(tmp[:], ld2b[pb][:], bup_c[:, oc:oc+1],
                                           p_mm[pb][:], ALU.add, ALU.add)
                    h.tensor_tensor(stg2[pb][:], tmp[:], ld2[pb][:],
                                    ALU.add).then_inc(sem, 1)

                @P.step("sync", waits={"vector": P.now("vector")}, incs=16)
                def _(h, sem, pb=pb, oc=oc, cols=cols):
                    h.dma_start(out=x1_dst[:, oc, cols], in_=stg2[pb][:]).then_inc(sem, 16)

        @P.step("sync", waits={"sync": P.now("sync")}, incs=0)
        def _(h, sem):
            pass

        emit_ln_dram(P, x1_dram, bufC, consts, scratch, ld2b, gamma_beta=None)

        for tt in range(8):
            @P.step("tensor", waits={"vector": P.now("vector"),
                                     "scalar": P.now("scalar")}, incs=1)
            def _(h, sem, tt=tt):
                for k in range(NCH):
                    ins = h.matmul(p_g[:], bufC[:, k, tt*128:(tt+1)*128],
                                   wg0b[:, k, :], start=(k == 0), stop=(k == NCH - 1))
                ins.then_inc(sem, 1)

            @P.step("vector", waits={"tensor": P.now("tensor")}, incs=1)
            def _(h, sem):
                h.reduce_max(mx8[:], p_g[:], axis=AX.X).then_inc(sem, 1)

            @P.step("vector", waits={"vector": P.now("vector")}, incs=1)
            def _(h, sem):
                h.tensor_scalar(ohd[:], p_g[:], mx8[:], None, ALU.is_ge)
                h.tensor_scalar(sh8[:], p_g[:], mx8[:], None, ALU.subtract).then_inc(sem, 1)

            @P.step("scalar", waits={"vector": P.now("vector")}, incs=1)
            def _(h, sem):
                h.activation(expd[:], sh8[:], AF.Exp, bias=0.0,
                             scale=1.0).then_inc(sem, 1)

            @P.step("vector", waits={"scalar": P.now("scalar")}, incs=1)
            def _(h, sem):
                h.reduce_sum(se8b[:], expd[:], axis=AX.X).then_inc(sem, 1)

            @P.step("vector", waits={"vector": P.now("vector")}, incs=1)
            def _(h, sem):
                h.reciprocal(rc8[:], se8b[:])
                h.tensor_tensor_scan(cs8[:], ones128x8[:], ohd[:], 0.0, ALU.mult, ALU.add)
                h.drain().then_inc(sem, 1)

            @P.step("vector", waits={"vector": P.now("vector")}, incs=1)
            def _(h, sem):
                h.tensor_scalar(g4[:], rc8[:], 4.0, None, ALU.mult).then_inc(sem, 1)

            @P.step("vector", waits={"vector": P.now("vector")}, incs=1)
            def _(h, sem):
                h.scalar_tensor_tensor(ohf2[:], cs8[:], 1.0, ohd[:], ALU.is_equal, ALU.mult).then_inc(sem, 1)

            @P.step("vector", waits={"vector": P.now("vector")}, incs=1)
            def _(h, sem, tt=tt):
                h.tensor_scalar(ohf[:], ohf2[:], g4[:], None, ALU.mult).then_inc(sem, 1)

            @P.step("tensor", waits={"vector": P.now("vector")}, incs=1)
            def _(h, sem):
                h.transpose(pT[0:8, :], ohf[:], ident_b[:]).then_inc(sem, 1)

            @P.step("vector", waits={"tensor": P.now("tensor")}, incs=1)
            def _(h, sem, tt=tt):
                h.tensor_copy(ohTg[:, tt*128:(tt+1)*128], pT[0:8, :]).then_inc(sem, 1)

        for half in range(2):
            cols = slice(half * HALF, (half + 1) * HALF)
            for ec in range(4):
                i = half * 4 + ec
                pb = i % 2
                wb = i % 2
                @P.step("gpsimd", waits={"tensor": P.now("tensor")}, incs=16)
                def _(h, sem, wb=wb, ec=ec):
                    h.dma_start(out=wqs2[wb][:],
                                in_=a_t[:, :].rearrange("(k p) r -> p k r", p=128)[:, :, ec*128:(ec+1)*128]
                                ).then_inc(sem, 16)

                @P.step("tensor", waits={"vector": P.now("vector"),
                                         "gpsimd": P.now("gpsimd")}, incs=1)
                def _(h, sem, pb=pb, wb=wb, ec=ec, cols=cols):
                    for k in range(NCH):
                        h.matmul(p_mm[pb][:], wqs2[wb][:, k, :],
                                 bufC[:, k, cols], start=(k == 0), stop=(k == NCH - 1))
                    h.matmul(p_msk[:], msel[:, ec*128:(ec+1)*128], ohTg[:, cols],
                             start=True, stop=True).then_inc(sem, 1)

                @P.step("vector", waits={"tensor": P.now("tensor")}, incs=1)
                def _(h, sem, pb=pb, ec=ec, cols=cols):
                    h.tensor_copy(sq_a[:], p_mm[pb][:])
                    h.tensor_tensor(h0m_sb[:, ec, cols], sq_a[:], p_msk[:],
                                    ALU.mult).then_inc(sem, 1)

        for half in range(2):
            cols = slice(half * HALF, (half + 1) * HALF)
            for oc in range(NCH):
                i = half * NCH + oc
                pb = i % 2
                @P.step("gpsimd", waits={"tensor": P.now("tensor")}, incs=16)
                def _(h, sem, pb=pb, oc=oc):
                    h.dma_start(out=wqs2[pb][:, 0:4, :],
                                in_=bw_t[:, :].rearrange("(c p) o -> p c o", p=128)[:, :, oc*128:(oc+1)*128]
                                ).then_inc(sem, 16)

                @P.step("sync", waits={"vector": P.now("vector")} if i >= 2 else {}, incs=16)
                def _(h, sem, pb=pb, oc=oc, cols=cols):
                    h.dma_start(out=ld2[pb][:], in_=x1_src[:, oc, cols]).then_inc(sem, 16)

                @P.step("tensor", waits={"vector": P.now("vector"),
                                         "gpsimd": P.now("gpsimd")}, incs=1)
                def _(h, sem, pb=pb, oc=oc, cols=cols):
                    for ec in range(4):
                        ins = h.matmul(p_mm[pb][:], wqs2[pb][:, ec, :],
                                       h0m_sb[:, ec, cols], start=(ec == 0), stop=(ec == 3))
                    ins.then_inc(sem, 1)

                @P.step("vector", waits={"tensor": P.now("tensor"),
                                         "sync": P.now("sync")}, incs=1)
                def _(h, sem, pb=pb, oc=oc, cols=cols):
                    h.tensor_tensor(stg2[pb][:], p_mm[pb][:], ld2[pb][:],
                                    ALU.add).then_inc(sem, 1)

                @P.step("sync", waits={"vector": P.now("vector")}, incs=16)
                def _(h, sem, pb=pb, oc=oc, cols=cols):
                    h.dma_start(out=x2_dst[:, oc, cols], in_=stg2[pb][:]).then_inc(sem, 16)

        @P.step("sync", waits={"sync": P.now("sync")}, incs=0)
        def _(h, sem):
            pass

        emit_ln_dram(P, x2_out, bufC, consts, scratch, ld2b, gamma_beta=None)

        @P.step("vector", waits={}, incs=1)
        def _(h, sem):
            h.tensor_scalar(wqsc[:], qnw_v[:], 1.0 / np.sqrt(128.0), None,
                            ALU.mult).then_inc(sem, 1)

        def qk_pipeline(n_mtiles, wsrc_dram, w_vec, out_dram, n2_out_dram):
            wsrc = wsrc_dram[:, :].rearrange("(k p) o -> p k o", p=128)
            odst = out_dram[:, :].rearrange("(mm p) t -> p mm t", p=128)
            for m in range(n_mtiles):
                wb = m % 2
                @P.step("gpsimd", waits={"tensor": P.now("tensor")}, incs=16)
                def _(h, sem, m=m, wb=wb):
                    h.dma_start(out=wqs2[wb][:], in_=wsrc[:, :, m*128:(m+1)*128]
                                ).then_inc(sem, 16)
                g_w = P.now("gpsimd")
                for half in range(2):
                    cols = slice(half * HALF, (half + 1) * HALF)
                    i = m * 2 + half
                    pb = i % 2
                    @P.step("tensor", waits={"gpsimd": g_w, "vector": P.now("vector")}, incs=1)
                    def _(h, sem, pb=pb, wb=wb, cols=cols):
                        for k in range(NCH):
                            ins = h.matmul(p_mm[pb][:], wqs2[wb][:, k, :], bufC[:, k, cols],
                                           start=(k == 0), stop=(k == NCH - 1))
                        ins.then_inc(sem, 1)
                    t_mm = P.now("tensor")

                    @P.step("scalar", waits={"tensor": t_mm}, incs=1)
                    def _(h, sem, pb=pb):
                        h.activation(sq_a[:], p_mm[pb][:], AF.Square, bias=0.0,
                                     scale=1.0).then_inc(sem, 1)

                    @P.step("tensor", waits={"scalar": P.now("scalar"),
                                             "vector": P.now("vector")}, incs=1)
                    def _(h, sem):
                        h.matmul(p_row[:], ones128_f[:], sq_a[:], start=True,
                                 stop=True).then_inc(sem, 1)

                    @P.step("vector", waits={"tensor": P.now("tensor")}, incs=1)
                    def _(h, sem):
                        h.tensor_scalar(r_a[:], p_row[:], 1.0 / 128.0, EPS_RMS,
                                        ALU.mult, ALU.add).then_inc(sem, 1)

                    @P.step("scalar", waits={"vector": P.now("vector")}, incs=1)
                    def _(h, sem):
                        h.activation(r_a[:], r_a[:], AF.Sqrt, bias=0.0,
                                     scale=1.0).then_inc(sem, 1)

                    @P.step("vector", waits={"scalar": P.now("scalar")}, incs=1)
                    def _(h, sem):
                        h.reciprocal(r_b[:], r_a[:])
                        h.drain().then_inc(sem, 1)

                    @P.step("tensor", waits={"vector": P.now("vector")}, incs=1)
                    def _(h, sem):
                        h.matmul(pC[:], ones1_f[:], r_b[:], start=True,
                                 stop=True).then_inc(sem, 1)

                    @P.step("vector", waits={"tensor": P.now("tensor")}, incs=1)
                    def _(h, sem, pb=pb, cols=cols):
                        h.tensor_copy(rb_sb[:], pC[:])
                        h.tensor_scalar(qa_t[:], p_mm[pb][:], w_vec[:], None, ALU.mult)
                        h.tensor_tensor(qn_t[:], qa_t[:], rb_sb[:], ALU.mult)
                        h.tensor_scalar(rot_t[0:64, :], qn_t[64:128, :], -1.0, None, ALU.mult)
                        h.tensor_copy(rot_t[64:128, :], qn_t[0:64, :])
                        h.tensor_tensor(t1_t[:], qn_t[:], cos_sb[:, cols], ALU.mult).then_inc(sem, 1)

                    @P.step("gpsimd", waits={"vector": P.now("vector")}, incs=1)
                    def _(h, sem, cols=cols):
                        h.tensor_tensor(t2_t[:], rot_t[:], sin_sb[:, cols],
                                        ALU.mult).then_inc(sem, 1)

                    @P.step("vector", waits={"gpsimd": P.now("gpsimd"),
                                             "sync": P.now("sync")}, incs=1)
                    def _(h, sem):
                        h.tensor_tensor(qf_bf[:], t1_t[:], t2_t[:], ALU.add).then_inc(sem, 1)

                    @P.step("scalar", waits={"vector": P.now("vector")}, incs=1)
                    def _(h, sem):
                        h.activation(sq_a[:], qf_bf[:], AF.Square, bias=0.0,
                                     scale=1.0).then_inc(sem, 1)

                    @P.step("tensor", waits={"scalar": P.now("scalar"),
                                             "vector": P.now("vector")}, incs=1)
                    def _(h, sem):
                        h.matmul(p_row[:], ones128_f[:], sq_a[:], start=True,
                                 stop=True).then_inc(sem, 1)

                    @P.step("vector", waits={"tensor": P.now("tensor")}, incs=1)
                    def _(h, sem):
                        h.tensor_copy(n2row[:], p_row[:]).then_inc(sem, 1)

                    @P.step("sync", waits={"vector": P.now("vector")}, incs=32)
                    def _(h, sem, m=m, cols=cols):
                        h.dma_start(out=odst[:, m, cols], in_=qf_bf[:]).then_inc(sem, 16)
                        h.dma_start(out=n2_out_dram[m:m+1, cols], in_=n2row[:]).then_inc(sem, 16)

        qk_pipeline(NH, wq_t, wqsc, q_out, qn2_out)
        qk_pipeline(KVH, wk_t, knw_v, k_out, kn2_out)

        wvsrc = wv_t[:, :].rearrange("(k p) o -> p k o", p=128)
        for m in range(4):
            wb = m % 2
            @P.step("gpsimd", waits={"tensor": P.now("tensor")}, incs=16)
            def _(h, sem, m=m, wb=wb):
                h.dma_start(out=wqs2[wb][:], in_=wvsrc[:, :, m*128:(m+1)*128]).then_inc(sem, 16)
            g_w = P.now("gpsimd")
            for half in range(2):
                cols = slice(half * HALF, (half + 1) * HALF)
                i = m * 2 + half
                pb = i % 2
                @P.step("tensor", waits={"gpsimd": g_w, "vector": P.now("vector")}, incs=1)
                def _(h, sem, pb=pb, wb=wb, cols=cols):
                    for k in range(NCH):
                        ins = h.matmul(p_mm[pb][:], wqs2[wb][:, k, :], bufC[:, k, cols],
                                       start=(k == 0), stop=(k == NCH - 1))
                    ins.then_inc(sem, 1)

                @P.step("vector", waits={"tensor": P.now("tensor")}, incs=1)
                def _(h, sem, pb=pb):
                    h.tensor_copy(bstg2[pb][:], p_mm[pb][:]).then_inc(sem, 1)

                for s4 in range(4):
                    @P.step("tensor", waits={"vector": P.now("vector")}, incs=1)
                    def _(h, sem, pb=pb, s4=s4):
                        h.transpose(pT[:], bstg2[pb][:, s4*128:(s4+1)*128],
                                    ident_b[:]).then_inc(sem, 1)

                    @P.step("vector", waits={"tensor": P.now("tensor"),
                                             "sync": P.now("sync")}, incs=1)
                    def _(h, sem, s4=s4):
                        h.tensor_copy(qf_bf[:, 0:128], pT[:]).then_inc(sem, 1)

                    @P.step("sync", waits={"vector": P.now("vector")}, incs=16)
                    def _(h, sem, m=m, half=half, s4=s4):
                        h.dma_start(out=v_out[half*512+s4*128:half*512+(s4+1)*128,
                                              m*128:(m+1)*128],
                                    in_=qf_bf[:, 0:128]).then_inc(sem, 16)

        def r_pipeline(wsrc_dram, out_dram, n2_out_dram):
            wsrc = wsrc_dram[:, :].rearrange("(k p) o -> p k o", p=128)
            for rt in range(3):
                wb = rt % 2
                @P.step("gpsimd", waits={"tensor": P.now("tensor")}, incs=16)
                def _(h, sem, rt=rt, wb=wb):
                    h.dma_start(out=wqs2[wb][:], in_=wsrc[:, :, rt*128:(rt+1)*128]).then_inc(sem, 16)
                g_w = P.now("gpsimd")
                for half in range(2):
                    cols = slice(half * HALF, (half + 1) * HALF)
                    i = rt * 2 + half
                    pb = i % 2
                    @P.step("tensor", waits={"gpsimd": g_w, "vector": P.now("vector")}, incs=1)
                    def _(h, sem, pb=pb, wb=wb, cols=cols):
                        for k in range(NCH):
                            ins = h.matmul(p_mm[pb][:], wqs2[wb][:, k, :],
                                           bufC[:, k, cols], start=(k == 0), stop=(k == NCH - 1))
                        ins.then_inc(sem, 1)

                    @P.step("scalar", waits={"tensor": P.now("tensor")}, incs=1)
                    def _(h, sem, pb=pb):
                        h.activation(sq_a[:], p_mm[pb][:], AF.Square, bias=0.0,
                                     scale=1.0).then_inc(sem, 1)

                    @P.step("tensor", waits={"scalar": P.now("scalar"),
                                             "vector": P.now("vector")}, incs=1)
                    def _(h, sem):
                        h.matmul(p_rn[:], hgsel[:], sq_a[:], start=True,
                                 stop=True).then_inc(sem, 1)

                    @P.step("vector", waits={"tensor": P.now("tensor"),
                                             "sync": P.now("sync")}, incs=1)
                    def _(h, sem, pb=pb):
                        h.tensor_copy(bstg2[pb][:], p_mm[pb][:])
                        h.tensor_copy(rn_sb[:], p_rn[:]).then_inc(sem, 1)

                    @P.step("sync", waits={"vector": P.now("vector")}, incs=32)
                    def _(h, sem, rt=rt, cols=cols, pb=pb):
                        h.dma_start(out=out_dram[rt*128:(rt+1)*128, cols],
                                    in_=bstg2[pb][:]).then_inc(sem, 16)
                        h.dma_start(out=n2_out_dram[rt*4:(rt+1)*4, cols],
                                    in_=rn_sb[:]).then_inc(sem, 16)

        r_pipeline(wrq_t, rq_out, rqn2_out)
        r_pipeline(wrk_t, rk_out, rkn2_out)

        @P.step("gpsimd", waits={"sync": P.now("sync"), "gpsimd": P.now("gpsimd"),
                                 "vector": P.now("vector")}, incs=0)
        def _(h, sem):
            pass

        P.run()
    return nc


def build_p3():
    nc = bass.Bass("TRN2", target_bir_lowering=False)
    es = contextlib.ExitStack()
    def din(name, shape, dtype=F32):
        return nc.dram_tensor(name, shape, dtype, kind="ExternalInput").ap()
    def dout(name, shape, dtype=F32):
        return nc.dram_tensor(name, shape, dtype, kind="ExternalOutput").ap()

    x2_t = din("x2_t", [D, NTOK])
    q_in = din("q_in", [D, NTOK], BF16)
    k_full = din("k_full", [512, S], BF16)
    v_full = din("v_full", [S, 512], BF16)
    rq_full = din("rq_full", [384, S], BF16)
    rk_full = din("rk_full", [384, S], BF16)
    rq_own = din("rq_own", [384, NTOK], BF16)
    rk_own = din("rk_own", [384, NTOK], BF16)
    qn2_own = din("qn2_own", [NH, NTOK])
    kn2_full = din("kn2_full", [KVH, S])
    rqn2_own = din("rqn2_own", [12, NTOK])
    rkn2_own = din("rkn2_own", [12, NTOK])
    rqn2_full = din("rqn2_full", [12, S])
    rkn2_full = din("rkn2_full", [12, S])
    gs12 = din("gs12", [12, 1])
    masks = din("masks", [8, 128, 4, 512], BF16)
    wo_t = din("wo_t", [D, D])
    wg1 = din("wg1", [D, 8])
    sw1 = din("sw1", [D, HMOE])
    sw3 = din("sw3", [D, HMOE])
    sw2 = din("sw2", [HMOE, D])
    c_ones128_f = din("c_ones128_f", [128, 1])
    c_ones1_f = din("c_ones1_f", [1, 128])
    c_ones128_b = din("c_ones128_b", [128, 1], BF16)
    c_ones1_b = din("c_ones1_b", [1, 2048], BF16)
    c_ident_b = din("c_ident_b", [128, 128], BF16)
    c_gsel = din("c_gsel", [12, 512])

    x3b_out = dout("x3b_out", [D, NTOK])
    xn4tm_out = dout("xn4tm_out", [NTOK, D], BF16)
    logits_out = dout("logits_out", [NTOK, 8])
    x3_dram = dout("x3_dbg", [D, NTOK])
    dbg_ps = dout("dbg_ps", [128, 1024])
    dbg_pt = dout("dbg_pt", [128, 1024], BF16)
    dbg_rhs = dout("dbg_rhs", [65, 2048], BF16)
    dbg_rhs2 = dout("dbg_rhs2", [1, 512])
    attn_dram = nc.dram_tensor("attn_scr", [D, NTOK], BF16, kind="Internal").ap()

    with es:
        bufC = sb(nc, es, "bufC", [128, NCH, NTOK], BF16)
        q_sb = bufC
        ones_row_b = sb(nc, es, "ones_row_b", [1, 128], BF16)
        ones128_f = sb(nc, es, "ones128_f", [128, 1], F32)
        ones1_f = sb(nc, es, "ones1_f", [1, 128], F32)
        ones128_b = sb(nc, es, "ones128_b", [128, 1], BF16)
        ident_b = sb(nc, es, "ident_b", [128, 128], BF16)
        logit_stg = sb(nc, es, "logit_stg", [128, 8], F32)
        ldb2 = [sb(nc, es, f"ldb2_{i}", [128, HALF], BF16) for i in range(2)]
        bstg2 = [sb(nc, es, f"bstg2_{i}", [128, HALF], BF16) for i in range(2)]
        pA = ps(nc, es, "pA", [128, HALF])
        pB = ps(nc, es, "pB", [128, HALF])
        pC = ps(nc, es, "pC", [128, HALF])
        pD = ps(nc, es, "pD", [128, HALF])
        pE = ps(nc, es, "pE", [1, HALF])
        pF = ps(nc, es, "pF", [4, HALF])
        pT = ps(nc, es, "pT", [128, 128], BF16)
        scratch = dict(
            sq2=[sb(nc, es, f"lnsq{i}", [128, HALF], F32) for i in range(2)],
            tmp=sb(nc, es, "lntmp", [128, HALF], F32),
            p_sum=pE, p_sq=pF[0:1, :], p_mb=pC, p_rb=pD,
            stash_m=sb(nc, es, "stash_m", [1, NTOK], F32),
            stash_r=sb(nc, es, "stash_r", [1, NTOK], F32),
        )
        for nm in ("r_mean", "r_e2", "r_m2", "r_var", "r_sd", "r_rstd"):
            scratch[nm] = sb(nc, es, nm, [1, HALF], F32)
        es2 = contextlib.ExitStack()
        rhs_sym = sb(nc, es2, "rhs_sym", [65, 4, NTOK], BF16)
        rhs_ns = sb(nc, es2, "rhs_ns", [33, 8, NTOK], BF16)
        negm_van = sb(nc, es2, "negm_van", [1, 4, NTOK], BF16)
        rmx = sb(nc, es2, "rmx", [1, 1], F32)
        rmx4 = sb(nc, es2, "rmx4", [1, 4], F32)
        kmax_f = sb(nc, es2, "kmax_f", [1, 4], F32)
        rqmax_f = sb(nc, es2, "rqmax_f", [1, 12], F32)
        rkmax_f = sb(nc, es2, "rkmax_f", [1, 12], F32)
        absgs = sb(nc, es2, "absgs", [1, 12], F32)
        gsraw = sb(nc, es2, "gsraw", [1, 12], F32)
        gsh = sb(nc, es2, "gsh", [1, 12], F32)
        gs12_sb = sb(nc, es2, "gs12_sb", [12, 1], F32)
        gsel_sb = sb(nc, es2, "gsel_sb", [12, 512], F32)
        gsvec = sb(nc, es2, "gsvec", [32, 16], F32)
        qn_row = sb(nc, es2, "qn_row", [1, NTOK], F32)
        rqn_row = sb(nc, es2, "rqn_row", [1, NTOK], F32)
        rkn_row = sb(nc, es2, "rkn_row", [1, NTOK], F32)
        qn_s = sb(nc, es2, "qn_s", [1, NTOK], F32)
        rqn_s = sb(nc, es2, "rqn_s", [1, NTOK], F32)
        rkn_s = sb(nc, es2, "rkn_s", [1, NTOK], F32)
        m_a = sb(nc, es2, "m_a", [1, NTOK], F32)
        m_b = sb(nc, es2, "m_b", [1, NTOK], F32)
        m_c = sb(nc, es2, "m_c", [1, NTOK], F32)
        m_d = sb(nc, es2, "m_d", [1, NTOK], F32)
        m_e = sb(nc, es2, "m_e", [1, NTOK], F32)
        m_b2 = sb(nc, es2, "m_b2", [1, NTOK], F32)
        dbg_rhs2_sb = sb(nc, es2, "dbg_rhs2_sb", [1, 512], F32)
        m_d2 = sb(nc, es2, "m_d2", [1, NTOK], F32)
        neg_t = sb(nc, es2, "neg_t", [1, 12], F32)
        kmaxs = sb(nc, es2, "kmaxs", [1, 4], F32)
        rqmaxs = sb(nc, es2, "rqmaxs", [1, 12], F32)
        rkmaxs = sb(nc, es2, "rkmaxs", [1, 12], F32)
        rowf = qn_row
        rq_h = sb(nc, es2, "rq_h", [32, NTOK], BF16)
        kT2 = [sb(nc, es2, f"kT{i}", [128, HALF], BF16) for i in range(2)]
        vT2 = [sb(nc, es2, f"vT{i}", [128, 4, 128], BF16) for i in range(2)]
        symL2 = [sb(nc, es2, f"symL{i}", [65, 4, HALF], BF16) for i in range(2)]
        nsL2 = [sb(nc, es2, f"nsL{i}", [33, 4, HALF], BF16) for i in range(2)]
        mask_sb = sb(nc, es2, "mask_sb", [128, 4, HALF], BF16)
        pt2 = [sb(nc, es2, f"pt{i}", [128, HALF], BF16) for i in range(2)]
        rec_row = sb(nc, es2, "rec_row", [1, HALF], F32)
        sum_row = sb(nc, es2, "sum_row", [1, HALF], F32)
        bc_sb = sb(nc, es2, "bc_sb", [128, HALF], F32)
        p_S = [pA, pB]
        p_av = pD
        p_sums = pE
        p_bc = pC
        p_gv = pF[0:1, :]

        P = Prog(nc, es)
        consts = dict(ones128_f=ones128_f, ones1_f=ones1_f)

        ld_f = emit_dma_batch(P, "sync", [
            (gs12_sb[:], gs12[:, :]),
            (gsel_sb[:], c_gsel[:, :]),
            (ones128_f[:], c_ones128_f[:, :]),
            (ones1_f[:], c_ones1_f[:, :]),
            (ones128_b[:], c_ones128_b[:, :]),
            (ones_row_b[:], c_ones1_b[0:1, 0:128]),
            (ident_b[:], c_ident_b[:, :]),
            (nsL2[0][32:33, :, :], c_ones1_b[0:1, 0:2048].rearrange("o (a t) -> o a t", a=4)),
            (nsL2[1][32:33, :, :], c_ones1_b[0:1, 0:2048].rearrange("o (a t) -> o a t", a=4)),
            (symL2[0][64:65, :, :], c_ones1_b[0:1, 0:2048].rearrange("o (a t) -> o a t", a=4)),
            (symL2[1][64:65, :, :], c_ones1_b[0:1, 0:2048].rearrange("o (a t) -> o a t", a=4)),
        ])
        ld_q = emit_dma_batch(P, "gpsimd", [
            (q_sb[:], q_in[:, :].rearrange("(hh p) t -> p hh t", p=128)),
        ])
        for e in ("scalar", "tensor", "vector"):
            @P.step(e, waits={"sync": ld_f}, incs=0)
            def _(h, sem):
                pass

        def rowmax(src_ap, nrows, dst_tile):
            for r_ in range(nrows):
                for ch in range(4):
                    lr = emit_dma_batch(P, "sync",
                        [(rowf[:], src_ap[r_:r_+1, ch*NTOK:(ch+1)*NTOK])],
                        waits={"vector": P.now("vector")})
                    @P.step("vector", waits={"sync": lr}, incs=1)
                    def _(h, sem):
                        h.reduce_max(rmx[:], rowf[:], axis=AX.X).then_inc(sem, 1)

                    @P.step("vector", waits={"vector": P.now("vector")}, incs=1)
                    def _(h, sem, ch=ch):
                        h.tensor_copy(rmx4[0:1, ch:ch+1], rmx[:]).then_inc(sem, 1)

                @P.step("vector", waits={"vector": P.now("vector")}, incs=1)
                def _(h, sem):
                    h.reduce_max(rmx[:], rmx4[:], axis=AX.X).then_inc(sem, 1)

                @P.step("vector", waits={"vector": P.now("vector")}, incs=1)
                def _(h, sem, r_=r_):
                    h.tensor_copy(dst_tile[0:1, r_:r_+1], rmx[:]).then_inc(sem, 1)
        rowmax(kn2_full, KVH, kmax_f)
        rowmax(rqn2_full, 12, rqmax_f)
        rowmax(rkn2_full, 12, rkmax_f)

        @P.step("scalar", waits={"vector": P.now("vector")}, incs=1)
        def _(h, sem):
            h.activation(kmaxs[:], kmax_f[:], AF.Sqrt, bias=0.0, scale=1.0)
            h.activation(rqmaxs[:], rqmax_f[:], AF.Sqrt, bias=0.0, scale=1.0)
            h.activation(rkmaxs[:], rkmax_f[:], AF.Sqrt, bias=0.0, scale=1.0)
            h.drain().then_inc(sem, 1)

        lgs = emit_dma_batch(P, "sync", [(gsraw[:], gs12[:, :].rearrange("a o -> o a"))],
                             waits={})

        @P.step("vector", waits={"sync": lgs, "scalar": P.now("scalar")}, incs=1)
        def _(h, sem):
            h.tensor_scalar(neg_t[:], gsraw[:], -1.0, None, ALU.mult)
            h.drain()
            h.tensor_tensor(absgs[:], gsraw[:], neg_t[:], ALU.max)
            h.drain()
            h.tensor_scalar(gsh[:], absgs[:], 0.5, None, ALU.mult)
            h.drain().then_inc(sem, 1)

        for hh in range(12):
            @P.step("tensor", waits={"vector": P.now("vector")}, incs=1)
            def _(h, sem, hh=hh):
                h.matmul(pC[0:32, 0:1], gsel_sb[:, hh*32:(hh+1)*32],
                         gs12_sb[:], start=True, stop=True).then_inc(sem, 1)

            @P.step("vector", waits={"tensor": P.now("tensor")}, incs=1)
            def _(h, sem, hh=hh):
                h.tensor_copy(gsvec[:, hh:hh+1], pC[0:32, 0:1]).then_inc(sem, 1)

        for hh in range(NH):
            kvh = hh // 4
            pairs = [(qn_row[:], qn2_own[hh:hh+1, :])]
            if hh < 12:
                pairs.append((rqn_row[:], rqn2_own[hh:hh+1, :]))
                pairs.append((rq_h[:], rq_own[:, :].rearrange("(g r) t -> r g t", r=32)[:, hh, :]))
            if hh < 4:
                pairs.append((rkn_row[:], rkn2_own[hh:hh+1, :]))
            lm = emit_dma_batch(P, "sync", pairs, waits={"vector": P.now("vector")})

            @P.step("scalar", waits={"sync": lm, "vector": P.now("vector")}, incs=1)
            def _(h, sem, hh=hh):
                h.activation(qn_s[:], qn_row[:], AF.Sqrt, bias=0.0, scale=1.0)
                if hh < 12:
                    h.activation(rqn_s[:], rqn_row[:], AF.Sqrt, bias=0.0, scale=1.0)
                if hh < 4:
                    h.activation(rkn_s[:], rkn_row[:], AF.Sqrt, bias=0.0, scale=1.0)
                h.drain().then_inc(sem, 1)

            @P.step("vector", waits={"scalar": P.now("scalar")}, incs=1)
            def _(h, sem, hh=hh, kvh=kvh):
                h.tensor_scalar(m_a[:], qn_s[:], kmaxs[0:1, kvh:kvh+1], None, ALU.mult)
                mfin = m_a
                if hh < 12:
                    gfac = gsh if hh < 4 else absgs
                    h.tensor_scalar(m_b[:], rqn_s[:], rkmaxs[0:1, hh:hh+1],
                                    None, ALU.mult)
                    h.drain()
                    h.tensor_scalar(m_b2[:], m_b[:], gfac[0:1, hh:hh+1],
                                    None, ALU.mult)
                    h.drain()
                    h.tensor_tensor(m_c[:], m_a[:], m_b2[:], ALU.add)
                    mfin = m_c
                if hh < 4:
                    h.tensor_scalar(m_d[:], rkn_s[:], rqmaxs[0:1, hh:hh+1],
                                    None, ALU.mult)
                    h.drain()
                    h.tensor_scalar(m_d2[:], m_d[:], gsh[0:1, hh:hh+1],
                                    None, ALU.mult)
                    h.drain()
                    h.tensor_tensor(m_e[:], m_c[:], m_d2[:], ALU.add)
                    mfin = m_e
                h.drain()
                if hh == 0:
                    h.tensor_copy(dbg_rhs2_sb[0:1, 0:4], kmax_f[:])
                    h.tensor_copy(dbg_rhs2_sb[0:1, 4:8], kmaxs[:])
                    h.tensor_copy(dbg_rhs2_sb[0:1, 8:20], gsh[:])
                    h.tensor_copy(dbg_rhs2_sb[0:1, 20:32], absgs[:])
                    h.tensor_copy(dbg_rhs2_sb[0:1, 32:44], rqmaxs[:])
                    h.tensor_copy(dbg_rhs2_sb[0:1, 44:56], rkmaxs[:])
                    h.tensor_copy(dbg_rhs2_sb[0:1, 64:128], m_d[0:1, 0:64])
                    h.tensor_copy(dbg_rhs2_sb[0:1, 128:192], m_d2[0:1, 0:64])
                    h.tensor_copy(dbg_rhs2_sb[0:1, 192:256], m_c[0:1, 0:64])
                    h.tensor_copy(dbg_rhs2_sb[0:1, 256:320], m_e[0:1, 0:64])
                    h.tensor_copy(dbg_rhs2_sb[0:1, 320:384], m_a[0:1, 0:64])
                    h.tensor_copy(dbg_rhs2_sb[0:1, 384:448], m_b2[0:1, 0:64])
                    h.tensor_copy(dbg_rhs2_sb[0:1, 448:512], rkn_s[0:1, 0:64])
                if hh < 4:
                    h.tensor_scalar(rhs_sym[64:65, hh, :], mfin[:], -1.0, None, ALU.mult)
                    ins = h.tensor_scalar(rhs_sym[0:32, hh, :], rq_h[:], gsvec[:, hh:hh+1],
                                          None, ALU.mult)
                elif hh < 12:
                    h.tensor_scalar(rhs_ns[32:33, hh - 4, :], mfin[:], -1.0, None, ALU.mult)
                    ins = h.tensor_scalar(rhs_ns[0:32, hh - 4, :], rq_h[:], gsvec[:, hh:hh+1],
                                          None, ALU.mult)
                else:
                    ins = h.tensor_scalar(negm_van[0:1, hh - 12, :], mfin[:], -1.0, None, ALU.mult)
                ins.then_inc(sem, 1)
        for hh in range(4):
            lm = emit_dma_batch(P, "sync", [
                (rq_h[:], rk_own[:, :].rearrange("(g r) t -> r g t", r=32)[:, hh, :]),
            ], waits={"vector": P.now("vector")})

            @P.step("vector", waits={"sync": lm}, incs=1)
            def _(h, sem, hh=hh):
                h.tensor_scalar(rhs_sym[32:64, hh, :], rq_h[:], gsvec[:, hh:hh+1],
                                None, ALU.mult).then_inc(sem, 1)
        emit_dma_batch(P, "sync", [
            (dbg_rhs[:, 0:NTOK], rhs_sym[:, 0, :]),
            (dbg_rhs[0:33, NTOK:2*NTOK], rhs_ns[:, 1, :]),
            (dbg_rhs2[:, :], dbg_rhs2_sb[:]),
        ], waits={"vector": P.now("vector")})
        v_rhs = P.now("vector")

        attn_dst = attn_dram[:, :].rearrange("(hh p) t -> p hh t", p=128)
        kfull_src = k_full
        vfull_src = v_full
        started = {}
        for qi in range(8):
            qcols = slice(qi * 128, (qi + 1) * 128)
            lmk = emit_dma_batch(P, "sync", [
                (mask_sb[:], masks[qi, :, :, :]),
            ], waits={"vector": P.now("vector"), "tensor": P.now("tensor")})
            for hg4 in range(4):
                kvh = hg4
                n_kcg = qi + 1
                first = [True]
                for kcg in range(n_kcg):
                    kb = kcg % 2
                    kcols = slice(kcg * 512, (kcg + 1) * 512)
                    pairs = [
                        (kT2[kb][:], kfull_src[kvh*128:(kvh+1)*128, kcols]),
                        (vT2[kb][:], vfull_src[:, :].rearrange("(cg c p) v -> p cg c v", p=128, c=4)[:, kcg, :, kvh*128:(kvh+1)*128]),
                    ]
                    if hg4 == 0:
                        pairs.append((symL2[kb][0:32, :, :],
                                      rk_full[:, :].rearrange("(g r) t -> r g t", r=32)[:, 0:4, kcols]))
                        pairs.append((symL2[kb][32:64, :, :],
                                      rq_full[:, :].rearrange("(g r) t -> r g t", r=32)[:, 0:4, kcols]))
                    elif hg4 in (1, 2):
                        pairs.append((nsL2[kb][0:32, :, :],
                                      rk_full[:, :].rearrange("(g r) t -> r g t", r=32)[:, 4*hg4:4*hg4+4, kcols]))
                    lk = emit_dma_batch(P, "gpsimd", pairs,
                                        waits={"tensor": P.now("tensor")})
                    for kc in range(4):
                        par = (kcg * 4 + kc) % 2
                        kkc = slice(kc * 128, (kc + 1) * 128)
                        @P.step("tensor", waits={"gpsimd": lk,
                                                 "scalar": P.now("scalar"),
                                                 "sync": lmk,
                                                 "vector": v_rhs}, incs=1)
                        def _(h, sem, hg4=hg4, kb=kb, kc=kc, par=par, qcols=qcols):
                            h.matmul(p_S[par][:],
                                     kT2[kb][:, kc*128:(kc+1)*128],
                                     q_sb[:, 4*hg4:4*hg4+4, qcols],
                                     start=True, stop=False)
                            if hg4 == 0:
                                for hh in range(4):
                                    ins = h.matmul(p_S[par][:, hh*128:(hh+1)*128],
                                                   symL2[kb][:, hh, kc*128:(kc+1)*128],
                                                   rhs_sym[:, hh, qcols],
                                                   start=False, stop=(hh == 3))
                            elif hg4 in (1, 2):
                                for hh in range(4):
                                    ins = h.matmul(p_S[par][:, hh*128:(hh+1)*128],
                                                   nsL2[kb][:, hh, kc*128:(kc+1)*128],
                                                   rhs_ns[:, 4*(hg4-1)+hh, qcols],
                                                   start=False, stop=(hh == 3))
                            else:
                                ins = h.matmul(p_S[par][:], ones_row_b[:],
                                               negm_van[0:1, :, qcols],
                                               start=False, stop=True)
                            ins.then_inc(sem, 1)

                        if qi == 0 and hg4 in (0, 3) and kc == 0:
                            @P.step("vector", waits={"tensor": P.now("tensor")}, incs=1)
                            def _(h, sem, par=par, hg4=hg4):
                                h.tensor_copy(bc_sb[:], p_S[par][:]).then_inc(sem, 1)
                            sdbg = emit_dma_batch(P, "sync", [
                                (dbg_ps[:, (hg4 // 3) * 512:(hg4 // 3) * 512 + 512], bc_sb[:]),
                            ], waits={"vector": P.now("vector")})

                        @P.step("scalar", waits={"tensor": P.now("tensor"),
                                                 "vector": P.now("vector")}, incs=1)
                        def _(h, sem, par=par):
                            h.activation(pt2[par][:], p_S[par][:], AF.Exp, bias=0.0,
                                         scale=1.0).then_inc(sem, 1)

                        if kcg == n_kcg - 1:
                            @P.step("vector", waits={"scalar": P.now("scalar")}, incs=1)
                            def _(h, sem, par=par, kc=kc):
                                h.tensor_tensor(pt2[par][:], pt2[par][:],
                                                mask_sb[:, kc, :], ALU.mult).then_inc(sem, 1)

                        if qi == 0 and hg4 in (0, 3) and kc == 0:
                            sdbg2 = emit_dma_batch(P, "sync", [
                                (dbg_pt[:, (hg4 // 3) * 512:(hg4 // 3) * 512 + 512], pt2[par][:]),
                            ], waits={"vector": P.now("vector"), "scalar": P.now("scalar")})
                        wav = {"scalar": P.now("scalar"), "vector": P.now("vector")}
                        key = (qi, hg4)
                        st_flag = key not in started
                        started[key] = True
                        last = (kcg == n_kcg - 1 and kc == 3)
                        @P.step("tensor", waits=wav, incs=1)
                        def _(h, sem, kb=kb, kc=kc, par=par, st=st_flag, last=last):
                            h.matmul(p_av[:], vT2[kb][:, kc, :], pt2[par][:],
                                     start=st, stop=last)
                            h.matmul(p_sums[:], ones128_b[:], pt2[par][:],
                                     start=st, stop=last).then_inc(sem, 1)
                del started[(qi, hg4)]
                @P.step("vector", waits={"tensor": P.now("tensor")}, incs=1)
                def _(h, sem):
                    h.tensor_scalar(sum_row[:], p_sums[:], 1e-37, None, ALU.max).then_inc(sem, 1)

                @P.step("vector", waits={"vector": P.now("vector")}, incs=1)
                def _(h, sem):
                    h.reciprocal(rec_row[:], sum_row[:]).then_inc(sem, 1)

                @P.step("tensor", waits={"vector": P.now("vector")}, incs=1)
                def _(h, sem):
                    h.matmul(p_bc[:], ones1_f[:], rec_row[:], start=True,
                             stop=True).then_inc(sem, 1)

                @P.step("vector", waits={"tensor": P.now("tensor"),
                                         "sync": P.now("sync")}, incs=1)
                def _(h, sem):
                    h.tensor_copy(bc_sb[:], p_bc[:]).then_inc(sem, 1)

                @P.step("vector", waits={"vector": P.now("vector"),
                                         "sync": P.now("sync")}, incs=1)
                def _(h, sem, hg4=hg4, qcols=qcols):
                    h.tensor_tensor(bstg2[hg4 % 2][:], p_av[:], bc_sb[:],
                                    ALU.mult).then_inc(sem, 1)

                @P.step("sync", waits={"vector": P.now("vector")}, incs=16)
                def _(h, sem, hg4=hg4, qi=qi):
                    h.dma_start(out=attn_dst[:, 4*hg4:4*hg4+4, qi*128:(qi+1)*128],
                                in_=bstg2[hg4 % 2][:].rearrange("p (a t) -> p a t", a=4)
                                ).then_inc(sem, 16)

        es2.close()
        hsb = sb(nc, es, "hsb", [128, 22, NTOK], BF16)
        wqs2 = [sb(nc, es, f"wqs{i}", [128, NCH, 128], BF16) for i in range(2)]
        ws3 = [sb(nc, es, f"ws3{i}", [128, NCH, 128], BF16) for i in range(2)]
        wg1b = sb(nc, es, "wg1b", [128, NCH, 8], BF16)
        ld2 = [sb(nc, es, f"ld2_{i}", [128, HALF], F32) for i in range(2)]
        stg2 = [sb(nc, es, f"stg2_{i}", [128, HALF], F32) for i in range(2)]
        sil_t = sb(nc, es, "sil_t", [128, HALF], F32)
        @P.step("sync", waits={"sync": P.now("sync")}, incs=0)
        def _(h, sem):
            pass
        wo_src = wo_t[:, :].rearrange("(hh p) o -> p hh o", p=128)
        attn_src = attn_dram[:, :].rearrange("(hh p) t -> p hh t", p=128)
        x2_src = x2_t[:, :].rearrange("(k p) t -> p k t", p=128)
        x3_dst = x3_dram[:, :].rearrange("(k p) t -> p k t", p=128)
        ld_at = emit_dma_batch(P, "gpsimd", [
            (bufC[:], attn_src),
        ], waits={"sync": P.now("sync"), "tensor": P.now("tensor")})
        for half in range(2):
            cols = slice(half * HALF, (half + 1) * HALF)
            for oc in range(NCH):
                i = half * NCH + oc
                pb = i % 2
                @P.step("gpsimd", waits={"tensor": P.now("tensor")}, incs=16)
                def _(h, sem, pb=pb, oc=oc):
                    h.dma_start(out=wqs2[pb][:], in_=wo_src[:, :, oc*128:(oc+1)*128]
                                ).then_inc(sem, 16)

                @P.step("sync", waits={"vector": P.now("vector")} if i >= 2 else {}, incs=16)
                def _(h, sem, pb=pb, oc=oc, cols=cols):
                    h.dma_start(out=ld2[pb][:], in_=x2_src[:, oc, cols]).then_inc(sem, 16)

                @P.step("tensor", waits={"gpsimd": P.now("gpsimd"),
                                         "vector": P.now("vector")}, incs=1)
                def _(h, sem, pb=pb, oc=oc, cols=cols):
                    for k in range(NCH):
                        ins = h.matmul(p_S[pb][:], wqs2[pb][:, k, :], bufC[:, k, cols],
                                       start=(k == 0), stop=(k == NCH - 1))
                    ins.then_inc(sem, 1)

                @P.step("vector", waits={"tensor": P.now("tensor"),
                                         "sync": P.now("sync")}, incs=1)
                def _(h, sem, pb=pb, cols=cols):
                    h.tensor_tensor(stg2[pb][:], p_S[pb][:], ld2[pb][:],
                                    ALU.add).then_inc(sem, 1)

                @P.step("sync", waits={"vector": P.now("vector")}, incs=16)
                def _(h, sem, pb=pb, oc=oc, cols=cols):
                    h.dma_start(out=x3_dst[:, oc, cols], in_=stg2[pb][:]).then_inc(sem, 16)

        @P.step("sync", waits={"sync": P.now("sync")}, incs=0)
        def _(h, sem):
            pass

        emit_ln_dram(P, x3_dram, bufC, consts, scratch, ld2, gamma_beta=None)

        for k in range(NCH):
            for tt in range(8):
                @P.step("tensor", waits={"vector": P.now("vector")}, incs=1)
                def _(h, sem, k=k, tt=tt):
                    h.transpose(pT[:], bufC[:, k, tt*128:(tt+1)*128], ident_b[:]).then_inc(sem, 1)

                @P.step("vector", waits={"tensor": P.now("tensor"),
                                         "sync": P.now("sync")}, incs=1)
                def _(h, sem, tt=tt):
                    h.tensor_copy(ldb2[tt % 2][:, 0:128], pT[:]).then_inc(sem, 1)

                @P.step("sync", waits={"vector": P.now("vector")}, incs=16)
                def _(h, sem, k=k, tt=tt):
                    h.dma_start(out=xn4tm_out[tt*128:(tt+1)*128, k*128:(k+1)*128],
                                in_=ldb2[tt % 2][:, 0:128]).then_inc(sem, 16)

        ld_wg = emit_dma_batch(P, "gpsimd", [
            (wg1b[:], wg1[:, :].rearrange("(k p) r -> p k r", p=128)),
        ], waits={})
        for tt in range(8):
            @P.step("tensor", waits={"gpsimd": ld_wg, "vector": P.now("vector")}, incs=1)
            def _(h, sem, tt=tt):
                for k in range(NCH):
                    ins = h.matmul(pC[:, 0:8], bufC[:, k, tt*128:(tt+1)*128],
                                   wg1b[:, k, :], start=(k == 0), stop=(k == NCH - 1))
                ins.then_inc(sem, 1)

            @P.step("vector", waits={"tensor": P.now("tensor"),
                                     "sync": P.now("sync")}, incs=1)
            def _(h, sem):
                h.tensor_copy(logit_stg[:], pC[:, 0:8]).then_inc(sem, 1)

            @P.step("sync", waits={"vector": P.now("vector")}, incs=16)
            def _(h, sem, tt=tt):
                h.dma_start(out=logits_out[tt*128:(tt+1)*128, :],
                            in_=logit_stg[:]).then_inc(sem, 16)

        sw1_src = sw1[:, :].rearrange("(k p) o -> p k o", p=128)
        sw3_src = sw3[:, :].rearrange("(k p) o -> p k o", p=128)
        sw2_src = sw2[:, :].rearrange("(k p) o -> p k o", p=128)
        for hc in range(22):
            wb = hc % 2
            lw = emit_dma_batch(P, "gpsimd", [
                (wqs2[wb][:], sw1_src[:, :, hc*128:(hc+1)*128]),
                (ws3[wb][:], sw3_src[:, :, hc*128:(hc+1)*128]),
            ], waits={"tensor": P.now("tensor")})
            for half in range(2):
                cols = slice(half * HALF, (half + 1) * HALF)
                i = hc * 2 + half
                @P.step("tensor", waits={"gpsimd": lw, "vector": P.now("vector"),
                                         "scalar": P.now("scalar")}, incs=1)
                def _(h, sem, wb=wb, cols=cols):
                    for k in range(NCH):
                        h.matmul(pA[:], wqs2[wb][:, k, :], bufC[:, k, cols],
                                 start=(k == 0), stop=(k == NCH - 1))
                    for k in range(NCH):
                        ins = h.matmul(pB[:], ws3[wb][:, k, :], bufC[:, k, cols],
                                       start=(k == 0), stop=(k == NCH - 1))
                    ins.then_inc(sem, 1)

                @P.step("scalar", waits={"tensor": P.now("tensor")}, incs=1)
                def _(h, sem):
                    h.activation(sil_t[:], pA[:], AF.Silu, bias=0.0,
                                 scale=1.0).then_inc(sem, 1)

                @P.step("vector", waits={"scalar": P.now("scalar")}, incs=1)
                def _(h, sem, hc=hc, cols=cols):
                    h.tensor_tensor(hsb[:, hc, cols], sil_t[:], pB[:],
                                    ALU.mult).then_inc(sem, 1)

        for half in range(2):
            cols = slice(half * HALF, (half + 1) * HALF)
            for oc in range(NCH):
                i = half * NCH + oc
                pb = i % 2
                @P.step("gpsimd", waits={"tensor": P.now("tensor")}, incs=16)
                def _(h, sem, pb=pb, oc=oc):
                    h.dma_start(out=wqs2[pb][:, 0:NCH, :],
                                in_=sw2_src[:, 0:NCH, oc*128:(oc+1)*128]).then_inc(sem, 16)

                @P.step("gpsimd", waits={}, incs=16)
                def _(h, sem, pb=pb, oc=oc):
                    h.dma_start(out=ws3[pb][:, 0:6, :],
                                in_=sw2_src[:, NCH:22, oc*128:(oc+1)*128]).then_inc(sem, 16)

                @P.step("sync", waits={"vector": P.now("vector")} if i >= 2 else {}, incs=16)
                def _(h, sem, pb=pb, oc=oc, cols=cols):
                    h.dma_start(out=ld2[pb][:], in_=x3_dst[:, oc, cols]).then_inc(sem, 16)

                @P.step("tensor", waits={"gpsimd": P.now("gpsimd"),
                                         "vector": P.now("vector")}, incs=1)
                def _(h, sem, pb=pb, oc=oc, cols=cols):
                    for k in range(22):
                        wt = wqs2[pb][:, k, :] if k < NCH else ws3[pb][:, k - NCH, :]
                        ins = h.matmul(p_S[pb][:], wt, hsb[:, k, cols],
                                       start=(k == 0), stop=(k == 21))
                    ins.then_inc(sem, 1)

                @P.step("vector", waits={"tensor": P.now("tensor"),
                                         "sync": P.now("sync")}, incs=1)
                def _(h, sem, pb=pb, cols=cols):
                    h.tensor_tensor(stg2[pb][:], p_S[pb][:], ld2[pb][:],
                                    ALU.add).then_inc(sem, 1)

                @P.step("sync", waits={"vector": P.now("vector")}, incs=16)
                def _(h, sem, pb=pb, oc=oc, cols=cols):
                    h.dma_start(out=x3b_out[:, :].rearrange("(k p) t -> p k t", p=128)[:, oc, cols],
                                in_=stg2[pb][:]).then_inc(sem, 16)

        @P.step("gpsimd", waits={"sync": P.now("sync"), "gpsimd": P.now("gpsimd"),
                                 "vector": P.now("vector")}, incs=0)
        def _(h, sem):
            pass

        P.run()
    return nc


def build_p4():
    nc = bass.Bass("TRN2", target_bir_lowering=False)
    es = contextlib.ExitStack()
    def din(name, shape, dtype=F32):
        return nc.dram_tensor(name, shape, dtype, kind="ExternalInput").ap()
    def dout(name, shape, dtype=F32):
        return nc.dram_tensor(name, shape, dtype, kind="ExternalOutput").ap()

    TALL = B * S
    xn4_all = din("xn4_all", [TALL, D], BF16)
    logits_all = din("logits_all", [TALL, 8])
    w1e = din("w1e", [D, HMOE])
    w3e = din("w3e", [D, HMOE])
    w2e = din("w2e", [HMOE, D])
    eoff = din("eoff", [1, 1])
    c_ones1_f = din("c_ones1_f", [1, 128])
    c_ones8_f = din("c_ones8_f", [8, 1])
    c_evec = din("c_evec", [8, 1])
    c_ones128x8 = din("c_ones128x8", [128, 8])
    c_ident_b = din("c_ident_b", [128, 128], BF16)
    c_tokid = din("c_tokid", [128, 256], U32)
    c_bigidx = din("c_bigidx", [CAP, 4], U32)
    c_zeros2048 = din("c_zeros2048", [128, D], BF16)

    oe_out = dout("oe_out", [CAP, D], BF16)
    flat_out = dout("flat_out", [1, TALL])
    gates_out = dout("gates_out", [128, 64])
    keep_out = dout("keep_out", [1, TALL])
    idxmap_dram = nc.dram_tensor("idxmap", [CAP, 4], U32, kind="Internal").ap()

    with es:
        ones1_f = sb(nc, es, "ones1_f", [1, 128], F32)
        ones8_f = sb(nc, es, "ones8_f", [8, 1], F32)
        evec = sb(nc, es, "evec", [8, 1], F32)
        ones128x8 = sb(nc, es, "ones128x8", [128, 8], F32)
        ident_b = sb(nc, es, "ident_b", [128, 128], BF16)
        tokidT = sb(nc, es, "tokidT", [128, 256], U32)
        zeros2048 = sb(nc, es, "zeros2048", [128, D], BF16)
        eoff_sb = sb(nc, es, "eoff_sb", [1, 1], F32)
        xd_fm = sb(nc, es, "xd_fm", [128, NCH, NTOK], BF16)
        bstg2 = [sb(nc, es, f"bstg2_{i}", [128, HALF], BF16) for i in range(2)]
        sil_t = sb(nc, es, "sil_t", [128, HALF], F32)
        pA = ps(nc, es, "pA", [128, HALF])
        pB = ps(nc, es, "pB", [128, HALF])
        pC = ps(nc, es, "pC", [128, HALF])
        pE = ps(nc, es, "pE", [1, HALF])
        pT = ps(nc, es, "pT", [128, 128], BF16)

        P = Prog(nc, es)
        ld_f = emit_dma_batch(P, "sync", [
            (ones1_f[:], c_ones1_f[:, :]),
            (ones8_f[:], c_ones8_f[:, :]),
            (evec[:], c_evec[:, :]),
            (ones128x8[:], c_ones128x8[:, :]),
            (ident_b[:], c_ident_b[:, :]),
            (tokidT[:], c_tokid[:, :]),
            (zeros2048[:], c_zeros2048[:, :]),
            (eoff_sb[:], eoff[:, :]),
        ])
        regs = {}
        for e in ("scalar", "tensor", "vector"):
            @P.step(e, waits={"sync": ld_f}, incs=0)
            def _(h, sem):
                pass

        @P.step("gpsimd", waits={"sync": ld_f}, incs=0)
        def _(h, sem):
            regs["b_cap"] = h.to_reg(CAP - 1)
            regs["b_tall"] = h.to_reg(B * S - 1)

        es2 = contextlib.ExitStack()
        lg_t = sb(nc, es2, "lg_t", [128, 8], F32)
        mx8 = sb(nc, es2, "mx8", [128, 1], F32)
        sh8 = sb(nc, es2, "sh8", [128, 8], F32)
        expd = sb(nc, es2, "expd", [128, 8], F32)
        se8 = sb(nc, es2, "se8", [128, 1], F32)
        rc8 = sb(nc, es2, "rc8", [128, 1], F32)
        ohd = sb(nc, es2, "ohd", [128, 8], F32)
        cs8 = sb(nc, es2, "cs8", [128, 8], F32)
        ohf = sb(nc, es2, "ohf", [128, 8], BF16)
        g_all = sb(nc, es2, "g_all", [128, 64], F32)
        ohT_seg = sb(nc, es2, "ohT_seg", [8, NTOK], F32)
        cs_seg = sb(nc, es2, "cs_seg", [8, NTOK], F32)
        ones_seg = sb(nc, es2, "ones_seg", [8, NTOK], F32)
        tmp_seg = sb(nc, es2, "tmp_seg", [8, NTOK], F32)
        tmp2_seg = sb(nc, es2, "tmp2_seg", [8, NTOK], F32)
        kp_seg = sb(nc, es2, "kp_seg", [8, NTOK], F32)
        kpo_seg = sb(nc, es2, "kpo_seg", [8, NTOK], F32)
        cs_init = sb(nc, es2, "cs_init", [8, 1], F32)
        flat0_r = sb(nc, es2, "flat0_r", [1, NTOK], F32)
        ktok_r = sb(nc, es2, "ktok_r", [1, NTOK], F32)
        t1_r = sb(nc, es2, "t1_r", [1, NTOK], F32)
        fsend_r = sb(nc, es2, "fsend_r", [1, NTOK], F32)
        myidx_r = sb(nc, es2, "myidx_r", [1, NTOK], F32)
        negm_r = sb(nc, es2, "negm_r", [1, NTOK], F32)
        myidx2_r = sb(nc, es2, "myidx2_r", [1, NTOK], F32)
        myidxT = sb(nc, es2, "myidxT", [128, 64], U32)
        idx_sb = sb(nc, es2, "idx_sb", [128, 8, 4], U32)
        xd_tm = sb(nc, es2, "xd_tm", [128, 8, D], BF16)

        @P.step("vector", waits={}, incs=1)
        def _(h, sem):
            h.memset(cs_init[:], 0.0)
            h.drain().then_inc(sem, 1)

        for seg in range(8):
            for tt in range(8):
                gt = seg * 8 + tt
                ll = emit_dma_batch(P, "sync", [
                    (lg_t[:], logits_all[gt*128:(gt+1)*128, :]),
                ], waits={"scalar": P.now("scalar"), "vector": P.now("vector")})

                @P.step("vector", waits={"sync": ll}, incs=1)
                def _(h, sem):
                    h.reduce_max(mx8[:], lg_t[:], axis=AX.X).then_inc(sem, 1)

                @P.step("vector", waits={"vector": P.now("vector")}, incs=1)
                def _(h, sem):
                    h.tensor_scalar(ohd[:], lg_t[:], mx8[:], None, ALU.is_ge)
                    h.tensor_scalar(sh8[:], lg_t[:], mx8[:], None, ALU.subtract)
                    h.drain().then_inc(sem, 1)

                @P.step("scalar", waits={"vector": P.now("vector")}, incs=1)
                def _(h, sem):
                    h.activation(expd[:], sh8[:], AF.Exp, bias=0.0,
                                 scale=1.0).then_inc(sem, 1)

                @P.step("vector", waits={"scalar": P.now("scalar")}, incs=1)
                def _(h, sem):
                    h.reduce_sum(se8[:], expd[:], axis=AX.X).then_inc(sem, 1)

                @P.step("vector", waits={"vector": P.now("vector")}, incs=1)
                def _(h, sem, gt=gt):
                    h.reciprocal(rc8[:], se8[:])
                    h.tensor_tensor_scan(cs8[:], ones128x8[:], ohd[:], 0.0,
                                         ALU.mult, ALU.add)
                    h.drain().then_inc(sem, 1)

                @P.step("vector", waits={"vector": P.now("vector")}, incs=1)
                def _(h, sem, gt=gt):
                    h.tensor_copy(g_all[:, gt:gt+1], rc8[:])
                    h.scalar_tensor_tensor(ohf[:], cs8[:], 1.0, ohd[:],
                                           ALU.is_equal, ALU.mult)
                    h.drain().then_inc(sem, 1)

                @P.step("tensor", waits={"vector": P.now("vector")}, incs=1)
                def _(h, sem):
                    h.transpose(pT[0:8, :], ohf[:], ident_b[:]).then_inc(sem, 1)

                @P.step("vector", waits={"tensor": P.now("tensor")}, incs=1)
                def _(h, sem, tt=tt):
                    h.tensor_copy(ohT_seg[:, tt*128:(tt+1)*128], pT[0:8, :]).then_inc(sem, 1)

            @P.step("vector", waits={"vector": P.now("vector")}, incs=1)
            def _(h, sem, seg=seg):
                h.tensor_scalar(ones_seg[:], ohT_seg[:], 0.0, 1.0, ALU.mult, ALU.add)
                h.drain()
                h.tensor_tensor_scan(cs_seg[:], ones_seg[:], ohT_seg[:], cs_init[:],
                                     ALU.mult, ALU.add)
                h.drain()
                h.tensor_copy(cs_init[:], cs_seg[:, NTOK-1:NTOK])
                h.tensor_scalar(tmp_seg[:], cs_seg[:], evec[:], None, ALU.add)
                h.tensor_scalar(kp_seg[:], cs_seg[:], float(CAP) + 0.5, None, ALU.is_le)
                h.drain()
                h.tensor_tensor(tmp2_seg[:], tmp_seg[:], ohT_seg[:], ALU.mult)
                h.tensor_tensor(kpo_seg[:], kp_seg[:], ohT_seg[:], ALU.mult)
                h.drain().then_inc(sem, 1)

            for half in range(2):
                @P.step("tensor", waits={"vector": P.now("vector")}, incs=1)
                def _(h, sem, half=half):
                    h.matmul(pE[:], ones8_f[:], tmp2_seg[:, half*HALF:(half+1)*HALF],
                             start=True, stop=True).then_inc(sem, 1)

                @P.step("vector", waits={"tensor": P.now("tensor")}, incs=1)
                def _(h, sem, half=half):
                    h.tensor_copy(flat0_r[0:1, half*HALF:(half+1)*HALF], pE[:]).then_inc(sem, 1)

                @P.step("tensor", waits={"vector": P.now("vector")}, incs=1)
                def _(h, sem, half=half):
                    h.matmul(pE[:], ones8_f[:], kpo_seg[:, half*HALF:(half+1)*HALF],
                             start=True, stop=True).then_inc(sem, 1)

                @P.step("vector", waits={"tensor": P.now("tensor")}, incs=1)
                def _(h, sem, half=half):
                    h.tensor_copy(ktok_r[0:1, half*HALF:(half+1)*HALF], pE[:]).then_inc(sem, 1)

            @P.step("vector", waits={"vector": P.now("vector")}, incs=1)
            def _(h, sem):
                h.tensor_scalar(t1_r[:], ktok_r[:], -BIG, BIG, ALU.mult, ALU.add)
                h.drain()
                h.tensor_tensor(fsend_r[:], flat0_r[:], t1_r[:], ALU.add)
                h.drain()
                h.tensor_scalar(myidx_r[:], fsend_r[:], eoff_sb[:], None, ALU.subtract)
                h.drain()
                h.tensor_scalar(negm_r[:], myidx_r[:], 0.0, None, ALU.is_lt)
                h.drain()
                h.scalar_tensor_tensor(myidx2_r[:], negm_r[:], 1.0e6, myidx_r[:],
                                       ALU.mult, ALU.add)
                h.drain().then_inc(sem, 1)

            st_f = emit_dma_batch(P, "sync", [
                (flat_out[0:1, seg*NTOK:(seg+1)*NTOK], fsend_r[:]),
                (keep_out[0:1, seg*NTOK:(seg+1)*NTOK], ktok_r[:]),
            ], waits={"vector": P.now("vector")})

            for g in range(8):
                @P.step("tensor", waits={"vector": P.now("vector")}, incs=1)
                def _(h, sem, g=g):
                    h.transpose(pC[:, 0:1], myidx2_r[0:1, g*128:(g+1)*128],
                                ones1_f[0:1, 0:1]).then_inc(sem, 1)

                @P.step("vector", waits={"tensor": P.now("tensor")}, incs=1)
                def _(h, sem, g=g, seg=seg):
                    h.tensor_copy(myidxT[:, seg*8+g:seg*8+g+1], pC[:, 0:1]).then_inc(sem, 1)

        st_g = emit_dma_batch(P, "sync", [
            (gates_out[:, :], g_all[:]),
        ], waits={"vector": P.now("vector")})

        ini = emit_dma_batch(P, "gpsimd", [
            (idxmap_dram[:, :], c_bigidx[:, :]),
        ], waits={})

        @P.step("gpsimd", waits={"gpsimd": ini, "vector": P.now("vector")}, incs=16*64)
        def _(h, sem):
            for g in range(64):
                h.indirect_dma_start(
                    out=idxmap_dram[:, :], out_offset=bass.IndirectOffsetOnAxis(
                        ap=myidxT[:, g:g+1], axis=0),
                    in_=tokidT[:, 4*g:4*g+4], in_offset=None,
                    bounds_check=regs["b_cap"], oob_is_err=False).then_inc(sem, 16)

        ld_idx = emit_dma_batch(P, "gpsimd", [
            (idx_sb[:, g, :], idxmap_dram[g*128:(g+1)*128, :]) for g in range(8)
        ], waits={"gpsimd": P.now("gpsimd")})
        zr = emit_dma_batch(P, "gpsimd", [
            (xd_tm[:, g, :], zeros2048[:, :]) for g in range(8)
        ], waits={})

        @P.step("gpsimd", waits={"gpsimd": P.now("gpsimd")}, incs=16*8)
        def _(h, sem):
            for g in range(8):
                h.indirect_dma_start(
                    out=xd_tm[:, g, :], out_offset=None, in_=xn4_all[:, :],
                    in_offset=bass.IndirectOffsetOnAxis(ap=idx_sb[:, g, 0:1], axis=0),
                    bounds_check=regs["b_tall"], oob_is_err=False).then_inc(sem, 16)
        g_xd = P.now("gpsimd")

        for g in range(8):
            for k in range(NCH):
                @P.step("tensor", waits={"gpsimd": g_xd} if (g == 0 and k == 0) else
                        {"vector": P.now("vector")}, incs=1)
                def _(h, sem, g=g, k=k):
                    h.transpose(pT[:], xd_tm[:, g, k*128:(k+1)*128],
                                ident_b[:]).then_inc(sem, 1)

                @P.step("vector", waits={"tensor": P.now("tensor")}, incs=1)
                def _(h, sem, g=g, k=k):
                    h.tensor_copy(xd_fm[:, k, g*128:(g+1)*128], pT[:]).then_inc(sem, 1)

        es2.close()
        hb = sb(nc, es, "hb", [128, 22, NTOK], BF16)
        oe_fm = sb(nc, es, "oe_fm", [128, NCH, NTOK], BF16)
        wqs2 = [sb(nc, es, f"wqs{i}", [128, NCH, 128], BF16) for i in range(2)]
        ws3 = [sb(nc, es, f"ws3{i}", [128, NCH, 128], BF16) for i in range(2)]
        w1_src = w1e[:, :].rearrange("(k p) o -> p k o", p=128)
        w3_src = w3e[:, :].rearrange("(k p) o -> p k o", p=128)
        w2_src = w2e[:, :].rearrange("(k p) o -> p k o", p=128)
        for hc in range(22):
            wb = hc % 2
            lw = emit_dma_batch(P, "gpsimd", [
                (wqs2[wb][:], w1_src[:, :, hc*128:(hc+1)*128]),
                (ws3[wb][:], w3_src[:, :, hc*128:(hc+1)*128]),
            ], waits={"tensor": P.now("tensor")})
            for half in range(2):
                cols = slice(half * HALF, (half + 1) * HALF)
                @P.step("tensor", waits={"gpsimd": lw, "vector": P.now("vector"),
                                         "scalar": P.now("scalar")}, incs=1)
                def _(h, sem, wb=wb, cols=cols):
                    for k in range(NCH):
                        h.matmul(pA[:], wqs2[wb][:, k, :], xd_fm[:, k, cols],
                                 start=(k == 0), stop=(k == NCH - 1))
                    for k in range(NCH):
                        ins = h.matmul(pB[:], ws3[wb][:, k, :], xd_fm[:, k, cols],
                                       start=(k == 0), stop=(k == NCH - 1))
                    ins.then_inc(sem, 1)

                @P.step("scalar", waits={"tensor": P.now("tensor")}, incs=1)
                def _(h, sem):
                    h.activation(sil_t[:], pA[:], AF.Silu, bias=0.0,
                                 scale=1.0).then_inc(sem, 1)

                @P.step("vector", waits={"scalar": P.now("scalar")}, incs=1)
                def _(h, sem, hc=hc, cols=cols):
                    h.tensor_tensor(hb[:, hc, cols], sil_t[:], pB[:],
                                    ALU.mult).then_inc(sem, 1)

        for half in range(2):
            cols = slice(half * HALF, (half + 1) * HALF)
            for oc in range(NCH):
                i = half * NCH + oc
                pb = i % 2
                @P.step("gpsimd", waits={"tensor": P.now("tensor")}, incs=32)
                def _(h, sem, pb=pb, oc=oc):
                    h.dma_start(out=wqs2[pb][:, 0:NCH, :],
                                in_=w2_src[:, 0:NCH, oc*128:(oc+1)*128]).then_inc(sem, 16)
                    h.dma_start(out=ws3[pb][:, 0:6, :],
                                in_=w2_src[:, NCH:22, oc*128:(oc+1)*128]).then_inc(sem, 16)

                @P.step("tensor", waits={"gpsimd": P.now("gpsimd"),
                                         "vector": P.now("vector")}, incs=1)
                def _(h, sem, pb=pb, oc=oc, cols=cols):
                    for k in range(22):
                        wt = wqs2[pb][:, k, :] if k < NCH else ws3[pb][:, k - NCH, :]
                        ins = h.matmul(pA[:] if pb == 0 else pB[:], wt, hb[:, k, cols],
                                       start=(k == 0), stop=(k == 21))
                    ins.then_inc(sem, 1)

                @P.step("vector", waits={"tensor": P.now("tensor")}, incs=1)
                def _(h, sem, pb=pb, oc=oc, cols=cols):
                    h.tensor_copy(oe_fm[:, oc, cols], pA[:] if pb == 0 else pB[:]).then_inc(sem, 1)

        for g in range(8):
            for k in range(NCH):
                @P.step("tensor", waits={"vector": P.now("vector")}, incs=1)
                def _(h, sem, g=g, k=k):
                    h.transpose(pT[:], oe_fm[:, k, g*128:(g+1)*128],
                                ident_b[:]).then_inc(sem, 1)

                @P.step("vector", waits={"tensor": P.now("tensor"),
                                         "sync": P.now("sync")}, incs=1)
                def _(h, sem, g=g, k=k):
                    h.tensor_copy(bstg2[(g*NCH+k) % 2][:, 0:128], pT[:]).then_inc(sem, 1)

                @P.step("sync", waits={"vector": P.now("vector")}, incs=16)
                def _(h, sem, g=g, k=k):
                    h.dma_start(out=oe_out[g*128:(g+1)*128, k*128:(k+1)*128],
                                in_=bstg2[(g*NCH+k) % 2][:, 0:128]).then_inc(sem, 16)

        @P.step("gpsimd", waits={"sync": P.now("sync"), "gpsimd": P.now("gpsimd"),
                                 "vector": P.now("vector")}, incs=0)
        def _(h, sem):
            pass

        P.run()
    return nc


def build_p5():
    nc = bass.Bass("TRN2", target_bir_lowering=False)
    es = contextlib.ExitStack()
    def din(name, shape, dtype=F32):
        return nc.dram_tensor(name, shape, dtype, kind="ExternalInput").ap()
    def dout(name, shape, dtype=F32):
        return nc.dram_tensor(name, shape, dtype, kind="ExternalOutput").ap()

    TALL = B * S
    x3b_t = din("x3b_t", [D, NTOK])
    oe_all = din("oe_all", [TALL, D], BF16)
    flat_tm = din("flat_tm", [128, 8])
    gates_tm = din("gates_tm", [128, 8])
    keep_tm = din("keep_tm", [128, 8])
    c_ident_b = din("c_ident_b", [128, 128], BF16)
    x4_out = dout("x4_out", [D, NTOK])

    with es:
        ident_b = sb(nc, es, "ident_b", [128, 128], BF16)
        flat_sb = sb(nc, es, "flat_sb", [128, 8], F32)
        flat_cl = sb(nc, es, "flat_cl", [128, 8], F32)
        gates_sb = sb(nc, es, "gates_sb", [128, 8], F32)
        keep_sb = sb(nc, es, "keep_sb", [128, 8], F32)
        gk_sb = sb(nc, es, "gk_sb", [128, 8], F32)
        idx_u = sb(nc, es, "idx_u", [128, 8], U32)
        og_tm = sb(nc, es, "og_tm", [128, 8, D], BF16)
        og_gk = sb(nc, es, "og_gk", [128, 8, D], BF16)
        ld2 = [sb(nc, es, f"ld2_{i}", [128, 128], F32) for i in range(2)]
        stg2 = [sb(nc, es, f"stg2_{i}", [128, 128], F32) for i in range(2)]
        pT = ps(nc, es, "pT", [128, 128], BF16)

        P = Prog(nc, es)
        ld_f = emit_dma_batch(P, "sync", [
            (ident_b[:], c_ident_b[:, :]),
            (flat_sb[:], flat_tm[:, :]),
            (gates_sb[:], gates_tm[:, :]),
            (keep_sb[:], keep_tm[:, :]),
        ])
        @P.step("vector", waits={"sync": ld_f}, incs=1)
        def _(h, sem):
            h.tensor_tensor(gk_sb[:], gates_sb[:], keep_sb[:], ALU.mult)
            h.tensor_scalar(flat_cl[:], flat_sb[:], float(TALL - 1), None, ALU.min)
            h.drain()
            h.tensor_copy(idx_u[:], flat_cl[:])
            h.drain().then_inc(sem, 1)

        @P.step("gpsimd", waits={"vector": P.now("vector")}, incs=16*8)
        def _(h, sem):
            for g in range(8):
                h.indirect_dma_start(
                    out=og_tm[:, g, :], out_offset=None, in_=oe_all[:, :],
                    in_offset=bass.IndirectOffsetOnAxis(ap=idx_u[:, g:g+1], axis=0),
                    bounds_check=TALL - 1, oob_is_err=True).then_inc(sem, 16)
        g_og = P.now("gpsimd")

        for g in range(8):
            @P.step("vector", waits={"gpsimd": g_og} if g == 0 else
                    {"vector": P.now("vector")}, incs=1)
            def _(h, sem, g=g):
                h.tensor_scalar(og_gk[:, g, :], og_tm[:, g, :],
                                gk_sb[:, g:g+1], None, ALU.mult).then_inc(sem, 1)
        v_gk = P.now("vector")

        x3_src = x3b_t[:, :].rearrange("(k p) t -> p k t", p=128)
        x4_dst = x4_out[:, :].rearrange("(k p) t -> p k t", p=128)
        for g in range(8):
            for k in range(NCH):
                i = g * NCH + k
                pb = i % 2
                @P.step("sync", waits={"vector": P.now("vector")} if i >= 2 else {"vector": v_gk},
                        incs=16)
                def _(h, sem, pb=pb, g=g, k=k):
                    h.dma_start(out=ld2[pb][:], in_=x3_src[:, k, g*128:(g+1)*128]
                                ).then_inc(sem, 16)

                @P.step("tensor", waits={"vector": v_gk if i < 2 else P.now("vector")},
                        incs=1)
                def _(h, sem, g=g, k=k):
                    h.transpose(pT[:], og_gk[:, g, k*128:(k+1)*128],
                                ident_b[:]).then_inc(sem, 1)

                @P.step("vector", waits={"tensor": P.now("tensor"),
                                         "sync": P.now("sync")}, incs=1)
                def _(h, sem, pb=pb):
                    h.tensor_tensor(stg2[pb][:], pT[:], ld2[pb][:], ALU.add).then_inc(sem, 1)

                @P.step("sync", waits={"vector": P.now("vector")}, incs=16)
                def _(h, sem, pb=pb, g=g, k=k):
                    h.dma_start(out=x4_dst[:, k, g*128:(g+1)*128],
                                in_=stg2[pb][:]).then_inc(sem, 16)

        @P.step("gpsimd", waits={"sync": P.now("sync")}, incs=0)
        def _(h, sem):
            pass

        P.run()
    return nc



def core_chunks(core):
    cc = core % 4
    return [4 * m + cc for m in range(8)]


def core_token_ids(core):
    idx = []
    for c in core_chunks(core):
        idx.extend(range(128 * c, 128 * c + 128))
    return np.asarray(idx)


def host_slices(x, core):
    b = core // 4
    return x[b][core_token_ids(core)]


def bf16_np():
    import ml_dtypes
    return ml_dtypes.bfloat16


def rope_tables(core):
    pos = core_token_ids(core).astype(np.float64)
    inv = 1.0 / 10000 ** (np.arange(0, HD, 2) / HD)
    invd = np.concatenate([inv, inv])
    ang = invd[:, None] * pos[None, :]
    return (np.cos(ang).astype(np.float32),
            np.sin(ang).astype(np.float32))


def p2_inputs(z, core, res1):
    bf = bf16_np()
    b = core // 4
    cc = core % 4
    u_glob = np.zeros((8, S), np.float32)
    for c2 in range(4 * b, 4 * b + 4):
        u_glob[:, core_token_ids(c2)] = res1[c2]["u_out"]
    cos, sin = rope_tables(core)
    ind4 = np.zeros((128, 512), np.float32)
    ind4[:, cc * 128:(cc + 1) * 128] = np.eye(128)
    msel = np.zeros((8, 512), np.float32)
    for er in range(512):
        msel[er // 64, er] = 1.0
    hgsel = np.zeros((128, 4), np.float32)
    for j in range(4):
        hgsel[32 * j:32 * (j + 1), j] = 1.0
    return {
        "x_t": feat_major(host_slices(z["L0_attn_x"], core)).astype(np.float32),
        "xln1_in": res1[core]["xln1_out"],
        "xn_in": res1[core]["xn_out"],
        "u_all": u_glob,
        "dlogit": np.ascontiguousarray(z["L0_attn_decay_logit"][:, None]),
        "w_out": np.ascontiguousarray(z["L0_attn_W_out"]),
        "b_out_t": vec_col(z["L0_attn_b_out"]),
        "d_skip_t": vec_col(z["L0_attn_D_skip"]),
        "gamma_t": vec_col(z["L0_attn_ln_gamma"]),
        "beta_t": vec_col(z["L0_attn_ln_beta"]),
        "w_down": np.ascontiguousarray(z["L0_attn_W_down"]),
        "b_down": np.ascontiguousarray(z["L0_attn_b_down"][:, None]),
        "w_up": np.ascontiguousarray(z["L0_attn_W_up"]),
        "b_up_t": vec_col(z["L0_attn_b_up"]),
        "a_t": np.ascontiguousarray(z["L0_ffn_A"].reshape(512, D).T),
        "bw_t": np.ascontiguousarray(z["L0_ffn_Bw"].transpose(0, 2, 1).reshape(512, D)),
        "wg0_t": np.ascontiguousarray(z["L0_ffn_Wg"].T),
        "wq_t": np.ascontiguousarray(z["L1_attn_Wq"].T),
        "wk_t": np.ascontiguousarray(z["L1_attn_Wk"].T),
        "wv_t": np.ascontiguousarray(z["L1_attn_Wv"].T),
        "wrq_t": np.ascontiguousarray(z["L1_attn_Wrq"].T),
        "wrk_t": np.ascontiguousarray(z["L1_attn_Wrk"].T),
        "qnw": np.ascontiguousarray(z["L1_attn_q_norm_w"][:, None]),
        "knw": np.ascontiguousarray(z["L1_attn_k_norm_w"][:, None]),
        "cosq": cos, "sinq": sin,
        "c_ones128_f": np.ones((128, 1), np.float32),
        "c_ones1_f": np.ones((1, 128), np.float32),
        "c_ones8v": np.ones((8, S), np.float32),
        "c_ones128x8": np.ones((128, 8), np.float32),
        "c_ident_b": np.eye(128).astype(bf),
        "c_ind4": ind4.astype(bf),
        "c_msel": msel.astype(bf),
        "c_hgsel": hgsel,
    }


def p3_inputs(z, core, res2):
    bf = bf16_np()
    b = core // 4
    cc = core % 4
    cores_b = list(range(4 * b, 4 * b + 4))
    K_full = np.zeros((512, S), bf)
    V_full = np.zeros((S, 512), bf)
    Rq_full = np.zeros((384, S), bf)
    Rk_full = np.zeros((384, S), bf)
    kn2_full = np.zeros((KVH, S), np.float32)
    rqn2_full = np.zeros((12, S), np.float32)
    rkn2_full = np.zeros((12, S), np.float32)
    for c2 in cores_b:
        ids = core_token_ids(c2)
        K_full[:, ids] = res2[c2]["k_out"]
        V_full[ids, :] = res2[c2]["v_out"]
        Rq_full[:, ids] = res2[c2]["rq_out"]
        Rk_full[:, ids] = res2[c2]["rk_out"]
        kn2_full[:, ids] = res2[c2]["kn2_out"]
        rqn2_full[:, ids] = res2[c2]["rqn2_out"]
        rkn2_full[:, ids] = res2[c2]["rkn2_out"]
    masks = np.zeros((8, 128, 4, 512), np.float32)
    for i in range(8):
        qg = 128 * (4 * i + cc) + np.arange(128)
        for kc in range(4):
            kg = 512 * i + 128 * kc + np.arange(128)
            m = (kg[:, None] <= qg[None, :]).astype(np.float32)
            masks[i, :, kc, :] = np.tile(m, (1, 4))
    gsel = np.zeros((12, 512), np.float32)
    for hh in range(12):
        gsel[hh, 32*hh:32*(hh+1)] = 0.5 if hh < 4 else 1.0
    return {
        "x2_t": res2[core]["x2_out"],
        "q_in": res2[core]["q_out"],
        "k_full": K_full, "v_full": V_full,
        "rq_full": Rq_full, "rk_full": Rk_full,
        "rq_own": res2[core]["rq_out"], "rk_own": res2[core]["rk_out"],
        "qn2_own": res2[core]["qn2_out"],
        "kn2_full": kn2_full,
        "rqn2_own": res2[core]["rqn2_out"], "rkn2_own": res2[core]["rkn2_out"],
        "rqn2_full": rqn2_full, "rkn2_full": rkn2_full,
        "gs12": np.ascontiguousarray(z["L1_attn_graph_scale"][:, None]),
        "masks": masks.astype(bf),
        "wo_t": np.ascontiguousarray(z["L1_attn_Wo"].T),
        "wg1": np.ascontiguousarray(z["L1_ffn_Wg"]),
        "sw1": np.ascontiguousarray(z["L1_ffn_sw1"]),
        "sw3": np.ascontiguousarray(z["L1_ffn_sw3"]),
        "sw2": np.ascontiguousarray(z["L1_ffn_sw2"]),
        "c_ones128_f": np.ones((128, 1), np.float32),
        "c_ones1_f": np.ones((1, 128), np.float32),
        "c_ones128_b": np.ones((128, 1), bf),
        "c_ones1_b": np.ones((1, 2048), bf),
        "c_ident_b": np.eye(128).astype(bf),
        "c_gsel": gsel,
    }


def p4_inputs(z, core, res3):
    bf = bf16_np()
    TALL = B * S
    xn4_all = np.zeros((TALL, D), bf)
    logits_all = np.zeros((TALL, 8), np.float32)
    for c2 in range(8):
        gids = 4096 * (c2 // 4) + core_token_ids(c2)
        xn4_all[gids] = res3[c2]["xn4tm_out"]
        logits_all[gids] = res3[c2]["logits_out"]
    tokid = (np.arange(64)[None, :] * 128 + np.arange(128)[:, None]).astype(np.uint32)
    tokid = np.repeat(tokid, 4, axis=1)
    return {
        "xn4_all": xn4_all,
        "logits_all": logits_all,
        "w1e": np.ascontiguousarray(z["L1_ffn_W1"][core]),
        "w3e": np.ascontiguousarray(z["L1_ffn_W3"][core]),
        "w2e": np.ascontiguousarray(z["L1_ffn_W2"][core]),
        "eoff": np.full((1, 1), 1024.0 * core, np.float32),
        "c_ones1_f": np.ones((1, 128), np.float32),
        "c_ones8_f": np.ones((8, 1), np.float32),
        "c_evec": (1024.0 * np.arange(8) - 1.0).astype(np.float32)[:, None],
        "c_ones128x8": np.ones((128, 8), np.float32),
        "c_ident_b": np.eye(128).astype(bf),
        "c_tokid": tokid,
        "c_bigidx": np.full((CAP, 4), 1_000_000_000, np.uint32),
        "c_zeros2048": np.zeros((128, D), bf),
    }


def p5_inputs(z, core, res3, res4):
    bf = bf16_np()
    TALL = B * S
    oe_all = np.zeros((TALL, D), bf)
    for e in range(8):
        oe_all[1024*e:1024*(e+1)] = res4[e]["oe_out"]
    gids = 4096 * (core // 4) + core_token_ids(core)
    flat_own = res4[0]["flat_out"][0, gids]
    keep_own = res4[0]["keep_out"][0, gids]
    cc = core % 4
    own_cols = [32 * (core // 4) + 4 * m + cc for m in range(8)]
    gates_tm = np.ascontiguousarray(res4[0]["gates_out"][:, own_cols])
    return {
        "x3b_t": res3[core]["x3b_out"],
        "oe_all": oe_all,
        "flat_tm": np.ascontiguousarray(flat_own.reshape(8, 128).T),
        "gates_tm": gates_tm,
        "keep_tm": np.ascontiguousarray(keep_own.reshape(8, 128).T),
        "c_ident_b": np.eye(128).astype(bf),
    }


def kernel(**inputs):
    z = inputs
    nc1 = build_p1()
    res1 = run_phase(nc1, [p1_inputs(z, c) for c in range(NCORE)])
    nc2 = build_p2()
    res2 = run_phase(nc2, [p2_inputs(z, c, res1) for c in range(NCORE)])
    nc3 = build_p3()
    res3 = run_phase(nc3, [p3_inputs(z, c, res2) for c in range(NCORE)])
    nc4 = build_p4()
    res4 = run_phase(nc4, [p4_inputs(z, c, res3) for c in range(NCORE)])
    nc5 = build_p5()
    res5 = run_phase(nc5, [p5_inputs(z, c, res3, res4) for c in range(NCORE)])
    out = np.zeros((B, S, D), np.float32)
    for c in range(NCORE):
        out[c // 4][core_token_ids(c)] = res5[c]["x4_out"].T
    return out


def p1_inputs(z, core):
    return {
        "x_t": feat_major(host_slices(z["L0_attn_x"], core)).astype(np.float32),
        "gamma_t": vec_col(z["L0_attn_ln_gamma"]),
        "beta_t": vec_col(z["L0_attn_ln_beta"]),
        "w_in": np.ascontiguousarray(z["L0_attn_W_in"]),
        "b_in": np.ascontiguousarray(z["L0_attn_b_in"][:, None]),
        "c_ones128_f": np.ones((128, 1), np.float32),
        "c_ones1_f": np.ones((1, 128), np.float32),
    }


def feat_major(a):
    return np.ascontiguousarray(a.T)


def vec_col(v):
    return np.ascontiguousarray(v.reshape(NCH, 128).T)


def run_phase(nc, in_maps):
    res = run_bass_kernel_spmd(nc, in_maps, core_ids=list(range(NCORE)))
    return res.results


if __name__ == "__main__":
    pass


# version 03
# speedup vs baseline: 1.2675x; 1.2675x over previous
import contextlib
import numpy as np

import concourse.bass as bass
import concourse.mybir as mybir
from concourse.bass_utils import run_bass_kernel_spmd

F32 = mybir.dt.float32
BF16 = mybir.dt.bfloat16
U32 = mybir.dt.uint32
I32 = mybir.dt.int32
AF = mybir.ActivationFunctionType
ALU = mybir.AluOpType
AX = mybir.AxisListType

D = 2048
NCH = 16
B = 2
S = 4096
NTOK = 1024
NCORE = 8
HALF = 512
EPS_LN = 1e-5
EPS_RMS = 1e-6
NH = 16
KVH = 4
HD = 128
HG = 12
RK = 32
SYM = 4
E = 8
CAP = 1024
HMOE = 2816
BIG = 1.0e7

ENGINES = ("sync", "vector", "scalar", "tensor", "gpsimd")


class Prog:

    def __init__(self, nc, es):
        self.nc = nc
        self.es = es
        self.steps = {e: [] for e in ENGINES}
        self.sem = {e: es.enter_context(nc.semaphore(f"ms_{e}")) for e in ENGINES}
        self.cnt = {e: 0 for e in ENGINES}

    def step(self, eng, waits=None, incs=0):
        w = dict(waits or {})
        def deco(fn):
            self.steps[eng].append((fn, w))
            self.cnt[eng] += incs
            return None
        return deco

    def now(self, eng):
        return self.cnt[eng]

    def run(self):
        block = self.es.enter_context(self.nc.Block())
        for eng in ENGINES:
            steps = self.steps[eng]
            sems = self.sem

            def make(eng, steps):
                def body(h):
                    mysem = sems[eng]
                    for fn, waits in steps:
                        for se, v in waits.items():
                            h.wait_ge(sems[se], v)
                        fn(h, mysem)
                return body
            getattr(block, eng)(make(eng, steps))


def sb(nc, es, name, shape, dtype):
    return es.enter_context(nc.sbuf_tensor(name, shape, dtype))


def ps(nc, es, name, shape, dtype=F32):
    return es.enter_context(nc.psum_tensor(name, shape, dtype))



def emit_dma_batch(P, eng, pairs, waits=None):
    n = len(pairs)

    @P.step(eng, waits=waits, incs=16 * n)
    def _(h, sem):
        for out, in_ in pairs:
            h.dma_start(out=out, in_=in_).then_inc(sem, 16)
    return P.now(eng)


def emit_ln(P, T, x_sb, out_sb, consts, scratch, gamma_beta=None, out2_sb=None,
            eps=EPS_LN):
    nc = P.nc
    sq2 = scratch["sq2"]
    p_sum, p_sq = scratch["p_sum"], scratch["p_sq"]
    p_mb, p_rb = scratch["p_mb"], scratch["p_rb"]
    r_mean, r_e2, r_m2, r_var, r_sd, r_rstd = (scratch[k] for k in
        ("r_mean", "r_e2", "r_m2", "r_var", "r_sd", "r_rstd"))
    ones128_f = consts["ones128_f"]
    ones1_f = consts["ones1_f"]

    for half in range(2):
        cols = slice(half * HALF, (half + 1) * HALF)
        for k in range(NCH):
            kb = k % 2
            if k == 0:
                swaits = {"vector": P.now("vector")}
            elif k >= 2:
                swaits = {"tensor": P.now("tensor")}
            else:
                swaits = {}

            @P.step("scalar", waits=swaits, incs=1)
            def _(h, sem, k=k, kb=kb, cols=cols):
                h.activation(sq2[kb][:], x_sb[:, k, cols], AF.Square,
                             bias=0.0, scale=1.0).then_inc(sem, 1)

            @P.step("tensor", waits={"scalar": P.now("scalar")}, incs=1)
            def _(h, sem, k=k, kb=kb, cols=cols):
                h.matmul(p_sum[:], ones128_f[:], x_sb[:, k, cols],
                         start=(k == 0), stop=(k == NCH - 1))
                h.matmul(p_sq[:], ones128_f[:], sq2[kb][:],
                         start=(k == 0), stop=(k == NCH - 1)).then_inc(sem, 1)
        @P.step("vector", waits={"tensor": P.now("tensor")}, incs=1)
        def _(h, sem, cols=cols):
            h.tensor_scalar(r_mean[:], p_sum[:], 1.0 / D, None, ALU.mult)
            h.tensor_scalar(r_e2[:], p_sq[:], 1.0 / D, None, ALU.mult)
            h.drain()
            h.tensor_tensor(r_m2[:], r_mean[:], r_mean[:], ALU.mult)
            h.drain()
            h.tensor_tensor(r_var[:], r_e2[:], r_m2[:], ALU.subtract)
            h.drain()
            h.tensor_scalar(r_m2[:], r_var[:], eps, None, ALU.add)
            h.drain().then_inc(sem, 1)

        @P.step("scalar", waits={"vector": P.now("vector")}, incs=1)
        def _(h, sem):
            h.activation(r_sd[:], r_m2[:], AF.Sqrt, bias=0.0, scale=1.0).then_inc(sem, 1)

        @P.step("vector", waits={"scalar": P.now("scalar")}, incs=1)
        def _(h, sem):
            h.reciprocal(r_rstd[:], r_sd[:]).then_inc(sem, 1)

        @P.step("tensor", waits={"vector": P.now("vector")}, incs=1)
        def _(h, sem):
            h.matmul(p_mb[:], ones1_f[:], r_mean[:], start=True, stop=True)
            h.matmul(p_rb[:], ones1_f[:], r_rstd[:], start=True, stop=True).then_inc(sem, 1)

        tmb = P.now("tensor")
        for k in range(NCH):
            @P.step("vector", waits={"tensor": tmb}, incs=1)
            def _(h, sem, k=k, cols=cols):
                tmp = scratch["tmp"]
                h.tensor_tensor(tmp[:], x_sb[:, k, cols], p_mb[:], ALU.subtract)
                if gamma_beta is None:
                    h.tensor_tensor(out_sb[:, k, cols], tmp[:], p_rb[:],
                                    ALU.mult).then_inc(sem, 1)
                else:
                    g_col, b_col = gamma_beta
                    h.tensor_tensor(tmp[:], tmp[:], p_rb[:], ALU.mult)
                    h.tensor_scalar(out_sb[:, k, cols], tmp[:],
                                    g_col[:, k:k+1], b_col[:, k:k+1],
                                    ALU.mult, ALU.add).then_inc(sem, 1)
        vnow = P.now("vector")

        @P.step("tensor", waits={"vector": vnow}, incs=0)
        def _(h, sem):
            pass



def build_p1():
    nc = bass.Bass("TRN2", target_bir_lowering=False)
    es = contextlib.ExitStack()
    ap = {}
    def din(name, shape, dtype=F32):
        t = nc.dram_tensor(name, shape, dtype, kind="ExternalInput").ap()
        ap[name] = t
        return t
    def dout(name, shape, dtype=F32):
        t = nc.dram_tensor(name, shape, dtype, kind="ExternalOutput").ap()
        ap[name] = t
        return t

    x_t = din("x_t", [D, NTOK])
    gamma_t = din("gamma_t", [128, NCH])
    beta_t = din("beta_t", [128, NCH])
    w_in = din("w_in", [D, 8])
    b_in = din("b_in", [8, 1])
    c_ones128_f = din("c_ones128_f", [128, 1])
    c_ones1_f = din("c_ones1_f", [1, 128])

    u_out = dout("u_out", [8, NTOK])
    xln1_out = dout("xln1_out", [D, NTOK])
    xn_out = dout("xn_out", [D, NTOK], BF16)

    with es:
        xs = sb(nc, es, "xs", [128, NCH, NTOK], F32)
        xln1 = sb(nc, es, "xln1", [128, NCH, NTOK], F32)
        xnb = sb(nc, es, "xnb", [128, NCH, NTOK], BF16)
        g_col = sb(nc, es, "g_col", [128, NCH], F32)
        b_col = sb(nc, es, "b_col", [128, NCH], F32)
        winb = sb(nc, es, "winb", [128, NCH, 8], BF16)
        binv = sb(nc, es, "binv", [8, 1], F32)
        ones128_f = sb(nc, es, "ones128_f", [128, 1], F32)
        ones1_f = sb(nc, es, "ones1_f", [1, 128], F32)
        u_sb = sb(nc, es, "u_sb", [8, NTOK], F32)
        scratch = dict(
            sq2=[sb(nc, es, f"sq{i}", [128, HALF], F32) for i in range(2)],
            tmp=sb(nc, es, "tmp", [128, HALF], F32),
            p_sum=ps(nc, es, "p_sum", [1, HALF]),
            p_sq=ps(nc, es, "p_sq", [1, HALF]),
            p_mb=ps(nc, es, "p_mb", [128, HALF]),
            p_rb=ps(nc, es, "p_rb", [128, HALF]),
        )
        for nm in ("r_mean", "r_e2", "r_m2", "r_var", "r_sd", "r_rstd"):
            scratch[nm] = sb(nc, es, nm, [1, HALF], F32)
        p_u = ps(nc, es, "p_u", [8, HALF])

        P = Prog(nc, es)
        consts = dict(ones128_f=ones128_f, ones1_f=ones1_f)

        ld = emit_dma_batch(P, "sync", [
            (xs[:], x_t[:, :].rearrange("(k p) t -> p k t", p=128)),
            (g_col[:], gamma_t[:, :]),
            (b_col[:], beta_t[:, :]),
            (binv[:], b_in[:, :]),
            (ones128_f[:], c_ones128_f[:, :]),
            (ones1_f[:], c_ones1_f[:, :]),
        ])
        emit_dma_batch(P, "gpsimd", [
            (winb[:], w_in[:, :].rearrange("(k p) r -> p k r", p=128)),
        ])
        gw = P.now("gpsimd")

        for e in ("scalar", "tensor", "vector"):
            @P.step(e, waits={"sync": ld, "gpsimd": gw}, incs=0)
            def _(h, sem):
                pass

        emit_ln(P, None, xs, xln1, consts, scratch, gamma_beta=None)
        emit_ln(P, None, xln1, xnb, consts, scratch, gamma_beta=(g_col, b_col))

        for half in range(2):
            cols = slice(half * HALF, (half + 1) * HALF)
            for k in range(NCH):
                @P.step("tensor", waits={"vector": P.now("vector")}, incs=(1 if k == NCH - 1 else 0))
                def _(h, sem, k=k, cols=cols):
                    ins = h.matmul(p_u[:], winb[:, k, :], xnb[:, k, cols],
                                   start=(k == 0), stop=(k == NCH - 1))
                    if k == NCH - 1:
                        ins.then_inc(sem, 1)

            @P.step("vector", waits={"tensor": P.now("tensor")}, incs=1)
            def _(h, sem, cols=cols):
                h.tensor_scalar(u_sb[:, cols], p_u[:], binv[:], None,
                                ALU.add).then_inc(sem, 1)

        vs = P.now("vector")
        st = emit_dma_batch(P, "gpsimd", [
            (u_out[:, :], u_sb[:]),
            (xln1_out[:, :].rearrange("(k p) t -> p k t", p=128), xln1[:]),
            (xn_out[:, :].rearrange("(k p) t -> p k t", p=128), xnb[:]),
        ], waits={"vector": vs})

        @P.step("gpsimd", waits={"gpsimd": st}, incs=0)
        def _(h, sem):
            pass

        P.run()
    return nc



def emit_ln_dram(P, x_dram, out_sb, consts, scratch, ld2, gamma_beta=None,
                  eps=EPS_LN):
    nc = P.nc
    sq2 = scratch["sq2"]
    p_sum, p_sq = scratch["p_sum"], scratch["p_sq"]
    p_mb, p_rb = scratch["p_mb"], scratch["p_rb"]
    r_mean, r_m2, r_e2, r_var, r_sd, r_rstd = (scratch[k] for k in
        ("r_mean", "r_m2", "r_e2", "r_var", "r_sd", "r_rstd"))
    ones128_f = consts["ones128_f"]
    ones1_f = consts["ones1_f"]
    src = x_dram[:, :].rearrange("(k p) t -> p k t", p=128)
    stash_m, stash_r = scratch["stash_m"], scratch["stash_r"]
    lnlag = {}

    for half in range(2):
        cols = slice(half * HALF, (half + 1) * HALF)
        for k in range(NCH):
            kb = k % 2
            i = half * NCH + k
            wconsume = {"tensor": lnlag.get(kb, 0)} if i >= 2 else {"vector": P.now("vector")}
            @P.step("sync", waits=wconsume, incs=16)
            def _(h, sem, kb=kb, k=k, cols=cols):
                h.dma_start(out=ld2[kb][:], in_=src[:, k, cols]).then_inc(sem, 16)
            lv = P.now("sync")

            @P.step("scalar", waits={"sync": lv, "tensor": lnlag.get(kb, 0)}, incs=1)
            def _(h, sem, kb=kb):
                h.activation(sq2[kb][:], ld2[kb][:], AF.Square,
                             bias=0.0, scale=1.0).then_inc(sem, 1)

            @P.step("tensor", waits={"scalar": P.now("scalar")}, incs=1)
            def _(h, sem, kb=kb, k=k):
                h.matmul(p_sum[:], ones128_f[:], ld2[kb][:],
                         start=(k == 0), stop=(k == NCH - 1))
                h.matmul(p_sq[:], ones128_f[:], sq2[kb][:],
                         start=(k == 0), stop=(k == NCH - 1)).then_inc(sem, 1)
            lnlag[kb] = P.now("tensor")

        @P.step("vector", waits={"tensor": P.now("tensor")}, incs=1)
        def _(h, sem, half=half):
            h.tensor_scalar(r_mean[:], p_sum[:], 1.0 / D, None, ALU.mult)
            h.tensor_scalar(r_e2[:], p_sq[:], 1.0 / D, None, ALU.mult)
            h.drain()
            h.tensor_tensor(r_m2[:], r_mean[:], r_mean[:], ALU.mult)
            h.drain()
            h.tensor_tensor(r_var[:], r_e2[:], r_m2[:], ALU.subtract)
            h.drain()
            h.tensor_scalar(r_m2[:], r_var[:], eps, None, ALU.add)
            h.drain().then_inc(sem, 1)

        @P.step("scalar", waits={"vector": P.now("vector")}, incs=1)
        def _(h, sem):
            h.activation(r_sd[:], r_m2[:], AF.Sqrt, bias=0.0, scale=1.0).then_inc(sem, 1)

        @P.step("vector", waits={"scalar": P.now("scalar")}, incs=1)
        def _(h, sem, half=half):
            h.reciprocal(r_rstd[:], r_sd[:])
            h.tensor_copy(stash_m[0:1, half*HALF:(half+1)*HALF], r_mean[:]).then_inc(sem, 1)

        @P.step("vector", waits={"vector": P.now("vector")}, incs=1)
        def _(h, sem, half=half):
            h.tensor_copy(stash_r[0:1, half*HALF:(half+1)*HALF], r_rstd[:]).then_inc(sem, 1)

    for half in range(2):
        cols = slice(half * HALF, (half + 1) * HALF)
        @P.step("tensor", waits={"vector": P.now("vector")}, incs=1)
        def _(h, sem, half=half):
            h.matmul(p_mb[:], ones1_f[:], stash_m[0:1, half*HALF:(half+1)*HALF],
                     start=True, stop=True)
            h.matmul(p_rb[:], ones1_f[:], stash_r[0:1, half*HALF:(half+1)*HALF],
                     start=True, stop=True).then_inc(sem, 1)
        tb = P.now("tensor")
        for k in range(NCH):
            kb = k % 2
            i = half * NCH + k
            wconsume = {"vector": P.now("vector")} if i >= 2 else {}
            wconsume["tensor"] = tb
            @P.step("sync", waits=wconsume, incs=16)
            def _(h, sem, kb=kb, k=k, cols=cols):
                h.dma_start(out=ld2[kb][:], in_=src[:, k, cols]).then_inc(sem, 16)
            lv = P.now("sync")

            @P.step("vector", waits={"sync": lv, "tensor": tb}, incs=1)
            def _(h, sem, kb=kb, k=k, cols=cols):
                tmp = scratch["tmp"]
                h.tensor_tensor(tmp[:], ld2[kb][:], p_mb[:], ALU.subtract)
                if gamma_beta is None:
                    h.tensor_tensor(out_sb[:, k, cols], tmp[:], p_rb[:],
                                    ALU.mult).then_inc(sem, 1)
                else:
                    g_col, b_col = gamma_beta
                    h.tensor_tensor(tmp[:], tmp[:], p_rb[:], ALU.mult)
                    h.tensor_scalar(out_sb[:, k, cols], tmp[:],
                                    g_col[:, k:k+1], b_col[:, k:k+1],
                                    ALU.mult, ALU.add).then_inc(sem, 1)


def build_p2():
    nc = bass.Bass("TRN2", target_bir_lowering=False)
    es = contextlib.ExitStack()
    def din(name, shape, dtype=F32):
        return nc.dram_tensor(name, shape, dtype, kind="ExternalInput").ap()
    def dout(name, shape, dtype=F32):
        return nc.dram_tensor(name, shape, dtype, kind="ExternalOutput").ap()

    x_t = din("x_t", [D, NTOK])
    xln1_in = din("xln1_in", [D, NTOK])
    xn_in = din("xn_in", [D, NTOK], BF16)
    u_all = din("u_all", [8, S])
    dlogit = din("dlogit", [8, 1])
    w_out = din("w_out", [8, D])
    b_out_t = din("b_out_t", [128, NCH])
    d_skip_t = din("d_skip_t", [128, NCH])
    gamma_t = din("gamma_t", [128, NCH])
    beta_t = din("beta_t", [128, NCH])
    w_down = din("w_down", [D, 32])
    b_down = din("b_down", [32, 1])
    w_up = din("w_up", [32, D])
    b_up_t = din("b_up_t", [128, NCH])
    a_t = din("a_t", [D, 512])
    bw_t = din("bw_t", [512, D])
    wg0_t = din("wg0_t", [D, 8])
    wq_t = din("wq_t", [D, D])
    wk_t = din("wk_t", [D, 512])
    wv_t = din("wv_t", [D, 512])
    wrq_t = din("wrq_t", [D, 384])
    wrk_t = din("wrk_t", [D, 384])
    qnw = din("qnw", [128, 1])
    knw = din("knw", [128, 1])
    cosq = din("cosq", [128, NTOK])
    sinq = din("sinq", [128, NTOK])
    c_ones128_f = din("c_ones128_f", [128, 1])
    c_ones1_f = din("c_ones1_f", [1, 128])
    c_ones128x8 = din("c_ones128x8", [128, 8])
    c_ident_b = din("c_ident_b", [128, 128], BF16)
    c_ind4 = din("c_ind4", [128, 512], BF16)
    c_msel = din("c_msel", [8, 512], BF16)
    c_hgsel = din("c_hgsel", [128, 4])

    x2_out = dout("x2_out", [D, NTOK])
    q_out = dout("q_out", [D, NTOK], BF16)
    k_out = dout("k_out", [512, NTOK], BF16)
    v_out = dout("v_out", [NTOK, 512], BF16)
    rq_out = dout("rq_out", [384, NTOK], BF16)
    rk_out = dout("rk_out", [384, NTOK], BF16)
    qn2_out = dout("qn2_out", [NH, NTOK])
    kn2_out = dout("kn2_out", [KVH, NTOK])
    rqn2_out = dout("rqn2_out", [12, NTOK])
    rkn2_out = dout("rkn2_out", [12, NTOK])
    xi_dram = nc.dram_tensor("xi_scr", [D, NTOK], F32, kind="Internal").ap()
    x1_dram = nc.dram_tensor("x1_scr", [D, NTOK], F32, kind="Internal").ap()

    with es:
        bufC = sb(nc, es, "bufC", [128, NCH, NTOK], BF16)
        wqs2 = [sb(nc, es, f"wqs{i}", [128, NCH, 128], BF16) for i in range(2)]
        wqs2f = [sb(nc, es, f"wqsf{i}", [128, NCH, 128], F32) for i in range(2)]
        woutb = sb(nc, es, "woutb", [8, NCH, 128], BF16)
        wdownb = sb(nc, es, "wdownb", [128, NCH, 32], BF16)
        wupb = sb(nc, es, "wupb", [32, NCH, 128], BF16)
        wg0b = sb(nc, es, "wg0b", [128, NCH, 8], BF16)
        bout_c = sb(nc, es, "bout_c", [128, NCH], F32)
        dskip_c = sb(nc, es, "dskip_c", [128, NCH], F32)
        g_col = sb(nc, es, "g_col", [128, NCH], F32)
        b_col = sb(nc, es, "b_col", [128, NCH], F32)
        bdown_v = sb(nc, es, "bdown_v", [32, 1], F32)
        bup_c = sb(nc, es, "bup_c", [128, NCH], F32)
        qnw_v = sb(nc, es, "qnw_v", [128, 1], F32)
        knw_v = sb(nc, es, "knw_v", [128, 1], F32)
        wqsc = sb(nc, es, "wqsc", [128, 1], F32)
        cos_sb = sb(nc, es, "cos_sb", [128, NTOK], F32)
        sin_sb = sb(nc, es, "sin_sb", [128, NTOK], F32)
        ones128_f = sb(nc, es, "ones128_f", [128, 1], F32)
        ones1_f = sb(nc, es, "ones1_f", [1, 128], F32)
        ones128x8 = sb(nc, es, "ones128x8", [128, 8], F32)
        ident_b = sb(nc, es, "ident_b", [128, 128], BF16)
        ind4 = sb(nc, es, "ind4", [128, 512], BF16)
        msel = sb(nc, es, "msel", [8, 512], BF16)
        hgsel = sb(nc, es, "hgsel", [128, 4], F32)
        u_seg = sb(nc, es, "u_seg", [8, NTOK], F32)
        dec_v = sb(nc, es, "dec_v", [8, 1], F32)
        dec_s = sb(nc, es, "dec_s", [8, 1], F32)
        decbc1 = sb(nc, es, "decbc1", [8, NTOK], F32)
        h_seg = sb(nc, es, "h_seg", [8, NTOK], F32)
        h_init = sb(nc, es, "h_init", [8, 1], F32)
        h_bfs = sb(nc, es, "h_bfs", [8, NTOK], BF16)
        h_tm = sb(nc, es, "h_tm", [128, 32, 8], BF16)
        sel_tm = sb(nc, es, "sel_tm", [128, 8], BF16)
        h_loc = sb(nc, es, "h_loc", [8, NTOK], BF16)
        ld2 = [sb(nc, es, f"ld2_{i}", [128, HALF], F32) for i in range(2)]
        ld2b = [sb(nc, es, f"ld2b_{i}", [128, HALF], F32) for i in range(2)]
        stg2 = [sb(nc, es, f"stg2_{i}", [128, HALF], F32) for i in range(2)]
        bstg2 = [sb(nc, es, f"bstg2_{i}", [128, HALF], BF16) for i in range(2)]
        hfb = sb(nc, es, "hfb", [32, NTOK], BF16)
        h0m_sb = sb(nc, es, "h0m", [128, 4, NTOK], BF16)
        expd = sb(nc, es, "expd", [128, 8], F32)
        mx8 = sb(nc, es, "mx8", [128, 1], F32)
        sh8 = sb(nc, es, "sh8", [128, 8], F32)
        se8 = sb(nc, es, "se8", [128, 1], F32)
        se8b = sb(nc, es, "se8b", [128, 1], F32)
        rc8 = sb(nc, es, "rc8", [128, 1], F32)
        g4 = sb(nc, es, "g4", [128, 1], F32)
        ohd = sb(nc, es, "ohd", [128, 8], F32)
        cs8 = sb(nc, es, "cs8", [128, 8], F32)
        ohf = sb(nc, es, "ohf", [128, 8], BF16)
        ohf2 = sb(nc, es, "ohf2", [128, 8], F32)
        ohTg = sb(nc, es, "ohTg", [8, NTOK], BF16)
        sq_a = sb(nc, es, "sq_a", [128, HALF], F32)
        rb_sb = sb(nc, es, "rb_sb", [128, HALF], F32)
        qa_t = sb(nc, es, "qa_t", [128, HALF], F32)
        qn_t = sb(nc, es, "qn_t", [128, HALF], F32)
        rot_t = sb(nc, es, "rot_t", [128, HALF], F32)
        t1_t = sb(nc, es, "t1_t", [128, HALF], F32)
        t2_t = sb(nc, es, "t2_t", [128, HALF], F32)
        qf_bf = sb(nc, es, "qf_bf", [128, HALF], BF16)
        r_a = sb(nc, es, "r_a", [1, HALF], F32)
        r_b = sb(nc, es, "r_b", [1, HALF], F32)
        n2row = sb(nc, es, "n2row", [1, HALF], F32)
        rn_sb = sb(nc, es, "rn_sb", [4, HALF], F32)
        pA = ps(nc, es, "pA", [128, HALF])
        pB = ps(nc, es, "pB", [128, HALF])
        pC = ps(nc, es, "pC", [128, HALF])
        pD = ps(nc, es, "pD", [128, HALF])
        pE = ps(nc, es, "pE", [1, HALF])
        pF = ps(nc, es, "pF", [4, HALF])
        pT = ps(nc, es, "pT", [128, 128], BF16)
        p_mm = [pA, pB]
        scratch = dict(
            sq2=[sb(nc, es, f"lnsq{i}", [128, HALF], F32) for i in range(2)],
            tmp=sb(nc, es, "lntmp", [128, HALF], F32),
            p_sum=pE, p_sq=pF[0:1, :], p_mb=pC, p_rb=pD,
            stash_m=sb(nc, es, "stash_m", [1, NTOK], F32),
            stash_r=sb(nc, es, "stash_r", [1, NTOK], F32),
        )
        for nm in ("r_mean", "r_e2", "r_m2", "r_var", "r_sd", "r_rstd"):
            scratch[nm] = sb(nc, es, nm, [1, HALF], F32)
        p_g = pC[:, 0:8]
        p_row = pE
        p_rn = pF
        p_hf = pC[0:32, :]
        p_msk = pC
        p_loc = pC[:, 0:8]

        P = Prog(nc, es)
        consts = dict(ones128_f=ones128_f, ones1_f=ones1_f)

        ld_f = emit_dma_batch(P, "sync", [
            (dec_v[:], dlogit[:, :]),
            (bout_c[:], b_out_t[:, :]),
            (dskip_c[:], d_skip_t[:, :]),
            (g_col[:], gamma_t[:, :]),
            (b_col[:], beta_t[:, :]),
            (bdown_v[:], b_down[:, :]),
            (bup_c[:], b_up_t[:, :]),
            (qnw_v[:], qnw[:, :]),
            (knw_v[:], knw[:, :]),
            (cos_sb[:], cosq[:, :]),
            (sin_sb[:], sinq[:, :]),
            (ones128_f[:], c_ones128_f[:, :]),
            (ones1_f[:], c_ones1_f[:, :]),
            (ones128x8[:], c_ones128x8[:, :]),
            (ident_b[:], c_ident_b[:, :]),
            (ind4[:], c_ind4[:, :]),
            (msel[:], c_msel[:, :]),
            (hgsel[:], c_hgsel[:, :]),
        ])
        ld_w1 = emit_dma_batch(P, "gpsimd", [
            (woutb[:], w_out[:, :].rearrange("r (k m) -> r k m", m=128)),
            (wdownb[:], w_down[:, :].rearrange("(k p) r -> p k r", p=128)),
            (wupb[:], w_up[:, :].rearrange("r (k m) -> r k m", m=128)),
            (wg0b[:], wg0_t[:, :].rearrange("(k p) r -> p k r", p=128)),
        ])
        ld_xn = emit_dma_batch(P, "gpsimd", [
            (bufC[:], xn_in[:, :].rearrange("(k p) t -> p k t", p=128)),
        ])

        for e in ("scalar", "tensor", "vector"):
            @P.step(e, waits={"sync": ld_f}, incs=0)
            def _(h, sem):
                pass

        @P.step("scalar", waits={}, incs=1)
        def _(h, sem):
            h.activation(dec_s[:], dec_v[:], AF.Sigmoid, bias=0.0, scale=1.0).then_inc(sem, 1)

        @P.step("vector", waits={"scalar": P.now("scalar")}, incs=1)
        def _(h, sem):
            h.memset(h_init[:], 0.0).then_inc(sem, 1)

        @P.step("vector", waits={"vector": P.now("vector")}, incs=0)
        def _(h, sem):
            pass
        for seg in range(4):
            lu = emit_dma_batch(P, "sync", [
                (u_seg[:], u_all[:, seg*NTOK:(seg+1)*NTOK]),
            ], waits={"vector": P.now("vector")})

            @P.step("vector", waits={"sync": lu}, incs=1)
            def _(h, sem, seg=seg):
                if seg == 0:
                    h.tensor_scalar(decbc1[:], u_seg[:], 0.0, dec_s[:], ALU.mult, ALU.add)
                h.tensor_tensor_scan(h_seg[:], decbc1[:], u_seg[:], h_init[:],
                                     ALU.mult, ALU.add)
                h.drain()
                h.tensor_copy(h_bfs[:], h_seg[:])
                h.tensor_copy(h_init[:], h_seg[:, NTOK-1:NTOK])
                h.drain().then_inc(sem, 1)

            v_h = P.now("vector")
            for g in range(8):
                @P.step("tensor", waits={"vector": v_h if g == 0 else P.now("vector")}, incs=1)
                def _(h, sem, g=g):
                    h.transpose(pT[:, 0:8], h_bfs[:, g*128:(g+1)*128], ident_b[0:8, 0:8]).then_inc(sem, 1)

                @P.step("vector", waits={"tensor": P.now("tensor")}, incs=1)
                def _(h, sem, g=g, seg=seg):
                    h.tensor_copy(h_tm[:, seg*8+g, :], pT[:, 0:8]).then_inc(sem, 1)

        for m in range(8):
            @P.step("tensor", waits={"vector": P.now("vector")}, incs=1)
            def _(h, sem, m=m):
                for j in range(4):
                    ins = h.matmul(p_loc[:], ind4[:, j*128:(j+1)*128], h_tm[:, 4*m+j, :],
                                   start=(j == 0), stop=(j == 3))
                ins.then_inc(sem, 1)

            @P.step("vector", waits={"tensor": P.now("tensor")}, incs=1)
            def _(h, sem):
                h.tensor_copy(sel_tm[:], p_loc[:]).then_inc(sem, 1)

            @P.step("tensor", waits={"vector": P.now("vector")}, incs=1)
            def _(h, sem):
                h.transpose(pT[0:8, :], sel_tm[:], ident_b[:]).then_inc(sem, 1)

            @P.step("vector", waits={"tensor": P.now("tensor")}, incs=1)
            def _(h, sem, m=m):
                h.tensor_copy(h_loc[:, m*128:(m+1)*128], pT[0:8, :]).then_inc(sem, 1)
        v_hloc = P.now("vector")

        @P.step("tensor", waits={"vector": v_hloc, "gpsimd": ld_w1}, incs=0)
        def _(h, sem):
            pass
        xln1_src = xln1_in[:, :].rearrange("(k p) t -> p k t", p=128)
        x_src = x_t[:, :].rearrange("(k p) t -> p k t", p=128)
        xi_dst = xi_dram[:, :].rearrange("(k p) t -> p k t", p=128)
        xi_src = xi_dst
        x1_dst = x1_dram[:, :].rearrange("(k p) t -> p k t", p=128)
        x1_src = x1_dst
        x2_dst = x2_out[:, :].rearrange("(k p) t -> p k t", p=128)
        lagV, lagS = {}, {}
        tmp2p = [scratch["tmp"], scratch["sq2"][0]]
        for half in range(2):
            cols = slice(half * HALF, (half + 1) * HALF)
            for oc in range(NCH):
                i = half * NCH + oc
                pb = i % 2
                @P.step("sync", waits={"vector": lagV.get(pb, 0)} if i >= 2 else {}, incs=16)
                def _(h, sem, pb=pb, oc=oc, cols=cols):
                    h.dma_start(out=ld2[pb][:], in_=xln1_src[:, oc, cols]).then_inc(sem, 16)
                mysync = P.now("sync")

                @P.step("tensor", waits={"vector": lagV.get(pb, 0)} if i >= 2 else {}, incs=1)
                def _(h, sem, pb=pb, oc=oc, cols=cols):
                    h.matmul(p_mm[pb][:], woutb[:, oc, :], h_loc[:, cols],
                             start=True, stop=True).then_inc(sem, 1)
                myt = P.now("tensor")

                @P.step("vector", waits={"tensor": myt, "sync": max(mysync, lagS.get(pb, 0)),
                                         "gpsimd": ld_xn}, incs=1)
                def _(h, sem, pb=pb, oc=oc, cols=cols):
                    tmp = tmp2p[pb]
                    h.scalar_tensor_tensor(tmp[:], bufC[:, oc, cols], dskip_c[:, oc:oc+1],
                                           p_mm[pb][:], ALU.mult, ALU.add)
                    h.scalar_tensor_tensor(stg2[pb][:], tmp[:], bout_c[:, oc:oc+1],
                                           ld2[pb][:], ALU.add, ALU.add).then_inc(sem, 1)
                lagV[pb] = P.now("vector")

                @P.step("sync", waits={"vector": lagV[pb]}, incs=16)
                def _(h, sem, pb=pb, oc=oc, cols=cols):
                    h.dma_start(out=xi_dst[:, oc, cols], in_=stg2[pb][:]).then_inc(sem, 16)
                lagS[pb] = P.now("sync")

        @P.step("sync", waits={"sync": P.now("sync")}, incs=0)
        def _(h, sem):
            pass

        emit_ln_dram(P, xi_dram, bufC, consts, scratch, ld2b, gamma_beta=(g_col, b_col))

        for half in range(2):
            cols = slice(half * HALF, (half + 1) * HALF)
            @P.step("tensor", waits={"vector": P.now("vector"),
                                     "scalar": P.now("scalar")}, incs=1)
            def _(h, sem, cols=cols):
                for k in range(NCH):
                    ins = h.matmul(p_hf[:], wdownb[:, k, :], bufC[:, k, cols],
                                   start=(k == 0), stop=(k == NCH - 1))
                ins.then_inc(sem, 1)

            @P.step("scalar", waits={"tensor": P.now("tensor")}, incs=1)
            def _(h, sem, cols=cols):
                h.activation(hfb[:, cols], p_hf[:], AF.Relu, bias=bdown_v[:],
                             scale=1.0).then_inc(sem, 1)
        s_hf = P.now("scalar")
        lagV, lagS = {}, {}
        for half in range(2):
            cols = slice(half * HALF, (half + 1) * HALF)
            for oc in range(NCH):
                i = half * NCH + oc
                pb = i % 2
                @P.step("sync", waits={"vector": lagV.get(pb, 0)} if i >= 2 else {}, incs=32)
                def _(h, sem, pb=pb, oc=oc, cols=cols):
                    h.dma_start(out=ld2[pb][:], in_=x_src[:, oc, cols]).then_inc(sem, 16)
                    h.dma_start(out=ld2b[pb][:], in_=xi_src[:, oc, cols]).then_inc(sem, 16)
                mysync = P.now("sync")

                @P.step("tensor", waits={"scalar": s_hf} if i < 2 else
                        {"vector": lagV.get(pb, 0)}, incs=1)
                def _(h, sem, pb=pb, oc=oc, cols=cols):
                    h.matmul(p_mm[pb][:], wupb[:, oc, :], hfb[:, cols],
                             start=True, stop=True).then_inc(sem, 1)
                myt = P.now("tensor")

                @P.step("vector", waits={"tensor": myt, "sync": max(mysync, lagS.get(pb, 0))},
                        incs=1)
                def _(h, sem, pb=pb, oc=oc, cols=cols):
                    tmp = tmp2p[pb]
                    h.scalar_tensor_tensor(tmp[:], ld2b[pb][:], bup_c[:, oc:oc+1],
                                           p_mm[pb][:], ALU.add, ALU.add)
                    h.tensor_tensor(stg2[pb][:], tmp[:], ld2[pb][:],
                                    ALU.add).then_inc(sem, 1)
                lagV[pb] = P.now("vector")

                @P.step("sync", waits={"vector": lagV[pb]}, incs=16)
                def _(h, sem, pb=pb, oc=oc, cols=cols):
                    h.dma_start(out=x1_dst[:, oc, cols], in_=stg2[pb][:]).then_inc(sem, 16)
                lagS[pb] = P.now("sync")

        @P.step("sync", waits={"sync": P.now("sync")}, incs=0)
        def _(h, sem):
            pass

        emit_ln_dram(P, x1_dram, bufC, consts, scratch, ld2b, gamma_beta=None)

        for tt in range(8):
            @P.step("tensor", waits={"vector": P.now("vector"),
                                     "scalar": P.now("scalar")}, incs=1)
            def _(h, sem, tt=tt):
                for k in range(NCH):
                    ins = h.matmul(p_g[:], bufC[:, k, tt*128:(tt+1)*128],
                                   wg0b[:, k, :], start=(k == 0), stop=(k == NCH - 1))
                ins.then_inc(sem, 1)

            @P.step("vector", waits={"tensor": P.now("tensor")}, incs=1)
            def _(h, sem):
                h.reduce_max(mx8[:], p_g[:], axis=AX.X).then_inc(sem, 1)

            @P.step("vector", waits={"vector": P.now("vector")}, incs=1)
            def _(h, sem):
                h.tensor_scalar(ohd[:], p_g[:], mx8[:], None, ALU.is_ge)
                h.tensor_scalar(sh8[:], p_g[:], mx8[:], None, ALU.subtract).then_inc(sem, 1)

            @P.step("scalar", waits={"vector": P.now("vector")}, incs=1)
            def _(h, sem):
                h.activation(expd[:], sh8[:], AF.Exp, bias=0.0,
                             scale=1.0).then_inc(sem, 1)

            @P.step("vector", waits={"scalar": P.now("scalar")}, incs=1)
            def _(h, sem):
                h.reduce_sum(se8b[:], expd[:], axis=AX.X).then_inc(sem, 1)

            @P.step("vector", waits={"vector": P.now("vector")}, incs=1)
            def _(h, sem):
                h.reciprocal(rc8[:], se8b[:])
                h.tensor_tensor_scan(cs8[:], ones128x8[:], ohd[:], 0.0, ALU.mult, ALU.add)
                h.drain().then_inc(sem, 1)

            @P.step("vector", waits={"vector": P.now("vector")}, incs=1)
            def _(h, sem):
                h.tensor_scalar(g4[:], rc8[:], 4.0, None, ALU.mult).then_inc(sem, 1)

            @P.step("vector", waits={"vector": P.now("vector")}, incs=1)
            def _(h, sem):
                h.scalar_tensor_tensor(ohf2[:], cs8[:], 1.0, ohd[:], ALU.is_equal, ALU.mult).then_inc(sem, 1)

            @P.step("vector", waits={"vector": P.now("vector")}, incs=1)
            def _(h, sem, tt=tt):
                h.tensor_scalar(ohf[:], ohf2[:], g4[:], None, ALU.mult).then_inc(sem, 1)

            @P.step("tensor", waits={"vector": P.now("vector")}, incs=1)
            def _(h, sem):
                h.transpose(pT[0:8, :], ohf[:], ident_b[:]).then_inc(sem, 1)

            @P.step("vector", waits={"tensor": P.now("tensor")}, incs=1)
            def _(h, sem, tt=tt):
                h.tensor_copy(ohTg[:, tt*128:(tt+1)*128], pT[0:8, :]).then_inc(sem, 1)

        lagV, lagT, lagC = {}, {}, {}
        for half in range(2):
            cols = slice(half * HALF, (half + 1) * HALF)
            for ec in range(4):
                i = half * 4 + ec
                pb = i % 2
                wb = i % 2
                @P.step("sync", waits={"vector": lagC.get(wb, P.now("vector"))}, incs=16)
                def _(h, sem, wb=wb, ec=ec):
                    h.dma_start(out=wqs2f[wb][:],
                                in_=a_t[:, :].rearrange("(k p) r -> p k r", p=128)[:, :, ec*128:(ec+1)*128]
                                ).then_inc(sem, 16)
                myld = P.now("sync")

                @P.step("vector", waits={"sync": myld,
                                         "tensor": lagT.get(wb, P.now("tensor"))}, incs=1)
                def _(h, sem, wb=wb):
                    h.tensor_copy(wqs2[wb][:], wqs2f[wb][:]).then_inc(sem, 1)
                lagC[wb] = P.now("vector")

                @P.step("tensor", waits={"vector": P.now("vector")}, incs=1)
                def _(h, sem, pb=pb, wb=wb, ec=ec, cols=cols):
                    for k in range(NCH):
                        h.matmul(p_mm[pb][:], wqs2[wb][:, k, :],
                                 bufC[:, k, cols], start=(k == 0), stop=(k == NCH - 1))
                    h.matmul(p_msk[:], msel[:, ec*128:(ec+1)*128], ohTg[:, cols],
                             start=True, stop=True).then_inc(sem, 1)
                lagT[wb] = P.now("tensor")

                @P.step("vector", waits={"tensor": lagT[wb]}, incs=1)
                def _(h, sem, pb=pb, ec=ec, cols=cols):
                    h.tensor_copy(tmp2p[pb][:], p_mm[pb][:])
                    h.tensor_tensor(h0m_sb[:, ec, cols], tmp2p[pb][:], p_msk[:],
                                    ALU.mult).then_inc(sem, 1)

        lagV, lagT, lagS, lagC = {}, {}, {}, {}
        for half in range(2):
            cols = slice(half * HALF, (half + 1) * HALF)
            for oc in range(NCH):
                i = half * NCH + oc
                pb = i % 2
                @P.step("sync", waits={"vector": lagC.get(8 + pb, P.now("vector"))}, incs=16)
                def _(h, sem, pb=pb, oc=oc):
                    h.dma_start(out=wqs2f[pb][:, 0:4, :],
                                in_=bw_t[:, :].rearrange("(c p) o -> p c o", p=128)[:, :, oc*128:(oc+1)*128]
                                ).then_inc(sem, 16)
                myld0 = P.now("sync")

                @P.step("vector", waits={"sync": myld0,
                                         "tensor": lagT.get(pb, P.now("tensor"))}, incs=1)
                def _(h, sem, pb=pb):
                    h.tensor_copy(wqs2[pb][:, 0:4, :], wqs2f[pb][:, 0:4, :]).then_inc(sem, 1)
                lagC[8 + pb] = P.now("vector")

                @P.step("sync", waits={"vector": lagV.get(pb, 0)} if i >= 2 else {}, incs=16)
                def _(h, sem, pb=pb, oc=oc, cols=cols):
                    h.dma_start(out=ld2[pb][:], in_=x1_src[:, oc, cols]).then_inc(sem, 16)
                mysync = P.now("sync")

                @P.step("tensor", waits={"vector": P.now("vector")}, incs=1)
                def _(h, sem, pb=pb, oc=oc, cols=cols):
                    for ec in range(4):
                        ins = h.matmul(p_mm[pb][:], wqs2[pb][:, ec, :],
                                       h0m_sb[:, ec, cols], start=(ec == 0), stop=(ec == 3))
                    ins.then_inc(sem, 1)
                lagT[pb] = P.now("tensor")

                @P.step("vector", waits={"tensor": lagT[pb],
                                         "sync": max(mysync, lagS.get(pb, 0))}, incs=1)
                def _(h, sem, pb=pb, oc=oc, cols=cols):
                    h.tensor_tensor(stg2[pb][:], p_mm[pb][:], ld2[pb][:],
                                    ALU.add).then_inc(sem, 1)
                lagV[pb] = P.now("vector")

                @P.step("sync", waits={"vector": lagV[pb]}, incs=16)
                def _(h, sem, pb=pb, oc=oc, cols=cols):
                    h.dma_start(out=x2_dst[:, oc, cols], in_=stg2[pb][:]).then_inc(sem, 16)
                lagS[pb] = P.now("sync")

        @P.step("sync", waits={"sync": P.now("sync")}, incs=0)
        def _(h, sem):
            pass

        emit_ln_dram(P, x2_out, bufC, consts, scratch, ld2b, gamma_beta=None)

        @P.step("vector", waits={}, incs=1)
        def _(h, sem):
            h.tensor_scalar(wqsc[:], qnw_v[:], 1.0 / np.sqrt(128.0), None,
                            ALU.mult).then_inc(sem, 1)

        def qk_pipeline(n_mtiles, wsrc_dram, w_vec, out_dram, n2_out_dram):
            qk_lag = {}
            qk_lagc = {}
            wsrc = wsrc_dram[:, :].rearrange("(k p) o -> p k o", p=128)
            odst = out_dram[:, :].rearrange("(mm p) t -> p mm t", p=128)
            for m in range(n_mtiles):
                wb = m % 2
                @P.step("sync", waits={"vector": qk_lagc.get(wb, P.now("vector"))}, incs=16)
                def _(h, sem, m=m, wb=wb):
                    h.dma_start(out=wqs2f[wb][:], in_=wsrc[:, :, m*128:(m+1)*128]
                                ).then_inc(sem, 16)
                myld = P.now("sync")

                @P.step("vector", waits={"sync": myld,
                                         "tensor": qk_lag.get(wb, P.now("tensor"))}, incs=1)
                def _(h, sem, wb=wb):
                    h.tensor_copy(wqs2[wb][:], wqs2f[wb][:]).then_inc(sem, 1)
                qk_lagc[wb] = P.now("vector")
                g_w = P.now("vector")
                for half in range(2):
                    cols = slice(half * HALF, (half + 1) * HALF)
                    i = m * 2 + half
                    pb = i % 2
                    @P.step("tensor", waits={"vector": g_w}, incs=1)
                    def _(h, sem, pb=pb, wb=wb, cols=cols):
                        for k in range(NCH):
                            ins = h.matmul(p_mm[pb][:], wqs2[wb][:, k, :], bufC[:, k, cols],
                                           start=(k == 0), stop=(k == NCH - 1))
                        ins.then_inc(sem, 1)
                    t_mm = P.now("tensor")
                    qk_lag[wb] = t_mm

                    @P.step("scalar", waits={"tensor": t_mm}, incs=1)
                    def _(h, sem, pb=pb):
                        h.activation(sq_a[:], p_mm[pb][:], AF.Square, bias=0.0,
                                     scale=1.0).then_inc(sem, 1)

                    @P.step("tensor", waits={"scalar": P.now("scalar"),
                                             "vector": P.now("vector")}, incs=1)
                    def _(h, sem):
                        h.matmul(p_row[:], ones128_f[:], sq_a[:], start=True,
                                 stop=True).then_inc(sem, 1)

                    @P.step("vector", waits={"tensor": P.now("tensor")}, incs=1)
                    def _(h, sem):
                        h.tensor_scalar(r_a[:], p_row[:], 1.0 / 128.0, EPS_RMS,
                                        ALU.mult, ALU.add).then_inc(sem, 1)

                    @P.step("scalar", waits={"vector": P.now("vector")}, incs=1)
                    def _(h, sem):
                        h.activation(r_a[:], r_a[:], AF.Sqrt, bias=0.0,
                                     scale=1.0).then_inc(sem, 1)

                    @P.step("vector", waits={"scalar": P.now("scalar")}, incs=1)
                    def _(h, sem):
                        h.reciprocal(r_b[:], r_a[:])
                        h.drain().then_inc(sem, 1)

                    @P.step("tensor", waits={"vector": P.now("vector")}, incs=1)
                    def _(h, sem):
                        h.matmul(pC[:], ones1_f[:], r_b[:], start=True,
                                 stop=True).then_inc(sem, 1)

                    @P.step("vector", waits={"tensor": P.now("tensor")}, incs=1)
                    def _(h, sem, pb=pb, cols=cols):
                        h.tensor_copy(rb_sb[:], pC[:])
                        h.tensor_scalar(qa_t[:], p_mm[pb][:], w_vec[:], None, ALU.mult)
                        h.tensor_tensor(qn_t[:], qa_t[:], rb_sb[:], ALU.mult)
                        h.tensor_scalar(rot_t[0:64, :], qn_t[64:128, :], -1.0, None, ALU.mult)
                        h.tensor_copy(rot_t[64:128, :], qn_t[0:64, :])
                        h.tensor_tensor(t1_t[:], qn_t[:], cos_sb[:, cols], ALU.mult).then_inc(sem, 1)

                    @P.step("gpsimd", waits={"vector": P.now("vector")}, incs=1)
                    def _(h, sem, cols=cols):
                        h.tensor_tensor(t2_t[:], rot_t[:], sin_sb[:, cols],
                                        ALU.mult).then_inc(sem, 1)

                    @P.step("vector", waits={"gpsimd": P.now("gpsimd"),
                                             "sync": P.now("sync")}, incs=1)
                    def _(h, sem):
                        h.tensor_tensor(qf_bf[:], t1_t[:], t2_t[:], ALU.add).then_inc(sem, 1)

                    @P.step("scalar", waits={"vector": P.now("vector")}, incs=1)
                    def _(h, sem):
                        h.activation(sq_a[:], qf_bf[:], AF.Square, bias=0.0,
                                     scale=1.0).then_inc(sem, 1)

                    @P.step("tensor", waits={"scalar": P.now("scalar"),
                                             "vector": P.now("vector")}, incs=1)
                    def _(h, sem):
                        h.matmul(p_row[:], ones128_f[:], sq_a[:], start=True,
                                 stop=True).then_inc(sem, 1)

                    @P.step("vector", waits={"tensor": P.now("tensor")}, incs=1)
                    def _(h, sem):
                        h.tensor_copy(n2row[:], p_row[:]).then_inc(sem, 1)

                    @P.step("sync", waits={"vector": P.now("vector")}, incs=32)
                    def _(h, sem, m=m, cols=cols):
                        h.dma_start(out=odst[:, m, cols], in_=qf_bf[:]).then_inc(sem, 16)
                        h.dma_start(out=n2_out_dram[m:m+1, cols], in_=n2row[:]).then_inc(sem, 16)

        qk_pipeline(NH, wq_t, wqsc, q_out, qn2_out)
        qk_pipeline(KVH, wk_t, knw_v, k_out, kn2_out)

        wvsrc = wv_t[:, :].rearrange("(k p) o -> p k o", p=128)
        v_lagc, v_lagt = {}, {}
        for m in range(4):
            wb = m % 2
            @P.step("sync", waits={"vector": v_lagc.get(wb, P.now("vector"))}, incs=16)
            def _(h, sem, m=m, wb=wb):
                h.dma_start(out=wqs2f[wb][:], in_=wvsrc[:, :, m*128:(m+1)*128]).then_inc(sem, 16)
            myld = P.now("sync")

            @P.step("vector", waits={"sync": myld,
                                     "tensor": v_lagt.get(wb, P.now("tensor"))}, incs=1)
            def _(h, sem, wb=wb):
                h.tensor_copy(wqs2[wb][:], wqs2f[wb][:]).then_inc(sem, 1)
            v_lagc[wb] = P.now("vector")
            g_w = P.now("vector")
            for half in range(2):
                cols = slice(half * HALF, (half + 1) * HALF)
                i = m * 2 + half
                pb = i % 2
                @P.step("tensor", waits={"vector": g_w}, incs=1)
                def _(h, sem, pb=pb, wb=wb, cols=cols):
                    for k in range(NCH):
                        ins = h.matmul(p_mm[pb][:], wqs2[wb][:, k, :], bufC[:, k, cols],
                                       start=(k == 0), stop=(k == NCH - 1))
                    ins.then_inc(sem, 1)
                v_lagt[wb] = P.now("tensor")

                @P.step("vector", waits={"tensor": P.now("tensor")}, incs=1)
                def _(h, sem, pb=pb):
                    h.tensor_copy(bstg2[pb][:], p_mm[pb][:]).then_inc(sem, 1)

                for s4 in range(4):
                    @P.step("tensor", waits={"vector": P.now("vector")}, incs=1)
                    def _(h, sem, pb=pb, s4=s4):
                        h.transpose(pT[:], bstg2[pb][:, s4*128:(s4+1)*128],
                                    ident_b[:]).then_inc(sem, 1)

                    @P.step("vector", waits={"tensor": P.now("tensor"),
                                             "sync": P.now("sync")}, incs=1)
                    def _(h, sem, s4=s4):
                        h.tensor_copy(qf_bf[:, 0:128], pT[:]).then_inc(sem, 1)

                    @P.step("sync", waits={"vector": P.now("vector")}, incs=16)
                    def _(h, sem, m=m, half=half, s4=s4):
                        h.dma_start(out=v_out[half*512+s4*128:half*512+(s4+1)*128,
                                              m*128:(m+1)*128],
                                    in_=qf_bf[:, 0:128]).then_inc(sem, 16)

        def r_pipeline(wsrc_dram, out_dram, n2_out_dram):
            wsrc = wsrc_dram[:, :].rearrange("(k p) o -> p k o", p=128)
            r_lagc, r_lagt = {}, {}
            for rt in range(3):
                wb = rt % 2
                @P.step("sync", waits={"vector": r_lagc.get(wb, P.now("vector"))}, incs=16)
                def _(h, sem, rt=rt, wb=wb):
                    h.dma_start(out=wqs2f[wb][:], in_=wsrc[:, :, rt*128:(rt+1)*128]).then_inc(sem, 16)
                myld = P.now("sync")

                @P.step("vector", waits={"sync": myld,
                                         "tensor": r_lagt.get(wb, P.now("tensor"))}, incs=1)
                def _(h, sem, wb=wb):
                    h.tensor_copy(wqs2[wb][:], wqs2f[wb][:]).then_inc(sem, 1)
                r_lagc[wb] = P.now("vector")
                g_w = P.now("vector")
                for half in range(2):
                    cols = slice(half * HALF, (half + 1) * HALF)
                    i = rt * 2 + half
                    pb = i % 2
                    @P.step("tensor", waits={"vector": g_w}, incs=1)
                    def _(h, sem, pb=pb, wb=wb, cols=cols):
                        for k in range(NCH):
                            ins = h.matmul(p_mm[pb][:], wqs2[wb][:, k, :],
                                           bufC[:, k, cols], start=(k == 0), stop=(k == NCH - 1))
                        ins.then_inc(sem, 1)
                    r_lagt[wb] = P.now("tensor")

                    @P.step("scalar", waits={"tensor": P.now("tensor")}, incs=1)
                    def _(h, sem, pb=pb):
                        h.activation(sq_a[:], p_mm[pb][:], AF.Square, bias=0.0,
                                     scale=1.0).then_inc(sem, 1)

                    @P.step("tensor", waits={"scalar": P.now("scalar"),
                                             "vector": P.now("vector")}, incs=1)
                    def _(h, sem):
                        h.matmul(p_rn[:], hgsel[:], sq_a[:], start=True,
                                 stop=True).then_inc(sem, 1)

                    @P.step("vector", waits={"tensor": P.now("tensor"),
                                             "sync": P.now("sync")}, incs=1)
                    def _(h, sem, pb=pb):
                        h.tensor_copy(bstg2[pb][:], p_mm[pb][:])
                        h.tensor_copy(rn_sb[:], p_rn[:]).then_inc(sem, 1)

                    @P.step("sync", waits={"vector": P.now("vector")}, incs=32)
                    def _(h, sem, rt=rt, cols=cols, pb=pb):
                        h.dma_start(out=out_dram[rt*128:(rt+1)*128, cols],
                                    in_=bstg2[pb][:]).then_inc(sem, 16)
                        h.dma_start(out=n2_out_dram[rt*4:(rt+1)*4, cols],
                                    in_=rn_sb[:]).then_inc(sem, 16)

        r_pipeline(wrq_t, rq_out, rqn2_out)
        r_pipeline(wrk_t, rk_out, rkn2_out)

        @P.step("gpsimd", waits={"sync": P.now("sync"), "gpsimd": P.now("gpsimd"),
                                 "vector": P.now("vector")}, incs=0)
        def _(h, sem):
            pass

        P.run()
    return nc


def build_p3():
    nc = bass.Bass("TRN2", target_bir_lowering=False)
    es = contextlib.ExitStack()
    def din(name, shape, dtype=F32):
        return nc.dram_tensor(name, shape, dtype, kind="ExternalInput").ap()
    def dout(name, shape, dtype=F32):
        return nc.dram_tensor(name, shape, dtype, kind="ExternalOutput").ap()

    x2_t = din("x2_t", [D, NTOK])
    q_in = din("q_in", [D, NTOK], BF16)
    k_full = din("k_full", [512, S], BF16)
    v_full = din("v_full", [S, 512], BF16)
    rq_full = din("rq_full", [384, S], BF16)
    rk_full = din("rk_full", [384, S], BF16)
    rq_own = din("rq_own", [384, NTOK], BF16)
    rk_own = din("rk_own", [384, NTOK], BF16)
    qn2_own = din("qn2_own", [NH, NTOK])
    kn2_full = din("kn2_full", [KVH, S])
    rqn2_own = din("rqn2_own", [12, NTOK])
    rkn2_own = din("rkn2_own", [12, NTOK])
    rqn2_full = din("rqn2_full", [12, S])
    rkn2_full = din("rkn2_full", [12, S])
    gs12 = din("gs12", [12, 1])
    masks = din("masks", [8, 128, 4, 512], BF16)
    wo_t = din("wo_t", [D, D])
    wg1 = din("wg1", [D, 8])
    sw1 = din("sw1", [D, HMOE])
    sw3 = din("sw3", [D, HMOE])
    sw2 = din("sw2", [HMOE, D])
    c_ones128_f = din("c_ones128_f", [128, 1])
    c_ones1_f = din("c_ones1_f", [1, 128])
    c_ones128_b = din("c_ones128_b", [128, 1], BF16)
    c_ones1_b = din("c_ones1_b", [1, 2048], BF16)
    c_ident_b = din("c_ident_b", [128, 128], BF16)
    c_gsel = din("c_gsel", [12, 512])

    x3b_out = dout("x3b_out", [D, NTOK])
    xn4tm_out = dout("xn4tm_out", [NTOK, D], BF16)
    logits_out = dout("logits_out", [NTOK, 8])
    x3_dram = dout("x3_dbg", [D, NTOK])
    attn_dram = nc.dram_tensor("attn_scr", [D, NTOK], BF16, kind="Internal").ap()

    with es:
        bufC = sb(nc, es, "bufC", [128, NCH, NTOK], BF16)
        q_sb = bufC
        ones_row_b = sb(nc, es, "ones_row_b", [1, 128], BF16)
        ones128_f = sb(nc, es, "ones128_f", [128, 1], F32)
        ones1_f = sb(nc, es, "ones1_f", [1, 128], F32)
        ones128_b = sb(nc, es, "ones128_b", [128, 1], BF16)
        ident_b = sb(nc, es, "ident_b", [128, 128], BF16)
        logit_stg = sb(nc, es, "logit_stg", [128, 8], F32)
        ldb2 = [sb(nc, es, f"ldb2_{i}", [128, HALF], BF16) for i in range(2)]
        bstg2 = [sb(nc, es, f"bstg2_{i}", [128, HALF], BF16) for i in range(2)]
        pA = ps(nc, es, "pA", [128, HALF])
        pB = ps(nc, es, "pB", [128, HALF])
        pC = ps(nc, es, "pC", [128, HALF])
        pD = ps(nc, es, "pD", [128, HALF])
        pE = ps(nc, es, "pE", [1, HALF])
        pF = ps(nc, es, "pF", [4, HALF])
        pT = ps(nc, es, "pT", [128, 128], BF16)
        scratch = dict(
            sq2=[sb(nc, es, f"lnsq{i}", [128, HALF], F32) for i in range(2)],
            tmp=sb(nc, es, "lntmp", [128, HALF], F32),
            p_sum=pE, p_sq=pF[0:1, :], p_mb=pC, p_rb=pD,
            stash_m=sb(nc, es, "stash_m", [1, NTOK], F32),
            stash_r=sb(nc, es, "stash_r", [1, NTOK], F32),
        )
        for nm in ("r_mean", "r_e2", "r_m2", "r_var", "r_sd", "r_rstd"):
            scratch[nm] = sb(nc, es, nm, [1, HALF], F32)
        es2 = contextlib.ExitStack()
        rhs_sym = sb(nc, es2, "rhs_sym", [65, 4, NTOK], BF16)
        rhs_ns = sb(nc, es2, "rhs_ns", [33, 8, NTOK], BF16)
        negm_van = sb(nc, es2, "negm_van", [1, 4, NTOK], BF16)
        rmx = sb(nc, es2, "rmx", [1, 1], F32)
        rmx4 = sb(nc, es2, "rmx4", [1, 4], F32)
        kmax_f = sb(nc, es2, "kmax_f", [1, 4], F32)
        rqmax_f = sb(nc, es2, "rqmax_f", [1, 12], F32)
        rkmax_f = sb(nc, es2, "rkmax_f", [1, 12], F32)
        absgs = sb(nc, es2, "absgs", [1, 12], F32)
        gsraw = sb(nc, es2, "gsraw", [1, 12], F32)
        gsh = sb(nc, es2, "gsh", [1, 12], F32)
        gs12_sb = sb(nc, es2, "gs12_sb", [12, 1], F32)
        gsel_sb = sb(nc, es2, "gsel_sb", [12, 512], F32)
        gsvec = sb(nc, es2, "gsvec", [32, 16], F32)
        qn_row = sb(nc, es2, "qn_row", [1, NTOK], F32)
        rqn_row = sb(nc, es2, "rqn_row", [1, NTOK], F32)
        rkn_row = sb(nc, es2, "rkn_row", [1, NTOK], F32)
        qn_s = sb(nc, es2, "qn_s", [1, NTOK], F32)
        rqn_s = sb(nc, es2, "rqn_s", [1, NTOK], F32)
        rkn_s = sb(nc, es2, "rkn_s", [1, NTOK], F32)
        m_a = sb(nc, es2, "m_a", [1, NTOK], F32)
        m_b = sb(nc, es2, "m_b", [1, NTOK], F32)
        m_c = sb(nc, es2, "m_c", [1, NTOK], F32)
        m_d = sb(nc, es2, "m_d", [1, NTOK], F32)
        m_e = sb(nc, es2, "m_e", [1, NTOK], F32)
        m_b2 = sb(nc, es2, "m_b2", [1, NTOK], F32)
        m_d2 = sb(nc, es2, "m_d2", [1, NTOK], F32)
        neg_t = sb(nc, es2, "neg_t", [1, 12], F32)
        kmaxs = sb(nc, es2, "kmaxs", [1, 4], F32)
        rqmaxs = sb(nc, es2, "rqmaxs", [1, 12], F32)
        rkmaxs = sb(nc, es2, "rkmaxs", [1, 12], F32)
        rowf = qn_row
        rq_h = sb(nc, es2, "rq_h", [32, NTOK], BF16)
        kT2 = [sb(nc, es2, f"kT{i}", [128, HALF], BF16) for i in range(2)]
        vT2 = [sb(nc, es2, f"vT{i}", [128, 4, 128], BF16) for i in range(2)]
        symL2 = [sb(nc, es2, f"symL{i}", [65, 4, HALF], BF16) for i in range(2)]
        nsL2 = [sb(nc, es2, f"nsL{i}", [33, 4, HALF], BF16) for i in range(2)]
        mask_sb = sb(nc, es2, "mask_sb", [128, 4, HALF], BF16)
        pt2 = [sb(nc, es2, f"pt{i}", [128, HALF], BF16) for i in range(2)]
        rec_row = sb(nc, es2, "rec_row", [1, HALF], F32)
        sum_row = sb(nc, es2, "sum_row", [1, HALF], F32)
        bc_sb = sb(nc, es2, "bc_sb", [128, HALF], F32)
        p_S = [pA, pB]
        p_av = pD
        p_sums = pE
        p_bc = pC
        p_gv = pF[0:1, :]

        P = Prog(nc, es)
        consts = dict(ones128_f=ones128_f, ones1_f=ones1_f)

        ld_f = emit_dma_batch(P, "sync", [
            (gs12_sb[:], gs12[:, :]),
            (gsel_sb[:], c_gsel[:, :]),
            (ones128_f[:], c_ones128_f[:, :]),
            (ones1_f[:], c_ones1_f[:, :]),
            (ones128_b[:], c_ones128_b[:, :]),
            (ones_row_b[:], c_ones1_b[0:1, 0:128]),
            (ident_b[:], c_ident_b[:, :]),
            (nsL2[0][32:33, :, :], c_ones1_b[0:1, 0:2048].rearrange("o (a t) -> o a t", a=4)),
            (nsL2[1][32:33, :, :], c_ones1_b[0:1, 0:2048].rearrange("o (a t) -> o a t", a=4)),
            (symL2[0][64:65, :, :], c_ones1_b[0:1, 0:2048].rearrange("o (a t) -> o a t", a=4)),
            (symL2[1][64:65, :, :], c_ones1_b[0:1, 0:2048].rearrange("o (a t) -> o a t", a=4)),
        ])
        ld_q = emit_dma_batch(P, "gpsimd", [
            (q_sb[:], q_in[:, :].rearrange("(hh p) t -> p hh t", p=128)),
        ])
        for e in ("scalar", "tensor", "vector"):
            @P.step(e, waits={"sync": ld_f}, incs=0)
            def _(h, sem):
                pass

        def rowmax(src_ap, nrows, dst_tile):
            for r_ in range(nrows):
                for ch in range(4):
                    lr = emit_dma_batch(P, "sync",
                        [(rowf[:], src_ap[r_:r_+1, ch*NTOK:(ch+1)*NTOK])],
                        waits={"vector": P.now("vector")})
                    @P.step("vector", waits={"sync": lr}, incs=1)
                    def _(h, sem):
                        h.reduce_max(rmx[:], rowf[:], axis=AX.X).then_inc(sem, 1)

                    @P.step("vector", waits={"vector": P.now("vector")}, incs=1)
                    def _(h, sem, ch=ch):
                        h.tensor_copy(rmx4[0:1, ch:ch+1], rmx[:]).then_inc(sem, 1)

                @P.step("vector", waits={"vector": P.now("vector")}, incs=1)
                def _(h, sem):
                    h.reduce_max(rmx[:], rmx4[:], axis=AX.X).then_inc(sem, 1)

                @P.step("vector", waits={"vector": P.now("vector")}, incs=1)
                def _(h, sem, r_=r_):
                    h.tensor_copy(dst_tile[0:1, r_:r_+1], rmx[:]).then_inc(sem, 1)
        rowmax(kn2_full, KVH, kmax_f)
        rowmax(rqn2_full, 12, rqmax_f)
        rowmax(rkn2_full, 12, rkmax_f)

        @P.step("scalar", waits={"vector": P.now("vector")}, incs=1)
        def _(h, sem):
            h.activation(kmaxs[:], kmax_f[:], AF.Sqrt, bias=0.0, scale=1.0)
            h.activation(rqmaxs[:], rqmax_f[:], AF.Sqrt, bias=0.0, scale=1.0)
            h.activation(rkmaxs[:], rkmax_f[:], AF.Sqrt, bias=0.0, scale=1.0)
            h.drain().then_inc(sem, 1)

        lgs = emit_dma_batch(P, "sync", [(gsraw[:], gs12[:, :].rearrange("a o -> o a"))],
                             waits={})

        @P.step("vector", waits={"sync": lgs, "scalar": P.now("scalar")}, incs=1)
        def _(h, sem):
            h.tensor_scalar(neg_t[:], gsraw[:], -1.0, None, ALU.mult)
            h.drain()
            h.tensor_tensor(absgs[:], gsraw[:], neg_t[:], ALU.max)
            h.drain()
            h.tensor_scalar(gsh[:], absgs[:], 0.5, None, ALU.mult)
            h.drain().then_inc(sem, 1)

        for hh in range(12):
            @P.step("tensor", waits={"vector": P.now("vector")}, incs=1)
            def _(h, sem, hh=hh):
                h.matmul(pC[0:32, 0:1], gsel_sb[:, hh*32:(hh+1)*32],
                         gs12_sb[:], start=True, stop=True).then_inc(sem, 1)

            @P.step("vector", waits={"tensor": P.now("tensor")}, incs=1)
            def _(h, sem, hh=hh):
                h.tensor_copy(gsvec[:, hh:hh+1], pC[0:32, 0:1]).then_inc(sem, 1)

        for hh in range(NH):
            kvh = hh // 4
            pairs = [(qn_row[:], qn2_own[hh:hh+1, :])]
            if hh < 12:
                pairs.append((rqn_row[:], rqn2_own[hh:hh+1, :]))
                pairs.append((rq_h[:], rq_own[:, :].rearrange("(g r) t -> r g t", r=32)[:, hh, :]))
            if hh < 4:
                pairs.append((rkn_row[:], rkn2_own[hh:hh+1, :]))
            lm = emit_dma_batch(P, "sync", pairs, waits={"vector": P.now("vector")})

            @P.step("scalar", waits={"sync": lm, "vector": P.now("vector")}, incs=1)
            def _(h, sem, hh=hh):
                h.activation(qn_s[:], qn_row[:], AF.Sqrt, bias=0.0, scale=1.0)
                if hh < 12:
                    h.activation(rqn_s[:], rqn_row[:], AF.Sqrt, bias=0.0, scale=1.0)
                if hh < 4:
                    h.activation(rkn_s[:], rkn_row[:], AF.Sqrt, bias=0.0, scale=1.0)
                h.drain().then_inc(sem, 1)

            @P.step("vector", waits={"scalar": P.now("scalar")}, incs=1)
            def _(h, sem, hh=hh, kvh=kvh):
                h.tensor_scalar(m_a[:], qn_s[:], kmaxs[0:1, kvh:kvh+1], None, ALU.mult)
                mfin = m_a
                if hh < 12:
                    gfac = gsh if hh < 4 else absgs
                    h.tensor_scalar(m_b[:], rqn_s[:], rkmaxs[0:1, hh:hh+1],
                                    None, ALU.mult)
                    h.drain()
                    h.tensor_scalar(m_b2[:], m_b[:], gfac[0:1, hh:hh+1],
                                    None, ALU.mult)
                    h.drain()
                    h.tensor_tensor(m_c[:], m_a[:], m_b2[:], ALU.add)
                    mfin = m_c
                if hh < 4:
                    h.tensor_scalar(m_d[:], rkn_s[:], rqmaxs[0:1, hh:hh+1],
                                    None, ALU.mult)
                    h.drain()
                    h.tensor_scalar(m_d2[:], m_d[:], gsh[0:1, hh:hh+1],
                                    None, ALU.mult)
                    h.drain()
                    h.tensor_tensor(m_e[:], m_c[:], m_d2[:], ALU.add)
                    mfin = m_e
                h.drain()
                if hh < 4:
                    h.tensor_scalar(rhs_sym[64:65, hh, :], mfin[:], -1.0, None, ALU.mult)
                    ins = h.tensor_scalar(rhs_sym[0:32, hh, :], rq_h[:], gsvec[:, hh:hh+1],
                                          None, ALU.mult)
                elif hh < 12:
                    h.tensor_scalar(rhs_ns[32:33, hh - 4, :], mfin[:], -1.0, None, ALU.mult)
                    ins = h.tensor_scalar(rhs_ns[0:32, hh - 4, :], rq_h[:], gsvec[:, hh:hh+1],
                                          None, ALU.mult)
                else:
                    ins = h.tensor_scalar(negm_van[0:1, hh - 12, :], mfin[:], -1.0, None, ALU.mult)
                ins.then_inc(sem, 1)
        for hh in range(4):
            lm = emit_dma_batch(P, "sync", [
                (rq_h[:], rk_own[:, :].rearrange("(g r) t -> r g t", r=32)[:, hh, :]),
            ], waits={"vector": P.now("vector")})

            @P.step("vector", waits={"sync": lm}, incs=1)
            def _(h, sem, hh=hh):
                h.tensor_scalar(rhs_sym[32:64, hh, :], rq_h[:], gsvec[:, hh:hh+1],
                                None, ALU.mult).then_inc(sem, 1)
        v_rhs = P.now("vector")

        attn_dst = attn_dram[:, :].rearrange("(hh p) t -> p hh t", p=128)
        kfull_src = k_full
        vfull_src = v_full
        started = {}
        lag_t, lag_exp, lag_mask, lag_av, lag_avload = {}, {}, {}, {}, {}
        for qi in range(8):
            qcols = slice(qi * 128, (qi + 1) * 128)
            lmk = emit_dma_batch(P, "sync", [
                (mask_sb[:], masks[qi, :, :, :]),
            ], waits={"vector": P.now("vector"), "tensor": P.now("tensor")})
            for hg4 in range(4):
                kvh = hg4
                n_kcg = qi + 1
                first = [True]
                pend = None
                for kcg in range(n_kcg):
                    kb = kcg % 2
                    kcols = slice(kcg * 512, (kcg + 1) * 512)
                    pairs = [
                        (kT2[kb][:], kfull_src[kvh*128:(kvh+1)*128, kcols]),
                        (vT2[kb][:], vfull_src[:, :].rearrange("(cg c p) v -> p cg c v", p=128, c=4)[:, kcg, :, kvh*128:(kvh+1)*128]),
                    ]
                    if hg4 == 0:
                        pairs.append((symL2[kb][0:32, :, :],
                                      rk_full[:, :].rearrange("(g r) t -> r g t", r=32)[:, 0:4, kcols]))
                        pairs.append((symL2[kb][32:64, :, :],
                                      rq_full[:, :].rearrange("(g r) t -> r g t", r=32)[:, 0:4, kcols]))
                    elif hg4 in (1, 2):
                        pairs.append((nsL2[kb][0:32, :, :],
                                      rk_full[:, :].rearrange("(g r) t -> r g t", r=32)[:, 4*hg4:4*hg4+4, kcols]))
                    lk = emit_dma_batch(P, "gpsimd", pairs,
                                        waits={"tensor": max(lag_t.get(kb, 0),
                                                             lag_avload.get(kb, 0))})
                    for kc in range(4):
                        par = (kcg * 4 + kc) % 2
                        @P.step("tensor", waits={"gpsimd": lk,
                                                 "scalar": lag_exp.get(par, 0),
                                                 "sync": lmk,
                                                 "vector": v_rhs}, incs=1)
                        def _(h, sem, hg4=hg4, kb=kb, kc=kc, par=par, qcols=qcols):
                            h.matmul(p_S[par][:],
                                     kT2[kb][:, kc*128:(kc+1)*128],
                                     q_sb[:, 4*hg4:4*hg4+4, qcols],
                                     start=True, stop=False)
                            if hg4 == 0:
                                for hh in range(4):
                                    ins = h.matmul(p_S[par][:, hh*128:(hh+1)*128],
                                                   symL2[kb][:, hh, kc*128:(kc+1)*128],
                                                   rhs_sym[:, hh, qcols],
                                                   start=False, stop=(hh == 3))
                            elif hg4 in (1, 2):
                                for hh in range(4):
                                    ins = h.matmul(p_S[par][:, hh*128:(hh+1)*128],
                                                   nsL2[kb][:, hh, kc*128:(kc+1)*128],
                                                   rhs_ns[:, 4*(hg4-1)+hh, qcols],
                                                   start=False, stop=(hh == 3))
                            else:
                                ins = h.matmul(p_S[par][:], ones_row_b[:],
                                               negm_van[0:1, :, qcols],
                                               start=False, stop=True)
                            ins.then_inc(sem, 1)
                        t_after_S = P.now("tensor")
                        lag_t[kb] = t_after_S

                        if pend is not None:
                            pkb, pkc, ppar, pmasked = pend
                            key = (qi, hg4)
                            st_flag = key not in started
                            started[key] = True
                            wav = {"scalar": lag_exp[ppar]}
                            if pmasked:
                                wav["vector"] = lag_mask[ppar]
                            if st_flag:
                                wav["vector"] = P.now("vector")
                            @P.step("tensor", waits=wav, incs=1)
                            def _(h, sem, kb=pkb, kc=pkc, par=ppar, st=st_flag):
                                h.matmul(p_av[:], vT2[kb][:, kc, :], pt2[par][:],
                                         start=st, stop=False)
                                h.matmul(p_sums[:], ones128_b[:], pt2[par][:],
                                         start=st, stop=False).then_inc(sem, 1)
                            lag_av[ppar] = P.now("tensor")
                            lag_avload[pkb] = P.now("tensor")

                        ew = {"tensor": max(t_after_S, lag_av.get(par, 0))}
                        if lag_mask.get(par, 0):
                            ew["vector"] = lag_mask[par]
                        @P.step("scalar", waits=ew, incs=1)
                        def _(h, sem, par=par):
                            h.activation(pt2[par][:], p_S[par][:], AF.Exp, bias=0.0,
                                         scale=1.0).then_inc(sem, 1)
                        lag_exp[par] = P.now("scalar")

                        masked = (kcg == n_kcg - 1)
                        if masked:
                            @P.step("vector", waits={"scalar": lag_exp[par]}, incs=1)
                            def _(h, sem, par=par, kc=kc):
                                h.tensor_tensor(pt2[par][:], pt2[par][:],
                                                mask_sb[:, kc, :], ALU.mult).then_inc(sem, 1)
                            lag_mask[par] = P.now("vector")
                        pend = (kb, kc, par, masked)
                pkb, pkc, ppar, pmasked = pend
                key = (qi, hg4)
                st_flag = key not in started
                started[key] = True
                wav = {"scalar": lag_exp[ppar]}
                if pmasked:
                    wav["vector"] = lag_mask[ppar]
                if st_flag:
                    wav["vector"] = P.now("vector")
                @P.step("tensor", waits=wav, incs=1)
                def _(h, sem, kb=pkb, kc=pkc, par=ppar, st=st_flag):
                    h.matmul(p_av[:], vT2[kb][:, kc, :], pt2[par][:],
                             start=st, stop=True)
                    h.matmul(p_sums[:], ones128_b[:], pt2[par][:],
                             start=st, stop=True).then_inc(sem, 1)
                lag_av[ppar] = P.now("tensor")
                lag_avload[pkb] = P.now("tensor")
                del started[(qi, hg4)]
                @P.step("vector", waits={"tensor": P.now("tensor")}, incs=1)
                def _(h, sem):
                    h.tensor_scalar(sum_row[:], p_sums[:], 1e-37, None, ALU.max).then_inc(sem, 1)

                @P.step("vector", waits={"vector": P.now("vector")}, incs=1)
                def _(h, sem):
                    h.reciprocal(rec_row[:], sum_row[:]).then_inc(sem, 1)

                @P.step("tensor", waits={"vector": P.now("vector")}, incs=1)
                def _(h, sem):
                    h.matmul(p_bc[:], ones1_f[:], rec_row[:], start=True,
                             stop=True).then_inc(sem, 1)

                @P.step("vector", waits={"tensor": P.now("tensor"),
                                         "sync": P.now("sync")}, incs=1)
                def _(h, sem):
                    h.tensor_copy(bc_sb[:], p_bc[:]).then_inc(sem, 1)

                @P.step("vector", waits={"vector": P.now("vector"),
                                         "sync": P.now("sync")}, incs=1)
                def _(h, sem, hg4=hg4, qcols=qcols):
                    h.tensor_tensor(bstg2[hg4 % 2][:], p_av[:], bc_sb[:],
                                    ALU.mult).then_inc(sem, 1)

                @P.step("sync", waits={"vector": P.now("vector")}, incs=16)
                def _(h, sem, hg4=hg4, qi=qi):
                    h.dma_start(out=attn_dst[:, 4*hg4:4*hg4+4, qi*128:(qi+1)*128],
                                in_=bstg2[hg4 % 2][:].rearrange("p (a t) -> p a t", a=4)
                                ).then_inc(sem, 16)

        es2.close()
        hsb = sb(nc, es, "hsb", [128, 22, NTOK], BF16)
        wqs2 = [sb(nc, es, f"wqs{i}", [128, NCH, 128], BF16) for i in range(2)]
        ws3 = [sb(nc, es, f"ws3{i}", [128, NCH, 128], BF16) for i in range(2)]
        wqs2f = [sb(nc, es, f"wqsf{i}", [128, NCH, 128], F32) for i in range(2)]
        ws3f = [sb(nc, es, f"ws3f{i}", [128, NCH, 128], F32) for i in range(2)]
        wg1b = sb(nc, es, "wg1b", [128, NCH, 8], BF16)
        ld2 = [sb(nc, es, f"ld2_{i}", [128, HALF], F32) for i in range(2)]
        stg2 = [sb(nc, es, f"stg2_{i}", [128, HALF], F32) for i in range(2)]
        sil_t = sb(nc, es, "sil_t", [128, HALF], F32)
        sil_t2 = sb(nc, es, "sil_t2", [128, HALF], F32)
        @P.step("sync", waits={"sync": P.now("sync")}, incs=0)
        def _(h, sem):
            pass
        wo_src = wo_t[:, :].rearrange("(hh p) o -> p hh o", p=128)
        attn_src = attn_dram[:, :].rearrange("(hh p) t -> p hh t", p=128)
        x2_src = x2_t[:, :].rearrange("(k p) t -> p k t", p=128)
        x3_dst = x3_dram[:, :].rearrange("(k p) t -> p k t", p=128)
        ld_at = emit_dma_batch(P, "gpsimd", [
            (bufC[:], attn_src),
        ], waits={"sync": P.now("sync"), "tensor": P.now("tensor")})
        lagT, lagV, lagS, lagC = {}, {}, {}, {}
        for half in range(2):
            cols = slice(half * HALF, (half + 1) * HALF)
            for oc in range(NCH):
                i = half * NCH + oc
                pb = i % 2
                @P.step("sync", waits={"vector": lagC.get(pb, P.now("vector"))}, incs=16)
                def _(h, sem, pb=pb, oc=oc):
                    h.dma_start(out=wqs2f[pb][:], in_=wo_src[:, :, oc*128:(oc+1)*128]
                                ).then_inc(sem, 16)
                myld0 = P.now("sync")

                @P.step("vector", waits={"sync": myld0,
                                         "tensor": lagT.get(pb, P.now("tensor"))}, incs=1)
                def _(h, sem, pb=pb):
                    h.tensor_copy(wqs2[pb][:], wqs2f[pb][:]).then_inc(sem, 1)
                lagC[pb] = P.now("vector")

                @P.step("sync", waits={"vector": lagV.get(pb, 0)} if i >= 2 else {}, incs=16)
                def _(h, sem, pb=pb, oc=oc, cols=cols):
                    h.dma_start(out=ld2[pb][:], in_=x2_src[:, oc, cols]).then_inc(sem, 16)
                mysync = P.now("sync")

                @P.step("tensor", waits={"gpsimd": P.now("gpsimd"),
                                         "vector": P.now("vector")}, incs=1)
                def _(h, sem, pb=pb, oc=oc, cols=cols):
                    for k in range(NCH):
                        ins = h.matmul(p_S[pb][:], wqs2[pb][:, k, :], bufC[:, k, cols],
                                       start=(k == 0), stop=(k == NCH - 1))
                    ins.then_inc(sem, 1)
                lagT[pb] = P.now("tensor")

                @P.step("vector", waits={"tensor": lagT[pb], "sync": max(mysync, lagS.get(pb, 0))}, incs=1)
                def _(h, sem, pb=pb, cols=cols):
                    h.tensor_tensor(stg2[pb][:], p_S[pb][:], ld2[pb][:],
                                    ALU.add).then_inc(sem, 1)
                lagV[pb] = P.now("vector")

                @P.step("sync", waits={"vector": lagV[pb]}, incs=16)
                def _(h, sem, pb=pb, oc=oc, cols=cols):
                    h.dma_start(out=x3_dst[:, oc, cols], in_=stg2[pb][:]).then_inc(sem, 16)
                lagS[pb] = P.now("sync")

        @P.step("sync", waits={"sync": P.now("sync")}, incs=0)
        def _(h, sem):
            pass

        emit_ln_dram(P, x3_dram, bufC, consts, scratch, ld2, gamma_beta=None)

        for k in range(NCH):
            for tt in range(8):
                @P.step("tensor", waits={"vector": P.now("vector")}, incs=1)
                def _(h, sem, k=k, tt=tt):
                    h.transpose(pT[:], bufC[:, k, tt*128:(tt+1)*128], ident_b[:]).then_inc(sem, 1)

                @P.step("vector", waits={"tensor": P.now("tensor"),
                                         "sync": P.now("sync")}, incs=1)
                def _(h, sem, tt=tt):
                    h.tensor_copy(ldb2[tt % 2][:, 0:128], pT[:]).then_inc(sem, 1)

                @P.step("sync", waits={"vector": P.now("vector")}, incs=16)
                def _(h, sem, k=k, tt=tt):
                    h.dma_start(out=xn4tm_out[tt*128:(tt+1)*128, k*128:(k+1)*128],
                                in_=ldb2[tt % 2][:, 0:128]).then_inc(sem, 16)

        ld_wg = emit_dma_batch(P, "gpsimd", [
            (wg1b[:], wg1[:, :].rearrange("(k p) r -> p k r", p=128)),
        ], waits={})
        for tt in range(8):
            @P.step("tensor", waits={"gpsimd": ld_wg, "vector": P.now("vector")}, incs=1)
            def _(h, sem, tt=tt):
                for k in range(NCH):
                    ins = h.matmul(pC[:, 0:8], bufC[:, k, tt*128:(tt+1)*128],
                                   wg1b[:, k, :], start=(k == 0), stop=(k == NCH - 1))
                ins.then_inc(sem, 1)

            @P.step("vector", waits={"tensor": P.now("tensor"),
                                     "sync": P.now("sync")}, incs=1)
            def _(h, sem):
                h.tensor_copy(logit_stg[:], pC[:, 0:8]).then_inc(sem, 1)

            @P.step("sync", waits={"vector": P.now("vector")}, incs=16)
            def _(h, sem, tt=tt):
                h.dma_start(out=logits_out[tt*128:(tt+1)*128, :],
                            in_=logit_stg[:]).then_inc(sem, 16)

        sw1_src = sw1[:, :].rearrange("(k p) o -> p k o", p=128)
        sw3_src = sw3[:, :].rearrange("(k p) o -> p k o", p=128)
        sw2_src = sw2[:, :].rearrange("(k p) o -> p k o", p=128)
        lagT2, lagV2, lagSc2, lagC2 = {}, {}, {}, {}
        for hc in range(22):
            wb = hc % 2
            lw0 = emit_dma_batch(P, "sync", [
                (wqs2f[wb][:], sw1_src[:, :, hc*128:(hc+1)*128]),
                (ws3f[wb][:], sw3_src[:, :, hc*128:(hc+1)*128]),
            ], waits={"vector": lagC2.get(wb, P.now("vector"))})

            @P.step("vector", waits={"sync": lw0,
                                     "tensor": lagT2.get(wb, P.now("tensor"))}, incs=1)
            def _(h, sem, wb=wb):
                h.tensor_copy(wqs2[wb][:], wqs2f[wb][:])
                h.tensor_copy(ws3[wb][:], ws3f[wb][:]).then_inc(sem, 1)
            lagC2[wb] = P.now("vector")
            lw = lagC2[wb]
            for half in range(2):
                cols = slice(half * HALF, (half + 1) * HALF)
                i = hc * 2 + half
                pX = pA if i % 2 == 0 else pC
                pY = pB if i % 2 == 0 else pD
                sl = sil_t if i % 2 == 0 else sil_t2
                @P.step("tensor", waits={"vector": max(lw, lagV2.get(i % 2, P.now("vector"))),
                                         "scalar": lagSc2.get(i % 2, 0)}, incs=1)
                def _(h, sem, wb=wb, cols=cols, pX=pX, pY=pY):
                    for k in range(NCH):
                        h.matmul(pX[:], wqs2[wb][:, k, :], bufC[:, k, cols],
                                 start=(k == 0), stop=(k == NCH - 1))
                    for k in range(NCH):
                        ins = h.matmul(pY[:], ws3[wb][:, k, :], bufC[:, k, cols],
                                       start=(k == 0), stop=(k == NCH - 1))
                    ins.then_inc(sem, 1)

                lagT2[wb] = P.now("tensor")

                @P.step("scalar", waits={"tensor": lagT2[wb]}, incs=1)
                def _(h, sem, pX=pX, sl=sl):
                    h.activation(sl[:], pX[:], AF.Silu, bias=0.0,
                                 scale=1.0).then_inc(sem, 1)
                lagSc2[i % 2] = P.now("scalar")

                @P.step("vector", waits={"scalar": lagSc2[i % 2]}, incs=1)
                def _(h, sem, hc=hc, cols=cols, pY=pY, sl=sl):
                    h.tensor_tensor(hsb[:, hc, cols], sl[:], pY[:],
                                    ALU.mult).then_inc(sem, 1)
                lagV2[i % 2] = P.now("vector")

        lagT3, lagV3, lagS3, lagC3 = {}, {}, {}, {}
        for half in range(2):
            cols = slice(half * HALF, (half + 1) * HALF)
            for oc in range(NCH):
                i = half * NCH + oc
                pb = i % 2
                @P.step("sync", waits={"vector": lagC3.get(pb, P.now("vector"))}, incs=32)
                def _(h, sem, pb=pb, oc=oc):
                    h.dma_start(out=wqs2f[pb][:, 0:NCH, :],
                                in_=sw2_src[:, 0:NCH, oc*128:(oc+1)*128]).then_inc(sem, 16)
                    h.dma_start(out=ws3f[pb][:, 0:6, :],
                                in_=sw2_src[:, NCH:22, oc*128:(oc+1)*128]).then_inc(sem, 16)
                myld0 = P.now("sync")

                @P.step("vector", waits={"sync": myld0,
                                         "tensor": lagT3.get(pb, P.now("tensor"))}, incs=1)
                def _(h, sem, pb=pb):
                    h.tensor_copy(wqs2[pb][:, 0:NCH, :], wqs2f[pb][:, 0:NCH, :])
                    h.tensor_copy(ws3[pb][:, 0:6, :], ws3f[pb][:, 0:6, :]).then_inc(sem, 1)
                lagC3[pb] = P.now("vector")

                @P.step("sync", waits={"vector": lagV3.get(pb, 0)} if i >= 2 else {}, incs=16)
                def _(h, sem, pb=pb, oc=oc, cols=cols):
                    h.dma_start(out=ld2[pb][:], in_=x3_dst[:, oc, cols]).then_inc(sem, 16)
                mysync = P.now("sync")

                @P.step("tensor", waits={"vector": P.now("vector")}, incs=1)
                def _(h, sem, pb=pb, oc=oc, cols=cols):
                    for k in range(22):
                        wt = wqs2[pb][:, k, :] if k < NCH else ws3[pb][:, k - NCH, :]
                        ins = h.matmul(p_S[pb][:], wt, hsb[:, k, cols],
                                       start=(k == 0), stop=(k == 21))
                    ins.then_inc(sem, 1)
                lagT3[pb] = P.now("tensor")

                @P.step("vector", waits={"tensor": lagT3[pb],
                                         "sync": max(mysync, lagS3.get(pb, 0))}, incs=1)
                def _(h, sem, pb=pb, cols=cols):
                    h.tensor_tensor(stg2[pb][:], p_S[pb][:], ld2[pb][:],
                                    ALU.add).then_inc(sem, 1)
                lagV3[pb] = P.now("vector")

                @P.step("sync", waits={"vector": lagV3[pb]}, incs=16)
                def _(h, sem, pb=pb, oc=oc, cols=cols):
                    h.dma_start(out=x3b_out[:, :].rearrange("(k p) t -> p k t", p=128)[:, oc, cols],
                                in_=stg2[pb][:]).then_inc(sem, 16)
                lagS3[pb] = P.now("sync")

        @P.step("gpsimd", waits={"sync": P.now("sync"), "gpsimd": P.now("gpsimd"),
                                 "vector": P.now("vector")}, incs=0)
        def _(h, sem):
            pass

        P.run()
    return nc


def build_p4():
    nc = bass.Bass("TRN2", target_bir_lowering=False)
    es = contextlib.ExitStack()
    def din(name, shape, dtype=F32):
        return nc.dram_tensor(name, shape, dtype, kind="ExternalInput").ap()
    def dout(name, shape, dtype=F32):
        return nc.dram_tensor(name, shape, dtype, kind="ExternalOutput").ap()

    TALL = B * S
    xn4_all = din("xn4_all", [TALL, D], BF16)
    logits_all = din("logits_all", [TALL, 8])
    w1e = din("w1e", [D, HMOE])
    w3e = din("w3e", [D, HMOE])
    w2e = din("w2e", [HMOE, D])
    eoff = din("eoff", [1, 1])
    c_ones1_f = din("c_ones1_f", [1, 128])
    c_ones8_f = din("c_ones8_f", [8, 1])
    c_evec = din("c_evec", [8, 1])
    c_ones128x8 = din("c_ones128x8", [128, 8])
    c_ident_b = din("c_ident_b", [128, 128], BF16)
    c_tokid = din("c_tokid", [128, 256], U32)
    c_bigidx = din("c_bigidx", [CAP, 4], U32)
    c_zeros2048 = din("c_zeros2048", [128, D], BF16)

    oe_out = dout("oe_out", [CAP, D], BF16)
    flat_out = dout("flat_out", [1, TALL])
    gates_out = dout("gates_out", [128, 64])
    keep_out = dout("keep_out", [1, TALL])
    idxmap_dram = nc.dram_tensor("idxmap", [CAP, 4], U32, kind="Internal").ap()

    with es:
        ones1_f = sb(nc, es, "ones1_f", [1, 128], F32)
        ones8_f = sb(nc, es, "ones8_f", [8, 1], F32)
        evec = sb(nc, es, "evec", [8, 1], F32)
        ones128x8 = sb(nc, es, "ones128x8", [128, 8], F32)
        ident_b = sb(nc, es, "ident_b", [128, 128], BF16)
        tokidT = sb(nc, es, "tokidT", [128, 256], U32)
        zeros2048 = sb(nc, es, "zeros2048", [128, D], BF16)
        eoff_sb = sb(nc, es, "eoff_sb", [1, 1], F32)
        xd_fm = sb(nc, es, "xd_fm", [128, NCH, NTOK], BF16)
        bstg2 = [sb(nc, es, f"bstg2_{i}", [128, HALF], BF16) for i in range(2)]
        sil_t = sb(nc, es, "sil_t", [128, HALF], F32)
        sil_t2 = sb(nc, es, "sil_t2", [128, HALF], F32)
        pA = ps(nc, es, "pA", [128, HALF])
        pB = ps(nc, es, "pB", [128, HALF])
        pC = ps(nc, es, "pC", [128, HALF])
        pE = ps(nc, es, "pE", [1, HALF])
        pD4 = ps(nc, es, "pD4", [128, HALF])
        pT = ps(nc, es, "pT", [128, 128], BF16)

        P = Prog(nc, es)
        ld_f = emit_dma_batch(P, "sync", [
            (ones1_f[:], c_ones1_f[:, :]),
            (ones8_f[:], c_ones8_f[:, :]),
            (evec[:], c_evec[:, :]),
            (ones128x8[:], c_ones128x8[:, :]),
            (ident_b[:], c_ident_b[:, :]),
            (tokidT[:], c_tokid[:, :]),
            (zeros2048[:], c_zeros2048[:, :]),
            (eoff_sb[:], eoff[:, :]),
        ])
        regs = {}
        for e in ("scalar", "tensor", "vector"):
            @P.step(e, waits={"sync": ld_f}, incs=0)
            def _(h, sem):
                pass

        @P.step("gpsimd", waits={"sync": ld_f}, incs=0)
        def _(h, sem):
            regs["b_cap"] = h.to_reg(CAP - 1)
            regs["b_tall"] = h.to_reg(B * S - 1)

        es2 = contextlib.ExitStack()
        lg_t = sb(nc, es2, "lg_t", [128, 8], F32)
        mx8 = sb(nc, es2, "mx8", [128, 1], F32)
        sh8 = sb(nc, es2, "sh8", [128, 8], F32)
        expd = sb(nc, es2, "expd", [128, 8], F32)
        se8 = sb(nc, es2, "se8", [128, 1], F32)
        rc8 = sb(nc, es2, "rc8", [128, 1], F32)
        ohd = sb(nc, es2, "ohd", [128, 8], F32)
        cs8 = sb(nc, es2, "cs8", [128, 8], F32)
        ohf = sb(nc, es2, "ohf", [128, 8], BF16)
        g_all = sb(nc, es2, "g_all", [128, 64], F32)
        ohT_seg = sb(nc, es2, "ohT_seg", [8, NTOK], F32)
        cs_seg = sb(nc, es2, "cs_seg", [8, NTOK], F32)
        ones_seg = sb(nc, es2, "ones_seg", [8, NTOK], F32)
        tmp_seg = sb(nc, es2, "tmp_seg", [8, NTOK], F32)
        tmp2_seg = sb(nc, es2, "tmp2_seg", [8, NTOK], F32)
        kp_seg = sb(nc, es2, "kp_seg", [8, NTOK], F32)
        kpo_seg = sb(nc, es2, "kpo_seg", [8, NTOK], F32)
        cs_init = sb(nc, es2, "cs_init", [8, 1], F32)
        flat0_r = sb(nc, es2, "flat0_r", [1, NTOK], F32)
        ktok_r = sb(nc, es2, "ktok_r", [1, NTOK], F32)
        t1_r = sb(nc, es2, "t1_r", [1, NTOK], F32)
        fsend_r = sb(nc, es2, "fsend_r", [1, NTOK], F32)
        myidx_r = sb(nc, es2, "myidx_r", [1, NTOK], F32)
        negm_r = sb(nc, es2, "negm_r", [1, NTOK], F32)
        myidx2_r = sb(nc, es2, "myidx2_r", [1, NTOK], F32)
        myidxT = sb(nc, es2, "myidxT", [128, 64], U32)
        idx_sb = sb(nc, es2, "idx_sb", [128, 8, 4], U32)
        xd_tm = sb(nc, es2, "xd_tm", [128, 8, D], BF16)

        @P.step("vector", waits={}, incs=1)
        def _(h, sem):
            h.memset(cs_init[:], 0.0)
            h.drain().then_inc(sem, 1)

        for seg in range(8):
            for tt in range(8):
                gt = seg * 8 + tt
                ll = emit_dma_batch(P, "sync", [
                    (lg_t[:], logits_all[gt*128:(gt+1)*128, :]),
                ], waits={"scalar": P.now("scalar"), "vector": P.now("vector")})

                @P.step("vector", waits={"sync": ll}, incs=1)
                def _(h, sem):
                    h.reduce_max(mx8[:], lg_t[:], axis=AX.X).then_inc(sem, 1)

                @P.step("vector", waits={"vector": P.now("vector")}, incs=1)
                def _(h, sem):
                    h.tensor_scalar(ohd[:], lg_t[:], mx8[:], None, ALU.is_ge)
                    h.tensor_scalar(sh8[:], lg_t[:], mx8[:], None, ALU.subtract)
                    h.drain().then_inc(sem, 1)

                @P.step("scalar", waits={"vector": P.now("vector")}, incs=1)
                def _(h, sem):
                    h.activation(expd[:], sh8[:], AF.Exp, bias=0.0,
                                 scale=1.0).then_inc(sem, 1)

                @P.step("vector", waits={"scalar": P.now("scalar")}, incs=1)
                def _(h, sem):
                    h.reduce_sum(se8[:], expd[:], axis=AX.X).then_inc(sem, 1)

                @P.step("vector", waits={"vector": P.now("vector")}, incs=1)
                def _(h, sem, gt=gt):
                    h.reciprocal(rc8[:], se8[:])
                    h.tensor_tensor_scan(cs8[:], ones128x8[:], ohd[:], 0.0,
                                         ALU.mult, ALU.add)
                    h.drain().then_inc(sem, 1)

                @P.step("vector", waits={"vector": P.now("vector")}, incs=1)
                def _(h, sem, gt=gt):
                    h.tensor_copy(g_all[:, gt:gt+1], rc8[:])
                    h.scalar_tensor_tensor(ohf[:], cs8[:], 1.0, ohd[:],
                                           ALU.is_equal, ALU.mult)
                    h.drain().then_inc(sem, 1)

                @P.step("tensor", waits={"vector": P.now("vector")}, incs=1)
                def _(h, sem):
                    h.transpose(pT[0:8, :], ohf[:], ident_b[:]).then_inc(sem, 1)

                @P.step("vector", waits={"tensor": P.now("tensor")}, incs=1)
                def _(h, sem, tt=tt):
                    h.tensor_copy(ohT_seg[:, tt*128:(tt+1)*128], pT[0:8, :]).then_inc(sem, 1)

            @P.step("vector", waits={"vector": P.now("vector")}, incs=1)
            def _(h, sem, seg=seg):
                h.tensor_scalar(ones_seg[:], ohT_seg[:], 0.0, 1.0, ALU.mult, ALU.add)
                h.drain()
                h.tensor_tensor_scan(cs_seg[:], ones_seg[:], ohT_seg[:], cs_init[:],
                                     ALU.mult, ALU.add)
                h.drain()
                h.tensor_copy(cs_init[:], cs_seg[:, NTOK-1:NTOK])
                h.tensor_scalar(tmp_seg[:], cs_seg[:], evec[:], None, ALU.add)
                h.tensor_scalar(kp_seg[:], cs_seg[:], float(CAP) + 0.5, None, ALU.is_le)
                h.drain()
                h.tensor_tensor(tmp2_seg[:], tmp_seg[:], ohT_seg[:], ALU.mult)
                h.tensor_tensor(kpo_seg[:], kp_seg[:], ohT_seg[:], ALU.mult)
                h.drain().then_inc(sem, 1)

            for half in range(2):
                @P.step("tensor", waits={"vector": P.now("vector")}, incs=1)
                def _(h, sem, half=half):
                    h.matmul(pE[:], ones8_f[:], tmp2_seg[:, half*HALF:(half+1)*HALF],
                             start=True, stop=True).then_inc(sem, 1)

                @P.step("vector", waits={"tensor": P.now("tensor")}, incs=1)
                def _(h, sem, half=half):
                    h.tensor_copy(flat0_r[0:1, half*HALF:(half+1)*HALF], pE[:]).then_inc(sem, 1)

                @P.step("tensor", waits={"vector": P.now("vector")}, incs=1)
                def _(h, sem, half=half):
                    h.matmul(pE[:], ones8_f[:], kpo_seg[:, half*HALF:(half+1)*HALF],
                             start=True, stop=True).then_inc(sem, 1)

                @P.step("vector", waits={"tensor": P.now("tensor")}, incs=1)
                def _(h, sem, half=half):
                    h.tensor_copy(ktok_r[0:1, half*HALF:(half+1)*HALF], pE[:]).then_inc(sem, 1)

            @P.step("vector", waits={"vector": P.now("vector")}, incs=1)
            def _(h, sem):
                h.tensor_scalar(t1_r[:], ktok_r[:], -BIG, BIG, ALU.mult, ALU.add)
                h.drain()
                h.tensor_tensor(fsend_r[:], flat0_r[:], t1_r[:], ALU.add)
                h.drain()
                h.tensor_scalar(myidx_r[:], fsend_r[:], eoff_sb[:], None, ALU.subtract)
                h.drain()
                h.tensor_scalar(negm_r[:], myidx_r[:], 0.0, None, ALU.is_lt)
                h.drain()
                h.scalar_tensor_tensor(myidx2_r[:], negm_r[:], 1.0e6, myidx_r[:],
                                       ALU.mult, ALU.add)
                h.drain().then_inc(sem, 1)

            st_f = emit_dma_batch(P, "sync", [
                (flat_out[0:1, seg*NTOK:(seg+1)*NTOK], fsend_r[:]),
                (keep_out[0:1, seg*NTOK:(seg+1)*NTOK], ktok_r[:]),
            ], waits={"vector": P.now("vector")})

            for g in range(8):
                @P.step("tensor", waits={"vector": P.now("vector")}, incs=1)
                def _(h, sem, g=g):
                    h.transpose(pC[:, 0:1], myidx2_r[0:1, g*128:(g+1)*128],
                                ones1_f[0:1, 0:1]).then_inc(sem, 1)

                @P.step("vector", waits={"tensor": P.now("tensor")}, incs=1)
                def _(h, sem, g=g, seg=seg):
                    h.tensor_copy(myidxT[:, seg*8+g:seg*8+g+1], pC[:, 0:1]).then_inc(sem, 1)

        st_g = emit_dma_batch(P, "sync", [
            (gates_out[:, :], g_all[:]),
        ], waits={"vector": P.now("vector")})

        ini = emit_dma_batch(P, "gpsimd", [
            (idxmap_dram[:, :], c_bigidx[:, :]),
        ], waits={})

        @P.step("gpsimd", waits={"gpsimd": ini, "vector": P.now("vector")}, incs=16*64)
        def _(h, sem):
            for g in range(64):
                h.indirect_dma_start(
                    out=idxmap_dram[:, :], out_offset=bass.IndirectOffsetOnAxis(
                        ap=myidxT[:, g:g+1], axis=0),
                    in_=tokidT[:, 4*g:4*g+4], in_offset=None,
                    bounds_check=regs["b_cap"], oob_is_err=False).then_inc(sem, 16)

        ld_idx = emit_dma_batch(P, "gpsimd", [
            (idx_sb[:, g, :], idxmap_dram[g*128:(g+1)*128, :]) for g in range(8)
        ], waits={"gpsimd": P.now("gpsimd")})
        zr = emit_dma_batch(P, "gpsimd", [
            (xd_tm[:, g, :], zeros2048[:, :]) for g in range(8)
        ], waits={})

        @P.step("gpsimd", waits={"gpsimd": P.now("gpsimd")}, incs=16*8)
        def _(h, sem):
            for g in range(8):
                h.indirect_dma_start(
                    out=xd_tm[:, g, :], out_offset=None, in_=xn4_all[:, :],
                    in_offset=bass.IndirectOffsetOnAxis(ap=idx_sb[:, g, 0:1], axis=0),
                    bounds_check=regs["b_tall"], oob_is_err=False).then_inc(sem, 16)
        g_xd = P.now("gpsimd")

        for g in range(8):
            for k in range(NCH):
                @P.step("tensor", waits={"gpsimd": g_xd} if (g == 0 and k == 0) else
                        {"vector": P.now("vector")}, incs=1)
                def _(h, sem, g=g, k=k):
                    h.transpose(pT[:], xd_tm[:, g, k*128:(k+1)*128],
                                ident_b[:]).then_inc(sem, 1)

                @P.step("vector", waits={"tensor": P.now("tensor")}, incs=1)
                def _(h, sem, g=g, k=k):
                    h.tensor_copy(xd_fm[:, k, g*128:(g+1)*128], pT[:]).then_inc(sem, 1)

        es2.close()
        hb = sb(nc, es, "hb", [128, 22, NTOK], BF16)
        oe_fm = sb(nc, es, "oe_fm", [128, NCH, NTOK], BF16)
        wqs2 = [sb(nc, es, f"wqs{i}", [128, NCH, 128], BF16) for i in range(2)]
        ws3 = [sb(nc, es, f"ws3{i}", [128, NCH, 128], BF16) for i in range(2)]
        wqs2f = [sb(nc, es, f"wqsf{i}", [128, NCH, 128], F32) for i in range(2)]
        ws3f = [sb(nc, es, f"ws3f{i}", [128, NCH, 128], F32) for i in range(2)]
        w1_src = w1e[:, :].rearrange("(k p) o -> p k o", p=128)
        w3_src = w3e[:, :].rearrange("(k p) o -> p k o", p=128)
        w2_src = w2e[:, :].rearrange("(k p) o -> p k o", p=128)
        lagT2, lagV2, lagSc2, lagC2 = {}, {}, {}, {}
        for hc in range(22):
            wb = hc % 2
            lw0 = emit_dma_batch(P, "sync", [
                (wqs2f[wb][:], w1_src[:, :, hc*128:(hc+1)*128]),
                (ws3f[wb][:], w3_src[:, :, hc*128:(hc+1)*128]),
            ], waits={"vector": lagC2.get(wb, P.now("vector"))})

            @P.step("vector", waits={"sync": lw0,
                                     "tensor": lagT2.get(wb, P.now("tensor"))}, incs=1)
            def _(h, sem, wb=wb):
                h.tensor_copy(wqs2[wb][:], wqs2f[wb][:])
                h.tensor_copy(ws3[wb][:], ws3f[wb][:]).then_inc(sem, 1)
            lagC2[wb] = P.now("vector")
            for half in range(2):
                cols = slice(half * HALF, (half + 1) * HALF)
                i = hc * 2 + half
                pX = pA if i % 2 == 0 else pC
                pY = pB if i % 2 == 0 else pD4
                sl = sil_t if i % 2 == 0 else sil_t2
                @P.step("tensor", waits={"vector": max(lagC2[wb], lagV2.get(i % 2, P.now("vector"))),
                                         "scalar": lagSc2.get(i % 2, 0)}, incs=1)
                def _(h, sem, wb=wb, cols=cols, pX=pX, pY=pY):
                    for k in range(NCH):
                        h.matmul(pX[:], wqs2[wb][:, k, :], xd_fm[:, k, cols],
                                 start=(k == 0), stop=(k == NCH - 1))
                    for k in range(NCH):
                        ins = h.matmul(pY[:], ws3[wb][:, k, :], xd_fm[:, k, cols],
                                       start=(k == 0), stop=(k == NCH - 1))
                    ins.then_inc(sem, 1)
                lagT2[wb] = P.now("tensor")

                @P.step("scalar", waits={"tensor": lagT2[wb]}, incs=1)
                def _(h, sem, pX=pX, sl=sl):
                    h.activation(sl[:], pX[:], AF.Silu, bias=0.0,
                                 scale=1.0).then_inc(sem, 1)
                lagSc2[i % 2] = P.now("scalar")

                @P.step("vector", waits={"scalar": lagSc2[i % 2]}, incs=1)
                def _(h, sem, hc=hc, cols=cols, pY=pY, sl=sl):
                    h.tensor_tensor(hb[:, hc, cols], sl[:], pY[:],
                                    ALU.mult).then_inc(sem, 1)
                lagV2[i % 2] = P.now("vector")

        lagT3, lagV3, lagC3 = {}, {}, {}
        for half in range(2):
            cols = slice(half * HALF, (half + 1) * HALF)
            for oc in range(NCH):
                i = half * NCH + oc
                pb = i % 2
                @P.step("sync", waits={"vector": lagC3.get(pb, P.now("vector"))}, incs=32)
                def _(h, sem, pb=pb, oc=oc):
                    h.dma_start(out=wqs2f[pb][:, 0:NCH, :],
                                in_=w2_src[:, 0:NCH, oc*128:(oc+1)*128]).then_inc(sem, 16)
                    h.dma_start(out=ws3f[pb][:, 0:6, :],
                                in_=w2_src[:, NCH:22, oc*128:(oc+1)*128]).then_inc(sem, 16)
                myld0 = P.now("sync")

                @P.step("vector", waits={"sync": myld0,
                                         "tensor": lagT3.get(pb, P.now("tensor"))}, incs=1)
                def _(h, sem, pb=pb):
                    h.tensor_copy(wqs2[pb][:, 0:NCH, :], wqs2f[pb][:, 0:NCH, :])
                    h.tensor_copy(ws3[pb][:, 0:6, :], ws3f[pb][:, 0:6, :]).then_inc(sem, 1)
                lagC3[pb] = P.now("vector")

                @P.step("tensor", waits={"vector": max(lagC3[pb], lagV3.get(pb, P.now("vector")))}, incs=1)
                def _(h, sem, pb=pb, oc=oc, cols=cols):
                    for k in range(22):
                        wt = wqs2[pb][:, k, :] if k < NCH else ws3[pb][:, k - NCH, :]
                        ins = h.matmul(pA[:] if pb == 0 else pB[:], wt, hb[:, k, cols],
                                       start=(k == 0), stop=(k == 21))
                    ins.then_inc(sem, 1)
                lagT3[pb] = P.now("tensor")

                @P.step("vector", waits={"tensor": lagT3[pb]}, incs=1)
                def _(h, sem, pb=pb, oc=oc, cols=cols):
                    h.tensor_copy(oe_fm[:, oc, cols], pA[:] if pb == 0 else pB[:]).then_inc(sem, 1)
                lagV3[pb] = P.now("vector")

        for g in range(8):
            for k in range(NCH):
                @P.step("tensor", waits={"vector": P.now("vector")}, incs=1)
                def _(h, sem, g=g, k=k):
                    h.transpose(pT[:], oe_fm[:, k, g*128:(g+1)*128],
                                ident_b[:]).then_inc(sem, 1)

                @P.step("vector", waits={"tensor": P.now("tensor"),
                                         "sync": P.now("sync")}, incs=1)
                def _(h, sem, g=g, k=k):
                    h.tensor_copy(bstg2[(g*NCH+k) % 2][:, 0:128], pT[:]).then_inc(sem, 1)

                @P.step("sync", waits={"vector": P.now("vector")}, incs=16)
                def _(h, sem, g=g, k=k):
                    h.dma_start(out=oe_out[g*128:(g+1)*128, k*128:(k+1)*128],
                                in_=bstg2[(g*NCH+k) % 2][:, 0:128]).then_inc(sem, 16)

        @P.step("gpsimd", waits={"sync": P.now("sync"), "gpsimd": P.now("gpsimd"),
                                 "vector": P.now("vector")}, incs=0)
        def _(h, sem):
            pass

        P.run()
    return nc


def build_p5():
    nc = bass.Bass("TRN2", target_bir_lowering=False)
    es = contextlib.ExitStack()
    def din(name, shape, dtype=F32):
        return nc.dram_tensor(name, shape, dtype, kind="ExternalInput").ap()
    def dout(name, shape, dtype=F32):
        return nc.dram_tensor(name, shape, dtype, kind="ExternalOutput").ap()

    TALL = B * S
    x3b_t = din("x3b_t", [D, NTOK])
    oe_all = din("oe_all", [TALL, D], BF16)
    flat_tm = din("flat_tm", [128, 8])
    gates_tm = din("gates_tm", [128, 8])
    keep_tm = din("keep_tm", [128, 8])
    c_ident_b = din("c_ident_b", [128, 128], BF16)
    x4_out = dout("x4_out", [D, NTOK])

    with es:
        ident_b = sb(nc, es, "ident_b", [128, 128], BF16)
        flat_sb = sb(nc, es, "flat_sb", [128, 8], F32)
        flat_cl = sb(nc, es, "flat_cl", [128, 8], F32)
        gates_sb = sb(nc, es, "gates_sb", [128, 8], F32)
        keep_sb = sb(nc, es, "keep_sb", [128, 8], F32)
        gk_sb = sb(nc, es, "gk_sb", [128, 8], F32)
        idx_u = sb(nc, es, "idx_u", [128, 8], U32)
        og_tm = sb(nc, es, "og_tm", [128, 8, D], BF16)
        og_gk = sb(nc, es, "og_gk", [128, 8, D], BF16)
        ld2 = [sb(nc, es, f"ld2_{i}", [128, 128], F32) for i in range(2)]
        stg2 = [sb(nc, es, f"stg2_{i}", [128, 128], F32) for i in range(2)]
        pT = ps(nc, es, "pT", [128, 128], BF16)

        P = Prog(nc, es)
        ld_f = emit_dma_batch(P, "sync", [
            (ident_b[:], c_ident_b[:, :]),
            (flat_sb[:], flat_tm[:, :]),
            (gates_sb[:], gates_tm[:, :]),
            (keep_sb[:], keep_tm[:, :]),
        ])
        @P.step("vector", waits={"sync": ld_f}, incs=1)
        def _(h, sem):
            h.tensor_tensor(gk_sb[:], gates_sb[:], keep_sb[:], ALU.mult)
            h.tensor_scalar(flat_cl[:], flat_sb[:], float(TALL - 1), None, ALU.min)
            h.drain()
            h.tensor_copy(idx_u[:], flat_cl[:])
            h.drain().then_inc(sem, 1)

        @P.step("gpsimd", waits={"vector": P.now("vector")}, incs=16*8)
        def _(h, sem):
            for g in range(8):
                h.indirect_dma_start(
                    out=og_tm[:, g, :], out_offset=None, in_=oe_all[:, :],
                    in_offset=bass.IndirectOffsetOnAxis(ap=idx_u[:, g:g+1], axis=0),
                    bounds_check=TALL - 1, oob_is_err=True).then_inc(sem, 16)
        g_og = P.now("gpsimd")

        for g in range(8):
            @P.step("vector", waits={"gpsimd": g_og} if g == 0 else
                    {"vector": P.now("vector")}, incs=1)
            def _(h, sem, g=g):
                h.tensor_scalar(og_gk[:, g, :], og_tm[:, g, :],
                                gk_sb[:, g:g+1], None, ALU.mult).then_inc(sem, 1)
        v_gk = P.now("vector")

        x3_src = x3b_t[:, :].rearrange("(k p) t -> p k t", p=128)
        x4_dst = x4_out[:, :].rearrange("(k p) t -> p k t", p=128)
        for g in range(8):
            for k in range(NCH):
                i = g * NCH + k
                pb = i % 2
                @P.step("sync", waits={"vector": P.now("vector")} if i >= 2 else {"vector": v_gk},
                        incs=16)
                def _(h, sem, pb=pb, g=g, k=k):
                    h.dma_start(out=ld2[pb][:], in_=x3_src[:, k, g*128:(g+1)*128]
                                ).then_inc(sem, 16)

                @P.step("tensor", waits={"vector": v_gk if i < 2 else P.now("vector")},
                        incs=1)
                def _(h, sem, g=g, k=k):
                    h.transpose(pT[:], og_gk[:, g, k*128:(k+1)*128],
                                ident_b[:]).then_inc(sem, 1)

                @P.step("vector", waits={"tensor": P.now("tensor"),
                                         "sync": P.now("sync")}, incs=1)
                def _(h, sem, pb=pb):
                    h.tensor_tensor(stg2[pb][:], pT[:], ld2[pb][:], ALU.add).then_inc(sem, 1)

                @P.step("sync", waits={"vector": P.now("vector")}, incs=16)
                def _(h, sem, pb=pb, g=g, k=k):
                    h.dma_start(out=x4_dst[:, k, g*128:(g+1)*128],
                                in_=stg2[pb][:]).then_inc(sem, 16)

        @P.step("gpsimd", waits={"sync": P.now("sync")}, incs=0)
        def _(h, sem):
            pass

        P.run()
    return nc



def core_chunks(core):
    cc = core % 4
    return [4 * m + cc for m in range(8)]


def core_token_ids(core):
    idx = []
    for c in core_chunks(core):
        idx.extend(range(128 * c, 128 * c + 128))
    return np.asarray(idx)


def host_slices(x, core):
    b = core // 4
    return x[b][core_token_ids(core)]


def bf16_np():
    import ml_dtypes
    return ml_dtypes.bfloat16


def rope_tables(core):
    pos = core_token_ids(core).astype(np.float64)
    inv = 1.0 / 10000 ** (np.arange(0, HD, 2) / HD)
    invd = np.concatenate([inv, inv])
    ang = invd[:, None] * pos[None, :]
    return (np.cos(ang).astype(np.float32),
            np.sin(ang).astype(np.float32))


def p2_inputs(z, core, res1):
    bf = bf16_np()
    b = core // 4
    cc = core % 4
    u_glob = np.zeros((8, S), np.float32)
    for c2 in range(4 * b, 4 * b + 4):
        u_glob[:, core_token_ids(c2)] = res1[c2]["u_out"]
    cos, sin = rope_tables(core)
    ind4 = np.zeros((128, 512), np.float32)
    ind4[:, cc * 128:(cc + 1) * 128] = np.eye(128)
    msel = np.zeros((8, 512), np.float32)
    for er in range(512):
        msel[er // 64, er] = 1.0
    hgsel = np.zeros((128, 4), np.float32)
    for j in range(4):
        hgsel[32 * j:32 * (j + 1), j] = 1.0
    return {
        "x_t": feat_major(host_slices(z["L0_attn_x"], core)).astype(np.float32),
        "xln1_in": res1[core]["xln1_out"],
        "xn_in": res1[core]["xn_out"],
        "u_all": u_glob,
        "dlogit": np.ascontiguousarray(z["L0_attn_decay_logit"][:, None]),
        "w_out": np.ascontiguousarray(z["L0_attn_W_out"]),
        "b_out_t": vec_col(z["L0_attn_b_out"]),
        "d_skip_t": vec_col(z["L0_attn_D_skip"]),
        "gamma_t": vec_col(z["L0_attn_ln_gamma"]),
        "beta_t": vec_col(z["L0_attn_ln_beta"]),
        "w_down": np.ascontiguousarray(z["L0_attn_W_down"]),
        "b_down": np.ascontiguousarray(z["L0_attn_b_down"][:, None]),
        "w_up": np.ascontiguousarray(z["L0_attn_W_up"]),
        "b_up_t": vec_col(z["L0_attn_b_up"]),
        "a_t": np.ascontiguousarray(z["L0_ffn_A"].reshape(512, D).T),
        "bw_t": np.ascontiguousarray(z["L0_ffn_Bw"].transpose(0, 2, 1).reshape(512, D)),
        "wg0_t": np.ascontiguousarray(z["L0_ffn_Wg"].T),
        "wq_t": np.ascontiguousarray(z["L1_attn_Wq"].T),
        "wk_t": np.ascontiguousarray(z["L1_attn_Wk"].T),
        "wv_t": np.ascontiguousarray(z["L1_attn_Wv"].T),
        "wrq_t": np.ascontiguousarray(z["L1_attn_Wrq"].T),
        "wrk_t": np.ascontiguousarray(z["L1_attn_Wrk"].T),
        "qnw": np.ascontiguousarray(z["L1_attn_q_norm_w"][:, None]),
        "knw": np.ascontiguousarray(z["L1_attn_k_norm_w"][:, None]),
        "cosq": cos, "sinq": sin,
        "c_ones128_f": np.ones((128, 1), np.float32),
        "c_ones1_f": np.ones((1, 128), np.float32),
        "c_ones8v": np.ones((8, S), np.float32),
        "c_ones128x8": np.ones((128, 8), np.float32),
        "c_ident_b": np.eye(128).astype(bf),
        "c_ind4": ind4.astype(bf),
        "c_msel": msel.astype(bf),
        "c_hgsel": hgsel,
    }


def p3_inputs(z, core, res2):
    bf = bf16_np()
    b = core // 4
    cc = core % 4
    cores_b = list(range(4 * b, 4 * b + 4))
    K_full = np.zeros((512, S), bf)
    V_full = np.zeros((S, 512), bf)
    Rq_full = np.zeros((384, S), bf)
    Rk_full = np.zeros((384, S), bf)
    kn2_full = np.zeros((KVH, S), np.float32)
    rqn2_full = np.zeros((12, S), np.float32)
    rkn2_full = np.zeros((12, S), np.float32)
    for c2 in cores_b:
        ids = core_token_ids(c2)
        K_full[:, ids] = res2[c2]["k_out"]
        V_full[ids, :] = res2[c2]["v_out"]
        Rq_full[:, ids] = res2[c2]["rq_out"]
        Rk_full[:, ids] = res2[c2]["rk_out"]
        kn2_full[:, ids] = res2[c2]["kn2_out"]
        rqn2_full[:, ids] = res2[c2]["rqn2_out"]
        rkn2_full[:, ids] = res2[c2]["rkn2_out"]
    masks = np.zeros((8, 128, 4, 512), np.float32)
    for i in range(8):
        qg = 128 * (4 * i + cc) + np.arange(128)
        for kc in range(4):
            kg = 512 * i + 128 * kc + np.arange(128)
            m = (kg[:, None] <= qg[None, :]).astype(np.float32)
            masks[i, :, kc, :] = np.tile(m, (1, 4))
    gsel = np.zeros((12, 512), np.float32)
    for hh in range(12):
        gsel[hh, 32*hh:32*(hh+1)] = 0.5 if hh < 4 else 1.0
    return {
        "x2_t": res2[core]["x2_out"],
        "q_in": res2[core]["q_out"],
        "k_full": K_full, "v_full": V_full,
        "rq_full": Rq_full, "rk_full": Rk_full,
        "rq_own": res2[core]["rq_out"], "rk_own": res2[core]["rk_out"],
        "qn2_own": res2[core]["qn2_out"],
        "kn2_full": kn2_full,
        "rqn2_own": res2[core]["rqn2_out"], "rkn2_own": res2[core]["rkn2_out"],
        "rqn2_full": rqn2_full, "rkn2_full": rkn2_full,
        "gs12": np.ascontiguousarray(z["L1_attn_graph_scale"][:, None]),
        "masks": masks.astype(bf),
        "wo_t": np.ascontiguousarray(z["L1_attn_Wo"].T),
        "wg1": np.ascontiguousarray(z["L1_ffn_Wg"]),
        "sw1": np.ascontiguousarray(z["L1_ffn_sw1"]),
        "sw3": np.ascontiguousarray(z["L1_ffn_sw3"]),
        "sw2": np.ascontiguousarray(z["L1_ffn_sw2"]),
        "c_ones128_f": np.ones((128, 1), np.float32),
        "c_ones1_f": np.ones((1, 128), np.float32),
        "c_ones128_b": np.ones((128, 1), bf),
        "c_ones1_b": np.ones((1, 2048), bf),
        "c_ident_b": np.eye(128).astype(bf),
        "c_gsel": gsel,
    }


def p4_inputs(z, core, res3):
    bf = bf16_np()
    TALL = B * S
    xn4_all = np.zeros((TALL, D), bf)
    logits_all = np.zeros((TALL, 8), np.float32)
    for c2 in range(8):
        gids = 4096 * (c2 // 4) + core_token_ids(c2)
        xn4_all[gids] = res3[c2]["xn4tm_out"]
        logits_all[gids] = res3[c2]["logits_out"]
    tokid = (np.arange(64)[None, :] * 128 + np.arange(128)[:, None]).astype(np.uint32)
    tokid = np.repeat(tokid, 4, axis=1)
    return {
        "xn4_all": xn4_all,
        "logits_all": logits_all,
        "w1e": np.ascontiguousarray(z["L1_ffn_W1"][core]),
        "w3e": np.ascontiguousarray(z["L1_ffn_W3"][core]),
        "w2e": np.ascontiguousarray(z["L1_ffn_W2"][core]),
        "eoff": np.full((1, 1), 1024.0 * core, np.float32),
        "c_ones1_f": np.ones((1, 128), np.float32),
        "c_ones8_f": np.ones((8, 1), np.float32),
        "c_evec": (1024.0 * np.arange(8) - 1.0).astype(np.float32)[:, None],
        "c_ones128x8": np.ones((128, 8), np.float32),
        "c_ident_b": np.eye(128).astype(bf),
        "c_tokid": tokid,
        "c_bigidx": np.full((CAP, 4), 1_000_000_000, np.uint32),
        "c_zeros2048": np.zeros((128, D), bf),
    }


def p5_inputs(z, core, res3, res4):
    bf = bf16_np()
    TALL = B * S
    oe_all = np.zeros((TALL, D), bf)
    for e in range(8):
        oe_all[1024*e:1024*(e+1)] = res4[e]["oe_out"]
    gids = 4096 * (core // 4) + core_token_ids(core)
    flat_own = res4[0]["flat_out"][0, gids]
    keep_own = res4[0]["keep_out"][0, gids]
    cc = core % 4
    own_cols = [32 * (core // 4) + 4 * m + cc for m in range(8)]
    gates_tm = np.ascontiguousarray(res4[0]["gates_out"][:, own_cols])
    return {
        "x3b_t": res3[core]["x3b_out"],
        "oe_all": oe_all,
        "flat_tm": np.ascontiguousarray(flat_own.reshape(8, 128).T),
        "gates_tm": gates_tm,
        "keep_tm": np.ascontiguousarray(keep_own.reshape(8, 128).T),
        "c_ident_b": np.eye(128).astype(bf),
    }


def kernel(**inputs):
    z = inputs
    nc1 = build_p1()
    res1 = run_phase(nc1, [p1_inputs(z, c) for c in range(NCORE)])
    nc2 = build_p2()
    res2 = run_phase(nc2, [p2_inputs(z, c, res1) for c in range(NCORE)])
    nc3 = build_p3()
    res3 = run_phase(nc3, [p3_inputs(z, c, res2) for c in range(NCORE)])
    nc4 = build_p4()
    res4 = run_phase(nc4, [p4_inputs(z, c, res3) for c in range(NCORE)])
    nc5 = build_p5()
    res5 = run_phase(nc5, [p5_inputs(z, c, res3, res4) for c in range(NCORE)])
    out = np.zeros((B, S, D), np.float32)
    for c in range(NCORE):
        out[c // 4][core_token_ids(c)] = res5[c]["x4_out"].T
    return out


def p1_inputs(z, core):
    return {
        "x_t": feat_major(host_slices(z["L0_attn_x"], core)).astype(np.float32),
        "gamma_t": vec_col(z["L0_attn_ln_gamma"]),
        "beta_t": vec_col(z["L0_attn_ln_beta"]),
        "w_in": np.ascontiguousarray(z["L0_attn_W_in"]),
        "b_in": np.ascontiguousarray(z["L0_attn_b_in"][:, None]),
        "c_ones128_f": np.ones((128, 1), np.float32),
        "c_ones1_f": np.ones((1, 128), np.float32),
    }


def feat_major(a):
    return np.ascontiguousarray(a.T)


def vec_col(v):
    return np.ascontiguousarray(v.reshape(NCH, 128).T)


def run_phase(nc, in_maps):
    res = run_bass_kernel_spmd(nc, in_maps, core_ids=list(range(NCORE)))
    return res.results


if __name__ == "__main__":
    pass


# version 05 (final)
# speedup vs baseline: 1.4495x; 1.1437x over previous
import contextlib
import numpy as np

import concourse.bass as bass
import concourse.mybir as mybir
from concourse.bass_utils import run_bass_kernel_spmd

F32 = mybir.dt.float32
BF16 = mybir.dt.bfloat16
U32 = mybir.dt.uint32
I32 = mybir.dt.int32
AF = mybir.ActivationFunctionType
ALU = mybir.AluOpType
AX = mybir.AxisListType

D = 2048
NCH = 16
B = 2
S = 4096
NTOK = 1024
NCORE = 8
HALF = 512
EPS_LN = 1e-5
EPS_RMS = 1e-6
NH = 16
KVH = 4
HD = 128
HG = 12
RK = 32
SYM = 4
E = 8
CAP = 1024
HMOE = 2816
BIG = 1.0e7

ENGINES = ("sync", "vector", "scalar", "tensor", "gpsimd")


class Prog:

    def __init__(self, nc, es):
        self.nc = nc
        self.es = es
        self.steps = {e: [] for e in ENGINES}
        self.sem = {e: es.enter_context(nc.semaphore(f"ms_{e}")) for e in ENGINES}
        self.cnt = {e: 0 for e in ENGINES}

    def step(self, eng, waits=None, incs=0):
        w = dict(waits or {})
        def deco(fn):
            self.steps[eng].append((fn, w))
            self.cnt[eng] += incs
            return None
        return deco

    def now(self, eng):
        return self.cnt[eng]

    def run(self):
        block = self.es.enter_context(self.nc.Block())
        for eng in ENGINES:
            steps = self.steps[eng]
            sems = self.sem

            def make(eng, steps):
                def body(h):
                    mysem = sems[eng]
                    for fn, waits in steps:
                        for se, v in waits.items():
                            h.wait_ge(sems[se], v)
                        fn(h, mysem)
                return body
            getattr(block, eng)(make(eng, steps))


def sb(nc, es, name, shape, dtype):
    return es.enter_context(nc.sbuf_tensor(name, shape, dtype))


def ps(nc, es, name, shape, dtype=F32):
    return es.enter_context(nc.psum_tensor(name, shape, dtype))



def emit_dma_batch(P, eng, pairs, waits=None):
    n = len(pairs)

    @P.step(eng, waits=waits, incs=16 * n)
    def _(h, sem):
        for out, in_ in pairs:
            h.dma_start(out=out, in_=in_).then_inc(sem, 16)
    return P.now(eng)


def emit_ln(P, T, x_sb, out_sb, consts, scratch, gamma_beta=None, out2_sb=None,
            eps=EPS_LN):
    nc = P.nc
    sq2 = scratch["sq2"]
    p_sum, p_sq = scratch["p_sum"], scratch["p_sq"]
    p_mb, p_rb = scratch["p_mb"], scratch["p_rb"]
    r_mean, r_e2, r_m2, r_var, r_sd, r_rstd = (scratch[k] for k in
        ("r_mean", "r_e2", "r_m2", "r_var", "r_sd", "r_rstd"))
    ones128_f = consts["ones128_f"]
    ones1_f = consts["ones1_f"]

    for half in range(2):
        cols = slice(half * HALF, (half + 1) * HALF)
        for k in range(NCH):
            kb = k % 2
            if k == 0:
                swaits = {"vector": P.now("vector")}
            elif k >= 2:
                swaits = {"tensor": P.now("tensor")}
            else:
                swaits = {}

            @P.step("scalar", waits=swaits, incs=1)
            def _(h, sem, k=k, kb=kb, cols=cols):
                h.activation(sq2[kb][:], x_sb[:, k, cols], AF.Square,
                             bias=0.0, scale=1.0).then_inc(sem, 1)

            @P.step("tensor", waits={"scalar": P.now("scalar")}, incs=1)
            def _(h, sem, k=k, kb=kb, cols=cols):
                h.matmul(p_sum[:], ones128_f[:], x_sb[:, k, cols],
                         start=(k == 0), stop=(k == NCH - 1))
                h.matmul(p_sq[:], ones128_f[:], sq2[kb][:],
                         start=(k == 0), stop=(k == NCH - 1)).then_inc(sem, 1)
        @P.step("vector", waits={"tensor": P.now("tensor")}, incs=1)
        def _(h, sem, cols=cols):
            h.tensor_scalar(r_mean[:], p_sum[:], 1.0 / D, None, ALU.mult)
            h.tensor_scalar(r_e2[:], p_sq[:], 1.0 / D, None, ALU.mult)
            h.drain()
            h.tensor_tensor(r_m2[:], r_mean[:], r_mean[:], ALU.mult)
            h.drain()
            h.tensor_tensor(r_var[:], r_e2[:], r_m2[:], ALU.subtract)
            h.drain()
            h.tensor_scalar(r_m2[:], r_var[:], eps, None, ALU.add)
            h.drain().then_inc(sem, 1)

        @P.step("scalar", waits={"vector": P.now("vector")}, incs=1)
        def _(h, sem):
            h.activation(r_sd[:], r_m2[:], AF.Sqrt, bias=0.0, scale=1.0).then_inc(sem, 1)

        @P.step("vector", waits={"scalar": P.now("scalar")}, incs=1)
        def _(h, sem):
            h.reciprocal(r_rstd[:], r_sd[:]).then_inc(sem, 1)

        @P.step("tensor", waits={"vector": P.now("vector")}, incs=1)
        def _(h, sem):
            h.matmul(p_mb[:], ones1_f[:], r_mean[:], start=True, stop=True)
            h.matmul(p_rb[:], ones1_f[:], r_rstd[:], start=True, stop=True).then_inc(sem, 1)

        tmb = P.now("tensor")
        for k in range(NCH):
            @P.step("vector", waits={"tensor": tmb}, incs=1)
            def _(h, sem, k=k, cols=cols):
                tmp = scratch["tmp"]
                h.tensor_tensor(tmp[:], x_sb[:, k, cols], p_mb[:], ALU.subtract)
                if gamma_beta is None:
                    h.tensor_tensor(out_sb[:, k, cols], tmp[:], p_rb[:],
                                    ALU.mult).then_inc(sem, 1)
                else:
                    g_col, b_col = gamma_beta
                    h.tensor_tensor(tmp[:], tmp[:], p_rb[:], ALU.mult)
                    h.tensor_scalar(out_sb[:, k, cols], tmp[:],
                                    g_col[:, k:k+1], b_col[:, k:k+1],
                                    ALU.mult, ALU.add).then_inc(sem, 1)
        vnow = P.now("vector")

        @P.step("tensor", waits={"vector": vnow}, incs=0)
        def _(h, sem):
            pass



def build_p1():
    nc = bass.Bass("TRN2", target_bir_lowering=False)
    es = contextlib.ExitStack()
    ap = {}
    def din(name, shape, dtype=F32):
        t = nc.dram_tensor(name, shape, dtype, kind="ExternalInput").ap()
        ap[name] = t
        return t
    def dout(name, shape, dtype=F32):
        t = nc.dram_tensor(name, shape, dtype, kind="ExternalOutput").ap()
        ap[name] = t
        return t

    x_t = din("x_t", [D, NTOK])
    gamma_t = din("gamma_t", [128, NCH])
    beta_t = din("beta_t", [128, NCH])
    w_in = din("w_in", [D, 8])
    b_in = din("b_in", [8, 1])
    c_ones128_f = din("c_ones128_f", [128, 1])
    c_ones1_f = din("c_ones1_f", [1, 128])

    u_out = dout("u_out", [8, NTOK])
    xln1_out = dout("xln1_out", [D, NTOK])
    xn_out = dout("xn_out", [D, NTOK], BF16)

    with es:
        xs = sb(nc, es, "xs", [128, NCH, NTOK], F32)
        xln1 = sb(nc, es, "xln1", [128, NCH, NTOK], F32)
        xnb = sb(nc, es, "xnb", [128, NCH, NTOK], BF16)
        g_col = sb(nc, es, "g_col", [128, NCH], F32)
        b_col = sb(nc, es, "b_col", [128, NCH], F32)
        winb = sb(nc, es, "winb", [128, NCH, 8], BF16)
        binv = sb(nc, es, "binv", [8, 1], F32)
        ones128_f = sb(nc, es, "ones128_f", [128, 1], F32)
        ones1_f = sb(nc, es, "ones1_f", [1, 128], F32)
        u_sb = sb(nc, es, "u_sb", [8, NTOK], F32)
        scratch = dict(
            sq2=[sb(nc, es, f"sq{i}", [128, HALF], F32) for i in range(2)],
            tmp=sb(nc, es, "tmp", [128, HALF], F32),
            p_sum=ps(nc, es, "p_sum", [1, HALF]),
            p_sq=ps(nc, es, "p_sq", [1, HALF]),
            p_mb=ps(nc, es, "p_mb", [128, HALF]),
            p_rb=ps(nc, es, "p_rb", [128, HALF]),
        )
        for nm in ("r_mean", "r_e2", "r_m2", "r_var", "r_sd", "r_rstd"):
            scratch[nm] = sb(nc, es, nm, [1, HALF], F32)
        p_u = ps(nc, es, "p_u", [8, HALF])

        P = Prog(nc, es)
        consts = dict(ones128_f=ones128_f, ones1_f=ones1_f)

        ld = emit_dma_batch(P, "sync", [
            (xs[:], x_t[:, :].rearrange("(k p) t -> p k t", p=128)),
            (g_col[:], gamma_t[:, :]),
            (b_col[:], beta_t[:, :]),
            (binv[:], b_in[:, :]),
            (ones128_f[:], c_ones128_f[:, :]),
            (ones1_f[:], c_ones1_f[:, :]),
        ])
        emit_dma_batch(P, "gpsimd", [
            (winb[:], w_in[:, :].rearrange("(k p) r -> p k r", p=128)),
        ])
        gw = P.now("gpsimd")

        for e in ("scalar", "tensor", "vector"):
            @P.step(e, waits={"sync": ld, "gpsimd": gw}, incs=0)
            def _(h, sem):
                pass

        emit_ln(P, None, xs, xln1, consts, scratch, gamma_beta=None)
        emit_ln(P, None, xln1, xnb, consts, scratch, gamma_beta=(g_col, b_col))

        for half in range(2):
            cols = slice(half * HALF, (half + 1) * HALF)
            for k in range(NCH):
                @P.step("tensor", waits={"vector": P.now("vector")}, incs=(1 if k == NCH - 1 else 0))
                def _(h, sem, k=k, cols=cols):
                    ins = h.matmul(p_u[:], winb[:, k, :], xnb[:, k, cols],
                                   start=(k == 0), stop=(k == NCH - 1))
                    if k == NCH - 1:
                        ins.then_inc(sem, 1)

            @P.step("vector", waits={"tensor": P.now("tensor")}, incs=1)
            def _(h, sem, cols=cols):
                h.tensor_scalar(u_sb[:, cols], p_u[:], binv[:], None,
                                ALU.add).then_inc(sem, 1)

        vs = P.now("vector")
        st = emit_dma_batch(P, "gpsimd", [
            (u_out[:, :], u_sb[:]),
            (xln1_out[:, :].rearrange("(k p) t -> p k t", p=128), xln1[:]),
            (xn_out[:, :].rearrange("(k p) t -> p k t", p=128), xnb[:]),
        ], waits={"vector": vs})

        @P.step("gpsimd", waits={"gpsimd": st}, incs=0)
        def _(h, sem):
            pass

        P.run()
    return nc



def emit_ln_dram(P, x_dram, out_sb, consts, scratch, ld4, gamma_beta=None,
                 eps=EPS_LN, gsz=4):
    nc = P.nc
    sq4 = scratch["sq4"]
    p_sum, p_sq = scratch["p_sum"], scratch["p_sq"]
    p_mb, p_rb = scratch["p_mb"], scratch["p_rb"]
    r_mean, r_m2, r_e2, r_var, r_sd, r_rstd = (scratch[k] for k in
        ("r_mean", "r_m2", "r_e2", "r_var", "r_sd", "r_rstd"))
    ones128_f = consts["ones128_f"]
    ones1_f = consts["ones1_f"]
    src = x_dram[:, :].rearrange("(k p) t -> p k t", p=128)
    stash_m, stash_r = scratch["stash_m"], scratch["stash_r"]
    lnlag = {}

    ng = NCH // gsz
    for half in range(2):
        cols = slice(half * HALF, (half + 1) * HALF)
        for kk in range(ng):
            kb = kk % 2
            wconsume = ({"tensor": lnlag[kb]} if kb in lnlag
                        else {"vector": P.now("vector")})
            @P.step("sync", waits=wconsume, incs=16)
            def _(h, sem, kb=kb, kk=kk, cols=cols):
                h.dma_start(out=ld4[kb][:], in_=src[:, kk*gsz:(kk+1)*gsz, cols]).then_inc(sem, 16)
            lv = P.now("sync")

            @P.step("scalar", waits={"sync": lv, "tensor": lnlag.get(kb, 0)}, incs=1)
            def _(h, sem, kb=kb):
                h.activation(sq4[kb][:], ld4[kb][:], AF.Square,
                             bias=0.0, scale=1.0).then_inc(sem, 1)

            @P.step("tensor", waits={"scalar": P.now("scalar")}, incs=1)
            def _(h, sem, kb=kb, kk=kk):
                for j in range(gsz):
                    h.matmul(p_sum[:], ones128_f[:], ld4[kb][:, j, :],
                             start=(kk == 0 and j == 0), stop=(kk == ng - 1 and j == gsz - 1))
                    ins = h.matmul(p_sq[:], ones128_f[:], sq4[kb][:, j, :],
                                   start=(kk == 0 and j == 0), stop=(kk == ng - 1 and j == gsz - 1))
                ins.then_inc(sem, 1)
            lnlag[kb] = P.now("tensor")

        @P.step("vector", waits={"tensor": P.now("tensor")}, incs=1)
        def _(h, sem, half=half):
            h.tensor_scalar(r_mean[:], p_sum[:], 1.0 / D, None, ALU.mult)
            h.tensor_scalar(r_e2[:], p_sq[:], 1.0 / D, None, ALU.mult)
            h.drain()
            h.tensor_tensor(r_m2[:], r_mean[:], r_mean[:], ALU.mult)
            h.drain()
            h.tensor_tensor(r_var[:], r_e2[:], r_m2[:], ALU.subtract)
            h.drain()
            h.tensor_scalar(r_m2[:], r_var[:], eps, None, ALU.add)
            h.drain().then_inc(sem, 1)

        @P.step("scalar", waits={"vector": P.now("vector")}, incs=1)
        def _(h, sem):
            h.activation(r_sd[:], r_m2[:], AF.Sqrt, bias=0.0, scale=1.0).then_inc(sem, 1)

        @P.step("vector", waits={"scalar": P.now("scalar")}, incs=1)
        def _(h, sem, half=half):
            h.reciprocal(r_rstd[:], r_sd[:])
            h.tensor_copy(stash_m[0:1, half*HALF:(half+1)*HALF], r_mean[:])
            h.drain().then_inc(sem, 1)

        @P.step("vector", waits={"vector": P.now("vector")}, incs=1)
        def _(h, sem, half=half):
            h.tensor_copy(stash_r[0:1, half*HALF:(half+1)*HALF], r_rstd[:]).then_inc(sem, 1)

    lnlag = {}
    for half in range(2):
        cols = slice(half * HALF, (half + 1) * HALF)
        @P.step("tensor", waits={"vector": P.now("vector")}, incs=1)
        def _(h, sem, half=half):
            h.matmul(p_mb[:], ones1_f[:], stash_m[0:1, half*HALF:(half+1)*HALF],
                     start=True, stop=True)
            h.matmul(p_rb[:], ones1_f[:], stash_r[0:1, half*HALF:(half+1)*HALF],
                     start=True, stop=True).then_inc(sem, 1)
        tb = P.now("tensor")
        for kk in range(ng):
            kb = kk % 2
            wconsume = {"vector": lnlag[kb]} if kb in lnlag else {"tensor": tb}
            @P.step("sync", waits=wconsume, incs=16)
            def _(h, sem, kb=kb, kk=kk, cols=cols):
                h.dma_start(out=ld4[kb][:], in_=src[:, kk*gsz:(kk+1)*gsz, cols]).then_inc(sem, 16)
            lv = P.now("sync")

            @P.step("vector", waits={"sync": lv, "tensor": tb}, incs=1)
            def _(h, sem, kb=kb, kk=kk, cols=cols):
                tmp = scratch["tmp"]
                for j in range(gsz):
                    h.tensor_tensor(tmp[:], ld4[kb][:, j, :], p_mb[:], ALU.subtract)
                    if gamma_beta is None:
                        ins = h.tensor_tensor(out_sb[:, kk*gsz+j, cols], tmp[:], p_rb[:],
                                              ALU.mult)
                    else:
                        g_col, b_col = gamma_beta
                        h.tensor_tensor(tmp[:], tmp[:], p_rb[:], ALU.mult)
                        ins = h.tensor_scalar(out_sb[:, kk*gsz+j, cols], tmp[:],
                                              g_col[:, kk*gsz+j:kk*gsz+j+1],
                                              b_col[:, kk*gsz+j:kk*gsz+j+1],
                                              ALU.mult, ALU.add)
                ins.then_inc(sem, 1)
            lnlag[kb] = P.now("vector")


def build_p2():
    nc = bass.Bass("TRN2", target_bir_lowering=False)
    es = contextlib.ExitStack()
    def din(name, shape, dtype=F32):
        return nc.dram_tensor(name, shape, dtype, kind="ExternalInput").ap()
    def dout(name, shape, dtype=F32):
        return nc.dram_tensor(name, shape, dtype, kind="ExternalOutput").ap()

    x_t = din("x_t", [D, NTOK])
    xln1_in = din("xln1_in", [D, NTOK])
    xn_in = din("xn_in", [D, NTOK], BF16)
    u_all = din("u_all", [8, S])
    dlogit = din("dlogit", [8, 1])
    w_out = din("w_out", [8, D])
    b_out_t = din("b_out_t", [128, NCH])
    d_skip_t = din("d_skip_t", [128, NCH])
    gamma_t = din("gamma_t", [128, NCH])
    beta_t = din("beta_t", [128, NCH])
    w_down = din("w_down", [D, 32])
    b_down = din("b_down", [32, 1])
    w_up = din("w_up", [32, D])
    b_up_t = din("b_up_t", [128, NCH])
    a_t = din("a_t", [D, 512])
    bw_t = din("bw_t", [512, D])
    wg0_t = din("wg0_t", [D, 8])
    wq_t = din("wq_t", [D, D])
    wk_t = din("wk_t", [D, 512])
    wv_t = din("wv_t", [D, 512])
    wrq_t = din("wrq_t", [D, 384])
    wrk_t = din("wrk_t", [D, 384])
    qnw = din("qnw", [128, 1])
    knw = din("knw", [128, 1])
    cosq = din("cosq", [128, NTOK])
    sinq = din("sinq", [128, NTOK])
    c_ones128_f = din("c_ones128_f", [128, 1])
    c_ones1_f = din("c_ones1_f", [1, 128])
    c_ones128x8 = din("c_ones128x8", [128, 8])
    c_ident_b = din("c_ident_b", [128, 128], BF16)
    c_ind4 = din("c_ind4", [128, 512], BF16)
    c_msel = din("c_msel", [8, 512], BF16)
    c_hgsel = din("c_hgsel", [128, 4])

    x2_out = dout("x2_out", [D, NTOK])
    q_out = dout("q_out", [D, NTOK], BF16)
    k_out = dout("k_out", [512, NTOK], BF16)
    v_out = dout("v_out", [NTOK, 512], BF16)
    rq_out = dout("rq_out", [384, NTOK], BF16)
    rk_out = dout("rk_out", [384, NTOK], BF16)
    qn2_out = dout("qn2_out", [NH, NTOK])
    kn2_out = dout("kn2_out", [KVH, NTOK])
    rqn2_out = dout("rqn2_out", [12, NTOK])
    rkn2_out = dout("rkn2_out", [12, NTOK])
    xi_dram = nc.dram_tensor("xi_scr", [D, NTOK], F32, kind="Internal").ap()
    x1_dram = nc.dram_tensor("x1_scr", [D, NTOK], F32, kind="Internal").ap()

    with es:
        bufC = sb(nc, es, "bufC", [128, NCH, NTOK], BF16)
        wqs2 = [sb(nc, es, f"wqs{i}", [128, NCH, 128], BF16) for i in range(2)]
        wqs2f = [sb(nc, es, f"wqsf{i}", [128, NCH, 128], F32) for i in range(2)]
        woutb = sb(nc, es, "woutb", [8, NCH, 128], BF16)
        wdownb = sb(nc, es, "wdownb", [128, NCH, 32], BF16)
        wupb = sb(nc, es, "wupb", [32, NCH, 128], BF16)
        wg0b = sb(nc, es, "wg0b", [128, NCH, 8], BF16)
        bout_c = sb(nc, es, "bout_c", [128, NCH], F32)
        dskip_c = sb(nc, es, "dskip_c", [128, NCH], F32)
        g_col = sb(nc, es, "g_col", [128, NCH], F32)
        b_col = sb(nc, es, "b_col", [128, NCH], F32)
        bdown_v = sb(nc, es, "bdown_v", [32, 1], F32)
        bup_c = sb(nc, es, "bup_c", [128, NCH], F32)
        qnw_v = sb(nc, es, "qnw_v", [128, 1], F32)
        knw_v = sb(nc, es, "knw_v", [128, 1], F32)
        wqsc = sb(nc, es, "wqsc", [128, 1], F32)
        cos_sb = sb(nc, es, "cos_sb", [128, NTOK], F32)
        sin_sb = sb(nc, es, "sin_sb", [128, NTOK], F32)
        ones128_f = sb(nc, es, "ones128_f", [128, 1], F32)
        ones1_f = sb(nc, es, "ones1_f", [1, 128], F32)
        ones128x8 = sb(nc, es, "ones128x8", [128, 8], F32)
        ident_b = sb(nc, es, "ident_b", [128, 128], BF16)
        ind4 = sb(nc, es, "ind4", [128, 512], BF16)
        msel = sb(nc, es, "msel", [8, 512], BF16)
        hgsel = sb(nc, es, "hgsel", [128, 4], F32)
        u_seg = sb(nc, es, "u_seg", [8, NTOK], F32)
        dec_v = sb(nc, es, "dec_v", [8, 1], F32)
        dec_s = sb(nc, es, "dec_s", [8, 1], F32)
        decbc1 = sb(nc, es, "decbc1", [8, NTOK], F32)
        h_seg = sb(nc, es, "h_seg", [8, NTOK], F32)
        h_init = sb(nc, es, "h_init", [8, 1], F32)
        h_bfs = sb(nc, es, "h_bfs", [8, NTOK], BF16)
        h_tm = sb(nc, es, "h_tm", [128, 32, 8], BF16)
        sel_tm = sb(nc, es, "sel_tm", [128, 8], BF16)
        h_loc = sb(nc, es, "h_loc", [8, NTOK], BF16)
        ld2 = [sb(nc, es, f"ld2_{i}", [128, HALF], F32) for i in range(2)]
        ld2b = [sb(nc, es, f"ld2b_{i}", [128, HALF], F32) for i in range(2)]
        stg2 = [sb(nc, es, f"stg2_{i}", [128, HALF], F32) for i in range(2)]
        bstg2 = [sb(nc, es, f"bstg2_{i}", [128, HALF], BF16) for i in range(2)]
        hfb = sb(nc, es, "hfb", [32, NTOK], BF16)
        h0m_sb = sb(nc, es, "h0m", [128, 4, NTOK], BF16)
        expd = sb(nc, es, "expd", [128, 8], F32)
        mx8 = sb(nc, es, "mx8", [128, 1], F32)
        sh8 = sb(nc, es, "sh8", [128, 8], F32)
        se8 = sb(nc, es, "se8", [128, 1], F32)
        se8b = sb(nc, es, "se8b", [128, 1], F32)
        rc8 = sb(nc, es, "rc8", [128, 1], F32)
        g4 = sb(nc, es, "g4", [128, 1], F32)
        ohd = sb(nc, es, "ohd", [128, 8], F32)
        cs8 = sb(nc, es, "cs8", [128, 8], F32)
        ohf = sb(nc, es, "ohf", [128, 8], BF16)
        ohf2 = sb(nc, es, "ohf2", [128, 8], F32)
        ohTg = sb(nc, es, "ohTg", [8, NTOK], BF16)
        sq_a = sb(nc, es, "sq_a", [128, HALF], F32)
        rb_sb = sb(nc, es, "rb_sb", [128, HALF], F32)
        qa_t = sb(nc, es, "qa_t", [128, HALF], F32)
        qn_t = sb(nc, es, "qn_t", [128, HALF], F32)
        rot_t = sb(nc, es, "rot_t", [128, HALF], F32)
        t1_t = sb(nc, es, "t1_t", [128, HALF], F32)
        t2_t = sb(nc, es, "t2_t", [128, HALF], F32)
        qf_bf = sb(nc, es, "qf_bf", [128, HALF], BF16)
        r_a = sb(nc, es, "r_a", [1, HALF], F32)
        r_b = sb(nc, es, "r_b", [1, HALF], F32)
        n2row = sb(nc, es, "n2row", [1, HALF], F32)
        rn_sb = sb(nc, es, "rn_sb", [4, HALF], F32)
        sq2u = [sq_a, sb(nc, es, "sq_b", [128, HALF], F32)]
        rb2u = [rb_sb, sb(nc, es, "rb_b", [128, HALF], F32)]
        qa2u = [qa_t, sb(nc, es, "qa_b", [128, HALF], F32)]
        qn2u = [qn_t, sb(nc, es, "qn_b", [128, HALF], F32)]
        rot2u = [rot_t, sb(nc, es, "rot_b", [128, HALF], F32)]
        t12u = [t1_t, sb(nc, es, "t1_b", [128, HALF], F32)]
        t22u = [t2_t, sb(nc, es, "t2_b", [128, HALF], F32)]
        qf2u = [qf_bf, sb(nc, es, "qf_b", [128, HALF], BF16)]
        ra2u = [r_a, sb(nc, es, "ra_b", [1, HALF], F32)]
        n2r2u = [n2row, sb(nc, es, "n2r_b", [1, HALF], F32)]
        pA = ps(nc, es, "pA", [128, HALF])
        pB = ps(nc, es, "pB", [128, HALF])
        pC = ps(nc, es, "pC", [128, HALF])
        pD = ps(nc, es, "pD", [128, HALF])
        pE = ps(nc, es, "pE", [1, HALF])
        pF = ps(nc, es, "pF", [4, HALF])
        pT = ps(nc, es, "pT", [128, 128], BF16)
        p_mm = [pA, pB]
        scratch = dict(
            sq2=[sb(nc, es, f"lnsq{i}", [128, HALF], F32) for i in range(2)],
            sq4=[sb(nc, es, f"lnsq4{i}", [128, 2, HALF], F32) for i in range(2)],
            tmp=sb(nc, es, "lntmp", [128, HALF], F32),
            p_sum=pE, p_sq=pF[0:1, :], p_mb=pC, p_rb=pD,
            stash_m=sb(nc, es, "stash_m", [1, NTOK], F32),
            stash_r=sb(nc, es, "stash_r", [1, NTOK], F32),
        )
        lnld4 = [sb(nc, es, f"lnld4{i}", [128, 2, HALF], F32) for i in range(2)]
        for nm in ("r_mean", "r_e2", "r_m2", "r_var", "r_sd", "r_rstd"):
            scratch[nm] = sb(nc, es, nm, [1, HALF], F32)
        p_g = pC[:, 0:8]
        p_row = pE
        p_rn = pF
        p_hf = pC[0:32, :]
        p_msk = pC
        p_loc = pC[:, 0:8]
        prow2u = [pE, pF[0:1, :]]
        pbc2u = [pC, pD]

        P = Prog(nc, es)
        consts = dict(ones128_f=ones128_f, ones1_f=ones1_f)

        ld_f = emit_dma_batch(P, "sync", [
            (dec_v[:], dlogit[:, :]),
            (bout_c[:], b_out_t[:, :]),
            (dskip_c[:], d_skip_t[:, :]),
            (g_col[:], gamma_t[:, :]),
            (b_col[:], beta_t[:, :]),
            (bdown_v[:], b_down[:, :]),
            (bup_c[:], b_up_t[:, :]),
            (qnw_v[:], qnw[:, :]),
            (knw_v[:], knw[:, :]),
            (cos_sb[:], cosq[:, :]),
            (sin_sb[:], sinq[:, :]),
            (ones128_f[:], c_ones128_f[:, :]),
            (ones1_f[:], c_ones1_f[:, :]),
            (ones128x8[:], c_ones128x8[:, :]),
            (ident_b[:], c_ident_b[:, :]),
            (ind4[:], c_ind4[:, :]),
            (msel[:], c_msel[:, :]),
            (hgsel[:], c_hgsel[:, :]),
        ])
        ld_w1 = emit_dma_batch(P, "gpsimd", [
            (woutb[:], w_out[:, :].rearrange("r (k m) -> r k m", m=128)),
            (wdownb[:], w_down[:, :].rearrange("(k p) r -> p k r", p=128)),
            (wupb[:], w_up[:, :].rearrange("r (k m) -> r k m", m=128)),
            (wg0b[:], wg0_t[:, :].rearrange("(k p) r -> p k r", p=128)),
        ])
        ld_xn = emit_dma_batch(P, "gpsimd", [
            (bufC[:], xn_in[:, :].rearrange("(k p) t -> p k t", p=128)),
        ])

        for e in ("scalar", "tensor", "vector"):
            @P.step(e, waits={"sync": ld_f}, incs=0)
            def _(h, sem):
                pass

        @P.step("scalar", waits={}, incs=1)
        def _(h, sem):
            h.activation(dec_s[:], dec_v[:], AF.Sigmoid, bias=0.0, scale=1.0).then_inc(sem, 1)

        @P.step("vector", waits={"scalar": P.now("scalar")}, incs=1)
        def _(h, sem):
            h.memset(h_init[:], 0.0).then_inc(sem, 1)

        @P.step("vector", waits={"vector": P.now("vector")}, incs=0)
        def _(h, sem):
            pass
        for seg in range(4):
            lu = emit_dma_batch(P, "sync", [
                (u_seg[:], u_all[:, seg*NTOK:(seg+1)*NTOK]),
            ], waits={"vector": P.now("vector")})

            @P.step("vector", waits={"sync": lu}, incs=1)
            def _(h, sem, seg=seg):
                if seg == 0:
                    h.tensor_scalar(decbc1[:], u_seg[:], 0.0, dec_s[:], ALU.mult, ALU.add)
                h.tensor_tensor_scan(h_seg[:], decbc1[:], u_seg[:], h_init[:],
                                     ALU.mult, ALU.add)
                h.drain()
                h.tensor_copy(h_bfs[:], h_seg[:])
                h.tensor_copy(h_init[:], h_seg[:, NTOK-1:NTOK])
                h.drain().then_inc(sem, 1)

            v_h = P.now("vector")
            for g in range(8):
                @P.step("tensor", waits={"vector": v_h if g == 0 else P.now("vector")}, incs=1)
                def _(h, sem, g=g):
                    h.transpose(pT[:, 0:8], h_bfs[:, g*128:(g+1)*128], ident_b[0:8, 0:8]).then_inc(sem, 1)

                @P.step("vector", waits={"tensor": P.now("tensor")}, incs=1)
                def _(h, sem, g=g, seg=seg):
                    h.tensor_copy(h_tm[:, seg*8+g, :], pT[:, 0:8]).then_inc(sem, 1)

        for m in range(8):
            @P.step("tensor", waits={"vector": P.now("vector")}, incs=1)
            def _(h, sem, m=m):
                for j in range(4):
                    ins = h.matmul(p_loc[:], ind4[:, j*128:(j+1)*128], h_tm[:, 4*m+j, :],
                                   start=(j == 0), stop=(j == 3))
                ins.then_inc(sem, 1)

            @P.step("vector", waits={"tensor": P.now("tensor")}, incs=1)
            def _(h, sem):
                h.tensor_copy(sel_tm[:], p_loc[:]).then_inc(sem, 1)

            @P.step("tensor", waits={"vector": P.now("vector")}, incs=1)
            def _(h, sem):
                h.transpose(pT[0:8, :], sel_tm[:], ident_b[:]).then_inc(sem, 1)

            @P.step("vector", waits={"tensor": P.now("tensor")}, incs=1)
            def _(h, sem, m=m):
                h.tensor_copy(h_loc[:, m*128:(m+1)*128], pT[0:8, :]).then_inc(sem, 1)
        v_hloc = P.now("vector")

        @P.step("tensor", waits={"vector": v_hloc, "gpsimd": ld_w1}, incs=0)
        def _(h, sem):
            pass
        xln1_src = xln1_in[:, :].rearrange("(k p) t -> p k t", p=128)
        x_src = x_t[:, :].rearrange("(k p) t -> p k t", p=128)
        xi_dst = xi_dram[:, :].rearrange("(k p) t -> p k t", p=128)
        xi_src = xi_dst
        x1_dst = x1_dram[:, :].rearrange("(k p) t -> p k t", p=128)
        x1_src = x1_dst
        x2_dst = x2_out[:, :].rearrange("(k p) t -> p k t", p=128)
        lagV, lagS = {}, {}
        tmp2p = [scratch["tmp"], scratch["sq2"][0]]
        for half in range(2):
            cols = slice(half * HALF, (half + 1) * HALF)
            for oc in range(NCH):
                i = half * NCH + oc
                pb = i % 2
                @P.step("sync", waits={"vector": lagV.get(pb, 0)} if i >= 2 else {}, incs=16)
                def _(h, sem, pb=pb, oc=oc, cols=cols):
                    h.dma_start(out=ld2[pb][:], in_=xln1_src[:, oc, cols]).then_inc(sem, 16)
                mysync = P.now("sync")

                @P.step("tensor", waits={"vector": lagV.get(pb, 0)} if i >= 2 else {}, incs=1)
                def _(h, sem, pb=pb, oc=oc, cols=cols):
                    h.matmul(p_mm[pb][:], woutb[:, oc, :], h_loc[:, cols],
                             start=True, stop=True).then_inc(sem, 1)
                myt = P.now("tensor")

                @P.step("vector", waits={"tensor": myt, "sync": max(mysync, lagS.get(pb, 0)),
                                         "gpsimd": ld_xn}, incs=1)
                def _(h, sem, pb=pb, oc=oc, cols=cols):
                    tmp = tmp2p[pb]
                    h.scalar_tensor_tensor(tmp[:], bufC[:, oc, cols], dskip_c[:, oc:oc+1],
                                           p_mm[pb][:], ALU.mult, ALU.add)
                    h.scalar_tensor_tensor(stg2[pb][:], tmp[:], bout_c[:, oc:oc+1],
                                           ld2[pb][:], ALU.add, ALU.add).then_inc(sem, 1)
                lagV[pb] = P.now("vector")

                @P.step("sync", waits={"vector": lagV[pb]}, incs=16)
                def _(h, sem, pb=pb, oc=oc, cols=cols):
                    h.dma_start(out=xi_dst[:, oc, cols], in_=stg2[pb][:]).then_inc(sem, 16)
                lagS[pb] = P.now("sync")

        @P.step("sync", waits={"sync": P.now("sync")}, incs=0)
        def _(h, sem):
            pass

        emit_ln_dram(P, xi_dram, bufC, consts, scratch, lnld4, gamma_beta=(g_col, b_col), gsz=2)

        for half in range(2):
            cols = slice(half * HALF, (half + 1) * HALF)
            @P.step("tensor", waits={"vector": P.now("vector"),
                                     "scalar": P.now("scalar")}, incs=1)
            def _(h, sem, cols=cols):
                for k in range(NCH):
                    ins = h.matmul(p_hf[:], wdownb[:, k, :], bufC[:, k, cols],
                                   start=(k == 0), stop=(k == NCH - 1))
                ins.then_inc(sem, 1)

            @P.step("scalar", waits={"tensor": P.now("tensor")}, incs=1)
            def _(h, sem, cols=cols):
                h.activation(hfb[:, cols], p_hf[:], AF.Relu, bias=bdown_v[:],
                             scale=1.0).then_inc(sem, 1)
        s_hf = P.now("scalar")
        lagV, lagS = {}, {}
        for half in range(2):
            cols = slice(half * HALF, (half + 1) * HALF)
            for oc in range(NCH):
                i = half * NCH + oc
                pb = i % 2
                @P.step("sync", waits={"vector": lagV.get(pb, 0)} if i >= 2 else {}, incs=32)
                def _(h, sem, pb=pb, oc=oc, cols=cols):
                    h.dma_start(out=ld2[pb][:], in_=x_src[:, oc, cols]).then_inc(sem, 16)
                    h.dma_start(out=ld2b[pb][:], in_=xi_src[:, oc, cols]).then_inc(sem, 16)
                mysync = P.now("sync")

                @P.step("tensor", waits={"scalar": s_hf} if i < 2 else
                        {"vector": lagV.get(pb, 0)}, incs=1)
                def _(h, sem, pb=pb, oc=oc, cols=cols):
                    h.matmul(p_mm[pb][:], wupb[:, oc, :], hfb[:, cols],
                             start=True, stop=True).then_inc(sem, 1)
                myt = P.now("tensor")

                @P.step("vector", waits={"tensor": myt, "sync": max(mysync, lagS.get(pb, 0))},
                        incs=1)
                def _(h, sem, pb=pb, oc=oc, cols=cols):
                    tmp = tmp2p[pb]
                    h.scalar_tensor_tensor(tmp[:], ld2b[pb][:], bup_c[:, oc:oc+1],
                                           p_mm[pb][:], ALU.add, ALU.add)
                    h.tensor_tensor(stg2[pb][:], tmp[:], ld2[pb][:],
                                    ALU.add).then_inc(sem, 1)
                lagV[pb] = P.now("vector")

                @P.step("sync", waits={"vector": lagV[pb]}, incs=16)
                def _(h, sem, pb=pb, oc=oc, cols=cols):
                    h.dma_start(out=x1_dst[:, oc, cols], in_=stg2[pb][:]).then_inc(sem, 16)
                lagS[pb] = P.now("sync")

        @P.step("sync", waits={"sync": P.now("sync")}, incs=0)
        def _(h, sem):
            pass

        emit_ln_dram(P, x1_dram, bufC, consts, scratch, lnld4, gamma_beta=None, gsz=2)

        for tt in range(8):
            @P.step("tensor", waits={"vector": P.now("vector"),
                                     "scalar": P.now("scalar")}, incs=1)
            def _(h, sem, tt=tt):
                for k in range(NCH):
                    ins = h.matmul(p_g[:], bufC[:, k, tt*128:(tt+1)*128],
                                   wg0b[:, k, :], start=(k == 0), stop=(k == NCH - 1))
                ins.then_inc(sem, 1)

            @P.step("vector", waits={"tensor": P.now("tensor")}, incs=1)
            def _(h, sem):
                h.reduce_max(mx8[:], p_g[:], axis=AX.X).then_inc(sem, 1)

            @P.step("vector", waits={"vector": P.now("vector")}, incs=1)
            def _(h, sem):
                h.tensor_scalar(ohd[:], p_g[:], mx8[:], None, ALU.is_ge)
                h.tensor_scalar(sh8[:], p_g[:], mx8[:], None, ALU.subtract).then_inc(sem, 1)

            @P.step("scalar", waits={"vector": P.now("vector")}, incs=1)
            def _(h, sem):
                h.activation(expd[:], sh8[:], AF.Exp, bias=0.0,
                             scale=1.0).then_inc(sem, 1)

            @P.step("vector", waits={"scalar": P.now("scalar")}, incs=1)
            def _(h, sem):
                h.reduce_sum(se8b[:], expd[:], axis=AX.X).then_inc(sem, 1)

            @P.step("vector", waits={"vector": P.now("vector")}, incs=1)
            def _(h, sem):
                h.reciprocal(rc8[:], se8b[:])
                h.tensor_tensor_scan(cs8[:], ones128x8[:], ohd[:], 0.0, ALU.mult, ALU.add)
                h.drain().then_inc(sem, 1)

            @P.step("vector", waits={"vector": P.now("vector")}, incs=1)
            def _(h, sem):
                h.tensor_scalar(g4[:], rc8[:], 4.0, None, ALU.mult).then_inc(sem, 1)

            @P.step("vector", waits={"vector": P.now("vector")}, incs=1)
            def _(h, sem):
                h.scalar_tensor_tensor(ohf2[:], cs8[:], 1.0, ohd[:], ALU.is_equal, ALU.mult).then_inc(sem, 1)

            @P.step("vector", waits={"vector": P.now("vector")}, incs=1)
            def _(h, sem, tt=tt):
                h.tensor_scalar(ohf[:], ohf2[:], g4[:], None, ALU.mult).then_inc(sem, 1)

            @P.step("tensor", waits={"vector": P.now("vector")}, incs=1)
            def _(h, sem):
                h.transpose(pT[0:8, :], ohf[:], ident_b[:]).then_inc(sem, 1)

            @P.step("vector", waits={"tensor": P.now("tensor")}, incs=1)
            def _(h, sem, tt=tt):
                h.tensor_copy(ohTg[:, tt*128:(tt+1)*128], pT[0:8, :]).then_inc(sem, 1)

        lagV, lagT, lagC = {}, {}, {}
        for half in range(2):
            cols = slice(half * HALF, (half + 1) * HALF)
            for ec in range(4):
                i = half * 4 + ec
                pb = i % 2
                wb = i % 2
                @P.step("sync", waits={"vector": lagC.get(wb, P.now("vector"))}, incs=16)
                def _(h, sem, wb=wb, ec=ec):
                    h.dma_start(out=wqs2f[wb][:],
                                in_=a_t[:, :].rearrange("(k p) r -> p k r", p=128)[:, :, ec*128:(ec+1)*128]
                                ).then_inc(sem, 16)
                myld = P.now("sync")

                @P.step("vector", waits={"sync": myld,
                                         "tensor": lagT.get(wb, P.now("tensor"))}, incs=1)
                def _(h, sem, wb=wb):
                    h.tensor_copy(wqs2[wb][:], wqs2f[wb][:]).then_inc(sem, 1)
                lagC[wb] = P.now("vector")

                @P.step("tensor", waits={"vector": P.now("vector")}, incs=1)
                def _(h, sem, pb=pb, wb=wb, ec=ec, cols=cols):
                    for k in range(NCH):
                        h.matmul(p_mm[pb][:], wqs2[wb][:, k, :],
                                 bufC[:, k, cols], start=(k == 0), stop=(k == NCH - 1))
                    h.matmul(p_msk[:], msel[:, ec*128:(ec+1)*128], ohTg[:, cols],
                             start=True, stop=True).then_inc(sem, 1)
                lagT[wb] = P.now("tensor")

                @P.step("vector", waits={"tensor": lagT[wb]}, incs=1)
                def _(h, sem, pb=pb, ec=ec, cols=cols):
                    h.tensor_copy(tmp2p[pb][:], p_mm[pb][:])
                    h.tensor_tensor(h0m_sb[:, ec, cols], tmp2p[pb][:], p_msk[:],
                                    ALU.mult).then_inc(sem, 1)

        lagV, lagT, lagS, lagC = {}, {}, {}, {}
        for half in range(2):
            cols = slice(half * HALF, (half + 1) * HALF)
            for oc in range(NCH):
                i = half * NCH + oc
                pb = i % 2
                @P.step("sync", waits={"vector": lagC.get(8 + pb, P.now("vector"))}, incs=16)
                def _(h, sem, pb=pb, oc=oc):
                    h.dma_start(out=wqs2f[pb][:, 0:4, :],
                                in_=bw_t[:, :].rearrange("(c p) o -> p c o", p=128)[:, :, oc*128:(oc+1)*128]
                                ).then_inc(sem, 16)
                myld0 = P.now("sync")

                @P.step("vector", waits={"sync": myld0,
                                         "tensor": lagT.get(pb, P.now("tensor"))}, incs=1)
                def _(h, sem, pb=pb):
                    h.tensor_copy(wqs2[pb][:, 0:4, :], wqs2f[pb][:, 0:4, :]).then_inc(sem, 1)
                lagC[8 + pb] = P.now("vector")

                @P.step("sync", waits={"vector": lagV.get(pb, 0)} if i >= 2 else {}, incs=16)
                def _(h, sem, pb=pb, oc=oc, cols=cols):
                    h.dma_start(out=ld2[pb][:], in_=x1_src[:, oc, cols]).then_inc(sem, 16)
                mysync = P.now("sync")

                @P.step("tensor", waits={"vector": P.now("vector")}, incs=1)
                def _(h, sem, pb=pb, oc=oc, cols=cols):
                    for ec in range(4):
                        ins = h.matmul(p_mm[pb][:], wqs2[pb][:, ec, :],
                                       h0m_sb[:, ec, cols], start=(ec == 0), stop=(ec == 3))
                    ins.then_inc(sem, 1)
                lagT[pb] = P.now("tensor")

                @P.step("vector", waits={"tensor": lagT[pb],
                                         "sync": max(mysync, lagS.get(pb, 0))}, incs=1)
                def _(h, sem, pb=pb, oc=oc, cols=cols):
                    h.tensor_tensor(stg2[pb][:], p_mm[pb][:], ld2[pb][:],
                                    ALU.add).then_inc(sem, 1)
                lagV[pb] = P.now("vector")

                @P.step("sync", waits={"vector": lagV[pb]}, incs=16)
                def _(h, sem, pb=pb, oc=oc, cols=cols):
                    h.dma_start(out=x2_dst[:, oc, cols], in_=stg2[pb][:]).then_inc(sem, 16)
                lagS[pb] = P.now("sync")

        @P.step("sync", waits={"sync": P.now("sync")}, incs=0)
        def _(h, sem):
            pass

        emit_ln_dram(P, x2_out, bufC, consts, scratch, lnld4, gamma_beta=None, gsz=2)

        @P.step("vector", waits={}, incs=1)
        def _(h, sem):
            h.tensor_scalar(wqsc[:], qnw_v[:], 1.0 / np.sqrt(128.0), None,
                            ALU.mult).then_inc(sem, 1)

        def qk_pipeline(n_mtiles, wsrc_dram, w_vec, out_dram, n2_out_dram):
            qk_lag = {}
            qk_lagc = {}
            lag_store = {}
            lag_qf = {}
            wsrc = wsrc_dram[:, :].rearrange("(k p) o -> p k o", p=128)
            odst = out_dram[:, :].rearrange("(mm p) t -> p mm t", p=128)
            for m in range(n_mtiles):
                wb = m % 2
                @P.step("sync", waits={"vector": qk_lagc.get(wb, P.now("vector"))}, incs=16)
                def _(h, sem, m=m, wb=wb):
                    h.dma_start(out=wqs2f[wb][:], in_=wsrc[:, :, m*128:(m+1)*128]
                                ).then_inc(sem, 16)
                myld = P.now("sync")

                @P.step("vector", waits={"sync": myld,
                                         "tensor": qk_lag.get(wb, P.now("tensor"))}, incs=1)
                def _(h, sem, wb=wb):
                    h.tensor_copy(wqs2[wb][:], wqs2f[wb][:]).then_inc(sem, 1)
                qk_lagc[wb] = P.now("vector")
                g_w = P.now("vector")
                for half in range(2):
                    cols = slice(half * HALF, (half + 1) * HALF)
                    i = m * 2 + half
                    pb = i % 2
                    u = i % 2
                    sqU, raU, rbU = sq2u[u], ra2u[u], rb2u[u]
                    qaU, qnU, rotU = qa2u[u], qn2u[u], rot2u[u]
                    t1U, t2U, qfU, n2U = t12u[u], t22u[u], qf2u[u], n2r2u[u]
                    prowU = prow2u[u]
                    pbcU = pbc2u[u]
                    @P.step("tensor", waits={"vector": g_w}, incs=1)
                    def _(h, sem, pb=pb, wb=wb, cols=cols):
                        for k in range(NCH):
                            ins = h.matmul(p_mm[pb][:], wqs2[wb][:, k, :], bufC[:, k, cols],
                                           start=(k == 0), stop=(k == NCH - 1))
                        ins.then_inc(sem, 1)
                    t_mm = P.now("tensor")
                    qk_lag[wb] = t_mm

                    @P.step("scalar", waits={"tensor": t_mm}, incs=1)
                    def _(h, sem, pb=pb, sqU=sqU):
                        h.activation(sqU[:], p_mm[pb][:], AF.Square, bias=0.0,
                                     scale=1.0).then_inc(sem, 1)

                    @P.step("tensor", waits={"scalar": P.now("scalar"),
                                             "vector": P.now("vector")}, incs=1)
                    def _(h, sem, sqU=sqU, prowU=prowU):
                        h.matmul(prowU[:], ones128_f[:], sqU[:], start=True,
                                 stop=True).then_inc(sem, 1)

                    @P.step("vector", waits={"tensor": P.now("tensor")}, incs=1)
                    def _(h, sem, raU=raU, prowU=prowU):
                        h.tensor_scalar(raU[:], prowU[:], 1.0 / 128.0, EPS_RMS,
                                        ALU.mult, ALU.add).then_inc(sem, 1)

                    @P.step("scalar", waits={"vector": P.now("vector")}, incs=1)
                    def _(h, sem, raU=raU):
                        h.activation(raU[:], raU[:], AF.Sqrt, bias=0.0,
                                     scale=1.0).then_inc(sem, 1)

                    @P.step("vector", waits={"scalar": P.now("scalar")}, incs=1)
                    def _(h, sem, raU=raU, rbU=rbU):
                        h.reciprocal(rbU[0:1, 0:HALF], raU[:])
                        h.drain().then_inc(sem, 1)

                    @P.step("tensor", waits={"vector": P.now("vector")}, incs=1)
                    def _(h, sem, rbU=rbU, pbcU=pbcU):
                        h.matmul(pbcU[:], ones1_f[:], rbU[0:1, 0:HALF], start=True,
                                 stop=True).then_inc(sem, 1)

                    @P.step("vector", waits={"tensor": P.now("tensor")}, incs=1)
                    def _(h, sem, pb=pb, cols=cols, qaU=qaU, qnU=qnU, rotU=rotU,
                          t1U=t1U, pbcU=pbcU, rbU=rbU):
                        h.tensor_copy(rbU[:], pbcU[:])
                        h.tensor_scalar(qaU[:], p_mm[pb][:], w_vec[:], None, ALU.mult)
                        h.tensor_tensor(qnU[:], qaU[:], rbU[:], ALU.mult)
                        h.tensor_scalar(rotU[0:64, :], qnU[64:128, :], -1.0, None, ALU.mult)
                        h.tensor_copy(rotU[64:128, :], qnU[0:64, :])
                        h.tensor_tensor(t1U[:], qnU[:], cos_sb[:, cols], ALU.mult).then_inc(sem, 1)

                    @P.step("gpsimd", waits={"vector": P.now("vector")}, incs=1)
                    def _(h, sem, cols=cols, rotU=rotU, t2U=t2U):
                        h.tensor_tensor(t2U[:], rotU[:], sin_sb[:, cols],
                                        ALU.mult).then_inc(sem, 1)

                    @P.step("vector", waits={"gpsimd": P.now("gpsimd"),
                                             "sync": lag_store.get(u, 0)}, incs=1)
                    def _(h, sem, t1U=t1U, t2U=t2U, qfU=qfU):
                        h.tensor_tensor(qfU[:], t1U[:], t2U[:], ALU.add).then_inc(sem, 1)
                    lag_qf[u] = P.now("vector")

                    @P.step("scalar", waits={"vector": lag_qf[u]}, incs=1)
                    def _(h, sem, sqU=sqU, qfU=qfU):
                        h.activation(sqU[:], qfU[:], AF.Square, bias=0.0,
                                     scale=1.0).then_inc(sem, 1)

                    @P.step("tensor", waits={"scalar": P.now("scalar"),
                                             "vector": P.now("vector")}, incs=1)
                    def _(h, sem, sqU=sqU, prowU=prowU):
                        h.matmul(prowU[:], ones128_f[:], sqU[:], start=True,
                                 stop=True).then_inc(sem, 1)

                    @P.step("vector", waits={"tensor": P.now("tensor")}, incs=1)
                    def _(h, sem, n2U=n2U, prowU=prowU):
                        h.tensor_copy(n2U[:], prowU[:]).then_inc(sem, 1)

                    @P.step("sync", waits={"vector": P.now("vector")}, incs=32)
                    def _(h, sem, m=m, cols=cols, qfU=qfU, n2U=n2U):
                        h.dma_start(out=odst[:, m, cols], in_=qfU[:]).then_inc(sem, 16)
                        h.dma_start(out=n2_out_dram[m:m+1, cols], in_=n2U[:]).then_inc(sem, 16)
                    lag_store[u] = P.now("sync")

        qk_pipeline(NH, wq_t, wqsc, q_out, qn2_out)
        qk_pipeline(KVH, wk_t, knw_v, k_out, kn2_out)

        wvsrc = wv_t[:, :].rearrange("(k p) o -> p k o", p=128)
        v_lagc, v_lagt = {}, {}
        for m in range(4):
            wb = m % 2
            @P.step("sync", waits={"vector": v_lagc.get(wb, P.now("vector"))}, incs=16)
            def _(h, sem, m=m, wb=wb):
                h.dma_start(out=wqs2f[wb][:], in_=wvsrc[:, :, m*128:(m+1)*128]).then_inc(sem, 16)
            myld = P.now("sync")

            @P.step("vector", waits={"sync": myld,
                                     "tensor": v_lagt.get(wb, P.now("tensor"))}, incs=1)
            def _(h, sem, wb=wb):
                h.tensor_copy(wqs2[wb][:], wqs2f[wb][:]).then_inc(sem, 1)
            v_lagc[wb] = P.now("vector")
            g_w = P.now("vector")
            for half in range(2):
                cols = slice(half * HALF, (half + 1) * HALF)
                i = m * 2 + half
                pb = i % 2
                @P.step("tensor", waits={"vector": g_w}, incs=1)
                def _(h, sem, pb=pb, wb=wb, cols=cols):
                    for k in range(NCH):
                        ins = h.matmul(p_mm[pb][:], wqs2[wb][:, k, :], bufC[:, k, cols],
                                       start=(k == 0), stop=(k == NCH - 1))
                    ins.then_inc(sem, 1)
                v_lagt[wb] = P.now("tensor")

                @P.step("vector", waits={"tensor": P.now("tensor")}, incs=1)
                def _(h, sem, pb=pb):
                    h.tensor_copy(bstg2[pb][:], p_mm[pb][:]).then_inc(sem, 1)

                for s4 in range(4):
                    @P.step("tensor", waits={"vector": P.now("vector")}, incs=1)
                    def _(h, sem, pb=pb, s4=s4):
                        h.transpose(pT[:], bstg2[pb][:, s4*128:(s4+1)*128],
                                    ident_b[:]).then_inc(sem, 1)

                    @P.step("vector", waits={"tensor": P.now("tensor"),
                                             "sync": P.now("sync")}, incs=1)
                    def _(h, sem, s4=s4):
                        h.tensor_copy(qf_bf[:, 0:128], pT[:]).then_inc(sem, 1)

                    @P.step("sync", waits={"vector": P.now("vector")}, incs=16)
                    def _(h, sem, m=m, half=half, s4=s4):
                        h.dma_start(out=v_out[half*512+s4*128:half*512+(s4+1)*128,
                                              m*128:(m+1)*128],
                                    in_=qf_bf[:, 0:128]).then_inc(sem, 16)

        def r_pipeline(wsrc_dram, out_dram, n2_out_dram):
            wsrc = wsrc_dram[:, :].rearrange("(k p) o -> p k o", p=128)
            r_lagc, r_lagt = {}, {}
            for rt in range(3):
                wb = rt % 2
                @P.step("sync", waits={"vector": r_lagc.get(wb, P.now("vector"))}, incs=16)
                def _(h, sem, rt=rt, wb=wb):
                    h.dma_start(out=wqs2f[wb][:], in_=wsrc[:, :, rt*128:(rt+1)*128]).then_inc(sem, 16)
                myld = P.now("sync")

                @P.step("vector", waits={"sync": myld,
                                         "tensor": r_lagt.get(wb, P.now("tensor"))}, incs=1)
                def _(h, sem, wb=wb):
                    h.tensor_copy(wqs2[wb][:], wqs2f[wb][:]).then_inc(sem, 1)
                r_lagc[wb] = P.now("vector")
                g_w = P.now("vector")
                for half in range(2):
                    cols = slice(half * HALF, (half + 1) * HALF)
                    i = rt * 2 + half
                    pb = i % 2
                    @P.step("tensor", waits={"vector": g_w}, incs=1)
                    def _(h, sem, pb=pb, wb=wb, cols=cols):
                        for k in range(NCH):
                            ins = h.matmul(p_mm[pb][:], wqs2[wb][:, k, :],
                                           bufC[:, k, cols], start=(k == 0), stop=(k == NCH - 1))
                        ins.then_inc(sem, 1)
                    r_lagt[wb] = P.now("tensor")

                    @P.step("scalar", waits={"tensor": P.now("tensor")}, incs=1)
                    def _(h, sem, pb=pb):
                        h.activation(sq_a[:], p_mm[pb][:], AF.Square, bias=0.0,
                                     scale=1.0).then_inc(sem, 1)

                    @P.step("tensor", waits={"scalar": P.now("scalar"),
                                             "vector": P.now("vector")}, incs=1)
                    def _(h, sem):
                        h.matmul(p_rn[:], hgsel[:], sq_a[:], start=True,
                                 stop=True).then_inc(sem, 1)

                    @P.step("vector", waits={"tensor": P.now("tensor"),
                                             "sync": P.now("sync")}, incs=1)
                    def _(h, sem, pb=pb):
                        h.tensor_copy(bstg2[pb][:], p_mm[pb][:])
                        h.tensor_copy(rn_sb[:], p_rn[:]).then_inc(sem, 1)

                    @P.step("sync", waits={"vector": P.now("vector")}, incs=32)
                    def _(h, sem, rt=rt, cols=cols, pb=pb):
                        h.dma_start(out=out_dram[rt*128:(rt+1)*128, cols],
                                    in_=bstg2[pb][:]).then_inc(sem, 16)
                        h.dma_start(out=n2_out_dram[rt*4:(rt+1)*4, cols],
                                    in_=rn_sb[:]).then_inc(sem, 16)

        r_pipeline(wrq_t, rq_out, rqn2_out)
        r_pipeline(wrk_t, rk_out, rkn2_out)

        @P.step("gpsimd", waits={"sync": P.now("sync"), "gpsimd": P.now("gpsimd"),
                                 "vector": P.now("vector")}, incs=0)
        def _(h, sem):
            pass

        P.run()
    return nc


def build_p3():
    nc = bass.Bass("TRN2", target_bir_lowering=False)
    es = contextlib.ExitStack()
    def din(name, shape, dtype=F32):
        return nc.dram_tensor(name, shape, dtype, kind="ExternalInput").ap()
    def dout(name, shape, dtype=F32):
        return nc.dram_tensor(name, shape, dtype, kind="ExternalOutput").ap()

    x2_t = din("x2_t", [D, NTOK])
    q_in = din("q_in", [D, NTOK], BF16)
    k_full = din("k_full", [512, S], BF16)
    v_full = din("v_full", [S, 512], BF16)
    rq_full = din("rq_full", [384, S], BF16)
    rk_full = din("rk_full", [384, S], BF16)
    rq_own = din("rq_own", [384, NTOK], BF16)
    rk_own = din("rk_own", [384, NTOK], BF16)
    qn2_own = din("qn2_own", [NH, NTOK])
    kn2_full = din("kn2_full", [KVH, S])
    rqn2_own = din("rqn2_own", [12, NTOK])
    rkn2_own = din("rkn2_own", [12, NTOK])
    rqn2_full = din("rqn2_full", [12, S])
    rkn2_full = din("rkn2_full", [12, S])
    gs12 = din("gs12", [12, 1])
    masks = din("masks", [8, 128, 4, 512], BF16)
    wo_t = din("wo_t", [D, D])
    wg1 = din("wg1", [D, 8])
    sw1 = din("sw1", [D, HMOE])
    sw3 = din("sw3", [D, HMOE])
    sw2 = din("sw2", [HMOE, D])
    c_ones128_f = din("c_ones128_f", [128, 1])
    c_ones1_f = din("c_ones1_f", [1, 128])
    c_ones128_b = din("c_ones128_b", [128, 1], BF16)
    c_ones1_b = din("c_ones1_b", [1, 2048], BF16)
    c_ident_b = din("c_ident_b", [128, 128], BF16)
    c_gsel = din("c_gsel", [12, 512])

    x3b_out = dout("x3b_out", [D, NTOK])
    xn4tm_out = dout("xn4tm_out", [NTOK, D], BF16)
    logits_out = dout("logits_out", [NTOK, 8])
    x3_dram = nc.dram_tensor("x3_scr", [D, NTOK], F32, kind="Internal").ap()
    attn_dram = nc.dram_tensor("attn_scr", [D, NTOK], BF16, kind="Internal").ap()

    with es:
        bufC = sb(nc, es, "bufC", [128, NCH, NTOK], BF16)
        q_sb = bufC
        ones_row_b = sb(nc, es, "ones_row_b", [1, 128], BF16)
        ones128_f = sb(nc, es, "ones128_f", [128, 1], F32)
        ones1_f = sb(nc, es, "ones1_f", [1, 128], F32)
        ones128_b = sb(nc, es, "ones128_b", [128, 1], BF16)
        ident_b = sb(nc, es, "ident_b", [128, 128], BF16)
        logit_stg = sb(nc, es, "logit_stg", [128, 8], F32)
        ldb2 = [sb(nc, es, f"ldb2_{i}", [128, HALF], BF16) for i in range(2)]
        bstg2 = [sb(nc, es, f"bstg2_{i}", [128, HALF], BF16) for i in range(2)]
        tstg2 = [sb(nc, es, f"tstg2_{i}", [128, 512], BF16) for i in range(2)]
        pA = ps(nc, es, "pA", [128, HALF])
        pB = ps(nc, es, "pB", [128, HALF])
        pC = ps(nc, es, "pC", [128, HALF])
        pD = ps(nc, es, "pD", [128, HALF])
        pE = ps(nc, es, "pE", [1, HALF])
        pF = ps(nc, es, "pF", [4, HALF])
        pT = ps(nc, es, "pT", [128, 128], BF16)
        scratch = dict(
            sq4=[sb(nc, es, f"lnsq4{i}", [128, 2, HALF], F32) for i in range(2)],
            tmp=sb(nc, es, "lntmp", [128, HALF], F32),
            p_sum=pE, p_sq=pF[0:1, :], p_mb=pC, p_rb=pD,
            stash_m=sb(nc, es, "stash_m", [1, NTOK], F32),
            stash_r=sb(nc, es, "stash_r", [1, NTOK], F32),
        )
        lnld4 = [sb(nc, es, f"lnld4{i}", [128, 2, HALF], F32) for i in range(2)]
        for nm in ("r_mean", "r_e2", "r_m2", "r_var", "r_sd", "r_rstd"):
            scratch[nm] = sb(nc, es, nm, [1, HALF], F32)
        es2 = contextlib.ExitStack()
        rhs_sym = sb(nc, es2, "rhs_sym", [65, 4, NTOK], BF16)
        rhs_ns = sb(nc, es2, "rhs_ns", [33, 8, NTOK], BF16)
        negm_van = sb(nc, es2, "negm_van", [1, 4, NTOK], BF16)
        rmx = sb(nc, es2, "rmx", [1, 1], F32)
        rmx4 = sb(nc, es2, "rmx4", [1, 4], F32)
        kmax_f = sb(nc, es2, "kmax_f", [1, 4], F32)
        rqmax_f = sb(nc, es2, "rqmax_f", [1, 12], F32)
        rkmax_f = sb(nc, es2, "rkmax_f", [1, 12], F32)
        absgs = sb(nc, es2, "absgs", [1, 12], F32)
        gsraw = sb(nc, es2, "gsraw", [1, 12], F32)
        gsh = sb(nc, es2, "gsh", [1, 12], F32)
        gs12_sb = sb(nc, es2, "gs12_sb", [12, 1], F32)
        gsel_sb = sb(nc, es2, "gsel_sb", [12, 512], F32)
        gsvec = sb(nc, es2, "gsvec", [32, 16], F32)
        qn_row = sb(nc, es2, "qn_row", [1, NTOK], F32)
        rqn_row = sb(nc, es2, "rqn_row", [1, NTOK], F32)
        rkn_row = sb(nc, es2, "rkn_row", [1, NTOK], F32)
        qn_s = sb(nc, es2, "qn_s", [1, NTOK], F32)
        rqn_s = sb(nc, es2, "rqn_s", [1, NTOK], F32)
        rkn_s = sb(nc, es2, "rkn_s", [1, NTOK], F32)
        m_a = sb(nc, es2, "m_a", [1, NTOK], F32)
        m_b = sb(nc, es2, "m_b", [1, NTOK], F32)
        m_c = sb(nc, es2, "m_c", [1, NTOK], F32)
        m_d = sb(nc, es2, "m_d", [1, NTOK], F32)
        m_e = sb(nc, es2, "m_e", [1, NTOK], F32)
        m_b2 = sb(nc, es2, "m_b2", [1, NTOK], F32)
        m_d2 = sb(nc, es2, "m_d2", [1, NTOK], F32)
        neg_t = sb(nc, es2, "neg_t", [1, 12], F32)
        kmaxs = sb(nc, es2, "kmaxs", [1, 4], F32)
        rqmaxs = sb(nc, es2, "rqmaxs", [1, 12], F32)
        rkmaxs = sb(nc, es2, "rkmaxs", [1, 12], F32)
        rowf = qn_row
        rq_h = sb(nc, es2, "rq_h", [32, NTOK], BF16)
        kT2 = [sb(nc, es2, f"kT{i}", [128, HALF], BF16) for i in range(2)]
        vT2 = [sb(nc, es2, f"vT{i}", [128, 4, 128], BF16) for i in range(2)]
        symL2 = [sb(nc, es2, f"symL{i}", [65, 4, HALF], BF16) for i in range(2)]
        nsL2 = [sb(nc, es2, f"nsL{i}", [33, 4, HALF], BF16) for i in range(2)]
        mask_sb = sb(nc, es2, "mask_sb", [128, 4, HALF], BF16)
        pt2 = [sb(nc, es2, f"pt{i}", [128, HALF], BF16) for i in range(2)]
        rec_row = sb(nc, es2, "rec_row", [1, HALF], F32)
        sum_row = sb(nc, es2, "sum_row", [1, HALF], F32)
        bc_sb = sb(nc, es2, "bc_sb", [128, HALF], F32)
        pt4a = ps(nc, es, "pT4_0", [128, 512], BF16)
        pT4 = [pt4a, pt4a]
        p_S = [pA, pB]
        p_av = pD
        p_sums = pE
        p_bc = pC
        p_gv = pF[0:1, :]

        P = Prog(nc, es)
        consts = dict(ones128_f=ones128_f, ones1_f=ones1_f)

        ld_f = emit_dma_batch(P, "sync", [
            (gs12_sb[:], gs12[:, :]),
            (gsel_sb[:], c_gsel[:, :]),
            (ones128_f[:], c_ones128_f[:, :]),
            (ones1_f[:], c_ones1_f[:, :]),
            (ones128_b[:], c_ones128_b[:, :]),
            (ones_row_b[:], c_ones1_b[0:1, 0:128]),
            (ident_b[:], c_ident_b[:, :]),
            (nsL2[0][32:33, :, :], c_ones1_b[0:1, 0:2048].rearrange("o (a t) -> o a t", a=4)),
            (nsL2[1][32:33, :, :], c_ones1_b[0:1, 0:2048].rearrange("o (a t) -> o a t", a=4)),
            (symL2[0][64:65, :, :], c_ones1_b[0:1, 0:2048].rearrange("o (a t) -> o a t", a=4)),
            (symL2[1][64:65, :, :], c_ones1_b[0:1, 0:2048].rearrange("o (a t) -> o a t", a=4)),
        ])
        ld_q = emit_dma_batch(P, "gpsimd", [
            (q_sb[:], q_in[:, :].rearrange("(hh p) t -> p hh t", p=128)),
        ])
        for e in ("scalar", "tensor", "vector"):
            @P.step(e, waits={"sync": ld_f}, incs=0)
            def _(h, sem):
                pass

        def rowmax(src_ap, nrows, dst_tile):
            for r_ in range(nrows):
                for ch in range(4):
                    lr = emit_dma_batch(P, "sync",
                        [(rowf[:], src_ap[r_:r_+1, ch*NTOK:(ch+1)*NTOK])],
                        waits={"vector": P.now("vector")})
                    @P.step("vector", waits={"sync": lr}, incs=1)
                    def _(h, sem):
                        h.reduce_max(rmx[:], rowf[:], axis=AX.X).then_inc(sem, 1)

                    @P.step("vector", waits={"vector": P.now("vector")}, incs=1)
                    def _(h, sem, ch=ch):
                        h.tensor_copy(rmx4[0:1, ch:ch+1], rmx[:]).then_inc(sem, 1)

                @P.step("vector", waits={"vector": P.now("vector")}, incs=1)
                def _(h, sem):
                    h.reduce_max(rmx[:], rmx4[:], axis=AX.X).then_inc(sem, 1)

                @P.step("vector", waits={"vector": P.now("vector")}, incs=1)
                def _(h, sem, r_=r_):
                    h.tensor_copy(dst_tile[0:1, r_:r_+1], rmx[:]).then_inc(sem, 1)
        rowmax(kn2_full, KVH, kmax_f)
        rowmax(rqn2_full, 12, rqmax_f)
        rowmax(rkn2_full, 12, rkmax_f)

        @P.step("scalar", waits={"vector": P.now("vector")}, incs=1)
        def _(h, sem):
            h.activation(kmaxs[:], kmax_f[:], AF.Sqrt, bias=0.0, scale=1.0)
            h.activation(rqmaxs[:], rqmax_f[:], AF.Sqrt, bias=0.0, scale=1.0)
            h.activation(rkmaxs[:], rkmax_f[:], AF.Sqrt, bias=0.0, scale=1.0)
            h.drain().then_inc(sem, 1)

        lgs = emit_dma_batch(P, "sync", [(gsraw[:], gs12[:, :].rearrange("a o -> o a"))],
                             waits={})

        @P.step("vector", waits={"sync": lgs, "scalar": P.now("scalar")}, incs=1)
        def _(h, sem):
            h.tensor_scalar(neg_t[:], gsraw[:], -1.0, None, ALU.mult)
            h.drain()
            h.tensor_tensor(absgs[:], gsraw[:], neg_t[:], ALU.max)
            h.drain()
            h.tensor_scalar(gsh[:], absgs[:], 0.5, None, ALU.mult)
            h.drain().then_inc(sem, 1)

        for hh in range(12):
            @P.step("tensor", waits={"vector": P.now("vector")}, incs=1)
            def _(h, sem, hh=hh):
                h.matmul(pC[0:32, 0:1], gsel_sb[:, hh*32:(hh+1)*32],
                         gs12_sb[:], start=True, stop=True).then_inc(sem, 1)

            @P.step("vector", waits={"tensor": P.now("tensor")}, incs=1)
            def _(h, sem, hh=hh):
                h.tensor_copy(gsvec[:, hh:hh+1], pC[0:32, 0:1]).then_inc(sem, 1)

        for hh in range(NH):
            kvh = hh // 4
            pairs = [(qn_row[:], qn2_own[hh:hh+1, :])]
            if hh < 12:
                pairs.append((rqn_row[:], rqn2_own[hh:hh+1, :]))
                pairs.append((rq_h[:], rq_own[:, :].rearrange("(g r) t -> r g t", r=32)[:, hh, :]))
            if hh < 4:
                pairs.append((rkn_row[:], rkn2_own[hh:hh+1, :]))
            lm = emit_dma_batch(P, "sync", pairs, waits={"vector": P.now("vector")})

            @P.step("scalar", waits={"sync": lm, "vector": P.now("vector")}, incs=1)
            def _(h, sem, hh=hh):
                h.activation(qn_s[:], qn_row[:], AF.Sqrt, bias=0.0, scale=1.0)
                if hh < 12:
                    h.activation(rqn_s[:], rqn_row[:], AF.Sqrt, bias=0.0, scale=1.0)
                if hh < 4:
                    h.activation(rkn_s[:], rkn_row[:], AF.Sqrt, bias=0.0, scale=1.0)
                h.drain().then_inc(sem, 1)

            @P.step("vector", waits={"scalar": P.now("scalar")}, incs=1)
            def _(h, sem, hh=hh, kvh=kvh):
                h.tensor_scalar(m_a[:], qn_s[:], kmaxs[0:1, kvh:kvh+1], None, ALU.mult)
                mfin = m_a
                if hh < 12:
                    gfac = gsh if hh < 4 else absgs
                    h.tensor_scalar(m_b[:], rqn_s[:], rkmaxs[0:1, hh:hh+1],
                                    None, ALU.mult)
                    h.drain()
                    h.tensor_scalar(m_b2[:], m_b[:], gfac[0:1, hh:hh+1],
                                    None, ALU.mult)
                    h.drain()
                    h.tensor_tensor(m_c[:], m_a[:], m_b2[:], ALU.add)
                    mfin = m_c
                if hh < 4:
                    h.tensor_scalar(m_d[:], rkn_s[:], rqmaxs[0:1, hh:hh+1],
                                    None, ALU.mult)
                    h.drain()
                    h.tensor_scalar(m_d2[:], m_d[:], gsh[0:1, hh:hh+1],
                                    None, ALU.mult)
                    h.drain()
                    h.tensor_tensor(m_e[:], m_c[:], m_d2[:], ALU.add)
                    mfin = m_e
                h.drain()
                if hh < 4:
                    h.tensor_scalar(rhs_sym[64:65, hh, :], mfin[:], -1.0, None, ALU.mult)
                    ins = h.tensor_scalar(rhs_sym[0:32, hh, :], rq_h[:], gsvec[:, hh:hh+1],
                                          None, ALU.mult)
                elif hh < 12:
                    h.tensor_scalar(rhs_ns[32:33, hh - 4, :], mfin[:], -1.0, None, ALU.mult)
                    ins = h.tensor_scalar(rhs_ns[0:32, hh - 4, :], rq_h[:], gsvec[:, hh:hh+1],
                                          None, ALU.mult)
                else:
                    ins = h.tensor_scalar(negm_van[0:1, hh - 12, :], mfin[:], -1.0, None, ALU.mult)
                ins.then_inc(sem, 1)
        for hh in range(4):
            lm = emit_dma_batch(P, "sync", [
                (rq_h[:], rk_own[:, :].rearrange("(g r) t -> r g t", r=32)[:, hh, :]),
            ], waits={"vector": P.now("vector")})

            @P.step("vector", waits={"sync": lm}, incs=1)
            def _(h, sem, hh=hh):
                h.tensor_scalar(rhs_sym[32:64, hh, :], rq_h[:], gsvec[:, hh:hh+1],
                                None, ALU.mult).then_inc(sem, 1)
        v_rhs = P.now("vector")

        attn_dst = attn_dram[:, :].rearrange("(hh p) t -> p hh t", p=128)
        kfull_src = k_full
        vfull_src = v_full
        started = {}
        lag_t, lag_exp, lag_mask, lag_av, lag_avload = {}, {}, {}, {}, {}
        for qi in range(8):
            qcols = slice(qi * 128, (qi + 1) * 128)
            lmk = emit_dma_batch(P, "sync", [
                (mask_sb[:], masks[qi, :, :, :]),
            ], waits={"vector": P.now("vector"), "tensor": P.now("tensor")})
            for hg4 in range(4):
                kvh = hg4
                n_kcg = qi + 1
                first = [True]
                pend = None
                for kcg in range(n_kcg):
                    kb = kcg % 2
                    kcols = slice(kcg * 512, (kcg + 1) * 512)
                    pairs = [
                        (kT2[kb][:], kfull_src[kvh*128:(kvh+1)*128, kcols]),
                        (vT2[kb][:], vfull_src[:, :].rearrange("(cg c p) v -> p cg c v", p=128, c=4)[:, kcg, :, kvh*128:(kvh+1)*128]),
                    ]
                    if hg4 == 0:
                        pairs.append((symL2[kb][0:32, :, :],
                                      rk_full[:, :].rearrange("(g r) t -> r g t", r=32)[:, 0:4, kcols]))
                        pairs.append((symL2[kb][32:64, :, :],
                                      rq_full[:, :].rearrange("(g r) t -> r g t", r=32)[:, 0:4, kcols]))
                    elif hg4 in (1, 2):
                        pairs.append((nsL2[kb][0:32, :, :],
                                      rk_full[:, :].rearrange("(g r) t -> r g t", r=32)[:, 4*hg4:4*hg4+4, kcols]))
                    lk = emit_dma_batch(P, "gpsimd", pairs,
                                        waits={"tensor": max(lag_t.get(kb, 0),
                                                             lag_avload.get(kb, 0))})
                    for kc in range(4):
                        par = (kcg * 4 + kc) % 2
                        @P.step("tensor", waits={"gpsimd": lk,
                                                 "scalar": lag_exp.get(par, 0),
                                                 "sync": lmk,
                                                 "vector": v_rhs}, incs=1)
                        def _(h, sem, hg4=hg4, kb=kb, kc=kc, par=par, qcols=qcols):
                            h.matmul(p_S[par][:],
                                     kT2[kb][:, kc*128:(kc+1)*128],
                                     q_sb[:, 4*hg4:4*hg4+4, qcols],
                                     start=True, stop=False)
                            if hg4 == 0:
                                for hh in range(4):
                                    ins = h.matmul(p_S[par][:, hh*128:(hh+1)*128],
                                                   symL2[kb][:, hh, kc*128:(kc+1)*128],
                                                   rhs_sym[:, hh, qcols],
                                                   start=False, stop=(hh == 3))
                            elif hg4 in (1, 2):
                                for hh in range(4):
                                    ins = h.matmul(p_S[par][:, hh*128:(hh+1)*128],
                                                   nsL2[kb][:, hh, kc*128:(kc+1)*128],
                                                   rhs_ns[:, 4*(hg4-1)+hh, qcols],
                                                   start=False, stop=(hh == 3))
                            else:
                                ins = h.matmul(p_S[par][:], ones_row_b[:],
                                               negm_van[0:1, :, qcols],
                                               start=False, stop=True)
                            ins.then_inc(sem, 1)
                        t_after_S = P.now("tensor")
                        lag_t[kb] = t_after_S

                        if pend is not None:
                            pkb, pkc, ppar, pmasked = pend
                            key = (qi, hg4)
                            st_flag = key not in started
                            started[key] = True
                            wav = {"scalar": lag_exp[ppar]}
                            if pmasked:
                                wav["vector"] = lag_mask[ppar]
                            if st_flag:
                                wav["vector"] = P.now("vector")
                            @P.step("tensor", waits=wav, incs=1)
                            def _(h, sem, kb=pkb, kc=pkc, par=ppar, st=st_flag):
                                h.matmul(p_av[:], vT2[kb][:, kc, :], pt2[par][:],
                                         start=st, stop=False)
                                h.matmul(p_sums[:], ones128_b[:], pt2[par][:],
                                         start=st, stop=False).then_inc(sem, 1)
                            lag_av[ppar] = P.now("tensor")
                            lag_avload[pkb] = P.now("tensor")

                        ew = {"tensor": max(t_after_S, lag_av.get(par, 0))}
                        if lag_mask.get(par, 0):
                            ew["vector"] = lag_mask[par]
                        @P.step("scalar", waits=ew, incs=1)
                        def _(h, sem, par=par):
                            h.activation(pt2[par][:], p_S[par][:], AF.Exp, bias=0.0,
                                         scale=1.0).then_inc(sem, 1)
                        lag_exp[par] = P.now("scalar")

                        masked = (kcg == n_kcg - 1)
                        if masked:
                            @P.step("vector", waits={"scalar": lag_exp[par]}, incs=1)
                            def _(h, sem, par=par, kc=kc):
                                h.tensor_tensor(pt2[par][:], pt2[par][:],
                                                mask_sb[:, kc, :], ALU.mult).then_inc(sem, 1)
                            lag_mask[par] = P.now("vector")
                        pend = (kb, kc, par, masked)
                pkb, pkc, ppar, pmasked = pend
                key = (qi, hg4)
                st_flag = key not in started
                started[key] = True
                wav = {"scalar": lag_exp[ppar]}
                if pmasked:
                    wav["vector"] = lag_mask[ppar]
                if st_flag:
                    wav["vector"] = P.now("vector")
                @P.step("tensor", waits=wav, incs=1)
                def _(h, sem, kb=pkb, kc=pkc, par=ppar, st=st_flag):
                    h.matmul(p_av[:], vT2[kb][:, kc, :], pt2[par][:],
                             start=st, stop=True)
                    h.matmul(p_sums[:], ones128_b[:], pt2[par][:],
                             start=st, stop=True).then_inc(sem, 1)
                lag_av[ppar] = P.now("tensor")
                lag_avload[pkb] = P.now("tensor")
                del started[(qi, hg4)]
                @P.step("vector", waits={"tensor": P.now("tensor")}, incs=1)
                def _(h, sem):
                    h.tensor_scalar(sum_row[:], p_sums[:], 1e-37, None, ALU.max).then_inc(sem, 1)

                @P.step("vector", waits={"vector": P.now("vector")}, incs=1)
                def _(h, sem):
                    h.reciprocal(rec_row[:], sum_row[:]).then_inc(sem, 1)

                @P.step("tensor", waits={"vector": P.now("vector")}, incs=1)
                def _(h, sem):
                    h.matmul(p_bc[:], ones1_f[:], rec_row[:], start=True,
                             stop=True).then_inc(sem, 1)

                @P.step("vector", waits={"tensor": P.now("tensor"),
                                         "sync": P.now("sync")}, incs=1)
                def _(h, sem):
                    h.tensor_copy(bc_sb[:], p_bc[:]).then_inc(sem, 1)

                @P.step("vector", waits={"vector": P.now("vector"),
                                         "sync": P.now("sync")}, incs=1)
                def _(h, sem, hg4=hg4, qcols=qcols):
                    h.tensor_tensor(bstg2[hg4 % 2][:], p_av[:], bc_sb[:],
                                    ALU.mult).then_inc(sem, 1)

                @P.step("sync", waits={"vector": P.now("vector")}, incs=16)
                def _(h, sem, hg4=hg4, qi=qi):
                    h.dma_start(out=attn_dst[:, 4*hg4:4*hg4+4, qi*128:(qi+1)*128],
                                in_=bstg2[hg4 % 2][:].rearrange("p (a t) -> p a t", a=4)
                                ).then_inc(sem, 16)

        es2.close()
        hsb = sb(nc, es, "hsb", [128, 22, NTOK], BF16)
        wqs2 = [sb(nc, es, f"wqs{i}", [128, NCH, 128], BF16) for i in range(2)]
        ws3 = [sb(nc, es, f"ws3{i}", [128, NCH, 128], BF16) for i in range(2)]
        wqs2f = [sb(nc, es, f"wqsf{i}", [128, NCH, 128], F32) for i in range(2)]
        ws3f = [sb(nc, es, f"ws3f{i}", [128, NCH, 128], F32) for i in range(2)]
        wg1b = sb(nc, es, "wg1b", [128, NCH, 8], BF16)
        ld2 = [sb(nc, es, f"ld2_{i}", [128, HALF], F32) for i in range(2)]
        stg2 = [sb(nc, es, f"stg2_{i}", [128, HALF], F32) for i in range(2)]
        sil_t = sb(nc, es, "sil_t", [128, HALF], F32)
        sil_t2 = sb(nc, es, "sil_t2", [128, HALF], F32)
        @P.step("sync", waits={"sync": P.now("sync")}, incs=0)
        def _(h, sem):
            pass
        wo_src = wo_t[:, :].rearrange("(hh p) o -> p hh o", p=128)
        attn_src = attn_dram[:, :].rearrange("(hh p) t -> p hh t", p=128)
        x2_src = x2_t[:, :].rearrange("(k p) t -> p k t", p=128)
        x3_dst = x3_dram[:, :].rearrange("(k p) t -> p k t", p=128)
        ld_at = emit_dma_batch(P, "gpsimd", [
            (bufC[:], attn_src),
        ], waits={"sync": P.now("sync"), "tensor": P.now("tensor")})
        lagT, lagV, lagS, lagC = {}, {}, {}, {}
        for half in range(2):
            cols = slice(half * HALF, (half + 1) * HALF)
            for oc in range(NCH):
                i = half * NCH + oc
                pb = i % 2
                @P.step("sync", waits={"vector": lagC.get(pb, P.now("vector"))}, incs=16)
                def _(h, sem, pb=pb, oc=oc):
                    h.dma_start(out=wqs2f[pb][:], in_=wo_src[:, :, oc*128:(oc+1)*128]
                                ).then_inc(sem, 16)
                myld0 = P.now("sync")

                @P.step("vector", waits={"sync": myld0,
                                         "tensor": lagT.get(pb, P.now("tensor"))}, incs=1)
                def _(h, sem, pb=pb):
                    h.tensor_copy(wqs2[pb][:], wqs2f[pb][:]).then_inc(sem, 1)
                lagC[pb] = P.now("vector")

                @P.step("sync", waits={"vector": lagV.get(pb, 0)} if i >= 2 else {}, incs=16)
                def _(h, sem, pb=pb, oc=oc, cols=cols):
                    h.dma_start(out=ld2[pb][:], in_=x2_src[:, oc, cols]).then_inc(sem, 16)
                mysync = P.now("sync")

                @P.step("tensor", waits={"gpsimd": P.now("gpsimd"),
                                         "vector": P.now("vector")}, incs=1)
                def _(h, sem, pb=pb, oc=oc, cols=cols):
                    for k in range(NCH):
                        ins = h.matmul(p_S[pb][:], wqs2[pb][:, k, :], bufC[:, k, cols],
                                       start=(k == 0), stop=(k == NCH - 1))
                    ins.then_inc(sem, 1)
                lagT[pb] = P.now("tensor")

                @P.step("vector", waits={"tensor": lagT[pb], "sync": max(mysync, lagS.get(pb, 0))}, incs=1)
                def _(h, sem, pb=pb, cols=cols):
                    h.tensor_tensor(stg2[pb][:], p_S[pb][:], ld2[pb][:],
                                    ALU.add).then_inc(sem, 1)
                lagV[pb] = P.now("vector")

                @P.step("sync", waits={"vector": lagV[pb]}, incs=16)
                def _(h, sem, pb=pb, oc=oc, cols=cols):
                    h.dma_start(out=x3_dst[:, oc, cols], in_=stg2[pb][:]).then_inc(sem, 16)
                lagS[pb] = P.now("sync")

        @P.step("sync", waits={"sync": P.now("sync")}, incs=0)
        def _(h, sem):
            pass

        emit_ln_dram(P, x3_dram, bufC, consts, scratch, lnld4, gamma_beta=None, gsz=2)

        xn4tm_dst = xn4tm_out[:, :].rearrange("(g p) dd -> p g dd", p=128)
        lagTt, lagVt, lagSt = {}, {}, {}
        for k in range(NCH):
            for tg in range(2):
                u = 0
                @P.step("tensor", waits={"vector": lagVt.get(u, P.now("vector"))}, incs=1)
                def _(h, sem, k=k, tg=tg, u=u):
                    for j in range(4):
                        ins = h.transpose(pT4[u][:, j*128:(j+1)*128],
                                          bufC[:, k, (tg*4+j)*128:(tg*4+j+1)*128],
                                          ident_b[:])
                    ins.then_inc(sem, 1)
                lagTt[u] = P.now("tensor")

                @P.step("vector", waits={"tensor": lagTt[u],
                                         "sync": lagSt.get(u, 0)}, incs=1)
                def _(h, sem, u=u):
                    h.tensor_copy(tstg2[u][:], pT4[u][:]).then_inc(sem, 1)
                lagVt[u] = P.now("vector")

                @P.step("sync", waits={"vector": lagVt[u]}, incs=16)
                def _(h, sem, k=k, tg=tg, u=u):
                    h.dma_start(out=xn4tm_dst[:, tg*4:(tg+1)*4, k*128:(k+1)*128],
                                in_=tstg2[u][:].rearrange("p (g dd) -> p g dd", g=4)
                                ).then_inc(sem, 16)
                lagSt[u] = P.now("sync")

        ld_wg = emit_dma_batch(P, "gpsimd", [
            (wg1b[:], wg1[:, :].rearrange("(k p) r -> p k r", p=128)),
        ], waits={})
        for tt in range(8):
            @P.step("tensor", waits={"gpsimd": ld_wg, "vector": P.now("vector")}, incs=1)
            def _(h, sem, tt=tt):
                for k in range(NCH):
                    ins = h.matmul(pC[:, 0:8], bufC[:, k, tt*128:(tt+1)*128],
                                   wg1b[:, k, :], start=(k == 0), stop=(k == NCH - 1))
                ins.then_inc(sem, 1)

            @P.step("vector", waits={"tensor": P.now("tensor"),
                                     "sync": P.now("sync")}, incs=1)
            def _(h, sem):
                h.tensor_copy(logit_stg[:], pC[:, 0:8]).then_inc(sem, 1)

            @P.step("sync", waits={"vector": P.now("vector")}, incs=16)
            def _(h, sem, tt=tt):
                h.dma_start(out=logits_out[tt*128:(tt+1)*128, :],
                            in_=logit_stg[:]).then_inc(sem, 16)

        sw1_src = sw1[:, :].rearrange("(k p) o -> p k o", p=128)
        sw3_src = sw3[:, :].rearrange("(k p) o -> p k o", p=128)
        sw2_src = sw2[:, :].rearrange("(k p) o -> p k o", p=128)
        lagT2, lagV2, lagSc2, lagC2 = {}, {}, {}, {}
        for hc in range(22):
            wb = hc % 2
            lw0 = emit_dma_batch(P, "sync", [
                (wqs2f[wb][:], sw1_src[:, :, hc*128:(hc+1)*128]),
                (ws3f[wb][:], sw3_src[:, :, hc*128:(hc+1)*128]),
            ], waits={"vector": lagC2.get(wb, P.now("vector"))})

            @P.step("vector", waits={"sync": lw0,
                                     "tensor": lagT2.get(wb, P.now("tensor"))}, incs=1)
            def _(h, sem, wb=wb):
                h.tensor_copy(wqs2[wb][:], wqs2f[wb][:])
                h.tensor_copy(ws3[wb][:], ws3f[wb][:]).then_inc(sem, 1)
            lagC2[wb] = P.now("vector")
            lw = lagC2[wb]
            for half in range(2):
                cols = slice(half * HALF, (half + 1) * HALF)
                i = hc * 2 + half
                pX = pA if i % 2 == 0 else pC
                pY = pB if i % 2 == 0 else pD
                sl = sil_t if i % 2 == 0 else sil_t2
                @P.step("tensor", waits={"vector": max(lw, lagV2.get(i % 2, P.now("vector"))),
                                         "scalar": lagSc2.get(i % 2, 0)}, incs=1)
                def _(h, sem, wb=wb, cols=cols, pX=pX, pY=pY):
                    for k in range(NCH):
                        h.matmul(pX[:], wqs2[wb][:, k, :], bufC[:, k, cols],
                                 start=(k == 0), stop=(k == NCH - 1))
                    for k in range(NCH):
                        ins = h.matmul(pY[:], ws3[wb][:, k, :], bufC[:, k, cols],
                                       start=(k == 0), stop=(k == NCH - 1))
                    ins.then_inc(sem, 1)

                lagT2[wb] = P.now("tensor")

                @P.step("scalar", waits={"tensor": lagT2[wb]}, incs=1)
                def _(h, sem, pX=pX, sl=sl):
                    h.activation(sl[:], pX[:], AF.Silu, bias=0.0,
                                 scale=1.0).then_inc(sem, 1)
                lagSc2[i % 2] = P.now("scalar")

                @P.step("vector", waits={"scalar": lagSc2[i % 2]}, incs=1)
                def _(h, sem, hc=hc, cols=cols, pY=pY, sl=sl):
                    h.tensor_tensor(hsb[:, hc, cols], sl[:], pY[:],
                                    ALU.mult).then_inc(sem, 1)
                lagV2[i % 2] = P.now("vector")

        lagT3, lagV3, lagS3, lagC3 = {}, {}, {}, {}
        for half in range(2):
            cols = slice(half * HALF, (half + 1) * HALF)
            for oc in range(NCH):
                i = half * NCH + oc
                pb = i % 2
                @P.step("sync", waits={"vector": lagC3.get(pb, P.now("vector"))}, incs=32)
                def _(h, sem, pb=pb, oc=oc):
                    h.dma_start(out=wqs2f[pb][:, 0:NCH, :],
                                in_=sw2_src[:, 0:NCH, oc*128:(oc+1)*128]).then_inc(sem, 16)
                    h.dma_start(out=ws3f[pb][:, 0:6, :],
                                in_=sw2_src[:, NCH:22, oc*128:(oc+1)*128]).then_inc(sem, 16)
                myld0 = P.now("sync")

                @P.step("vector", waits={"sync": myld0,
                                         "tensor": lagT3.get(pb, P.now("tensor"))}, incs=1)
                def _(h, sem, pb=pb):
                    h.tensor_copy(wqs2[pb][:, 0:NCH, :], wqs2f[pb][:, 0:NCH, :])
                    h.tensor_copy(ws3[pb][:, 0:6, :], ws3f[pb][:, 0:6, :]).then_inc(sem, 1)
                lagC3[pb] = P.now("vector")

                @P.step("sync", waits={"vector": lagV3.get(pb, 0)} if i >= 2 else {}, incs=16)
                def _(h, sem, pb=pb, oc=oc, cols=cols):
                    h.dma_start(out=ld2[pb][:], in_=x3_dst[:, oc, cols]).then_inc(sem, 16)
                mysync = P.now("sync")

                @P.step("tensor", waits={"vector": P.now("vector")}, incs=1)
                def _(h, sem, pb=pb, oc=oc, cols=cols):
                    for k in range(22):
                        wt = wqs2[pb][:, k, :] if k < NCH else ws3[pb][:, k - NCH, :]
                        ins = h.matmul(p_S[pb][:], wt, hsb[:, k, cols],
                                       start=(k == 0), stop=(k == 21))
                    ins.then_inc(sem, 1)
                lagT3[pb] = P.now("tensor")

                @P.step("vector", waits={"tensor": lagT3[pb],
                                         "sync": max(mysync, lagS3.get(pb, 0))}, incs=1)
                def _(h, sem, pb=pb, cols=cols):
                    h.tensor_tensor(stg2[pb][:], p_S[pb][:], ld2[pb][:],
                                    ALU.add).then_inc(sem, 1)
                lagV3[pb] = P.now("vector")

                @P.step("sync", waits={"vector": lagV3[pb]}, incs=16)
                def _(h, sem, pb=pb, oc=oc, cols=cols):
                    h.dma_start(out=x3b_out[:, :].rearrange("(k p) t -> p k t", p=128)[:, oc, cols],
                                in_=stg2[pb][:]).then_inc(sem, 16)
                lagS3[pb] = P.now("sync")

        @P.step("gpsimd", waits={"sync": P.now("sync"), "gpsimd": P.now("gpsimd"),
                                 "vector": P.now("vector")}, incs=0)
        def _(h, sem):
            pass

        P.run()
    return nc


def build_p4():
    nc = bass.Bass("TRN2", target_bir_lowering=False)
    es = contextlib.ExitStack()
    def din(name, shape, dtype=F32):
        return nc.dram_tensor(name, shape, dtype, kind="ExternalInput").ap()
    def dout(name, shape, dtype=F32):
        return nc.dram_tensor(name, shape, dtype, kind="ExternalOutput").ap()

    TALL = B * S
    xn4_all = din("xn4_all", [TALL, D], BF16)
    logits_all = din("logits_all", [TALL, 8])
    w1e = din("w1e", [D, HMOE])
    w3e = din("w3e", [D, HMOE])
    w2e = din("w2e", [HMOE, D])
    eoff = din("eoff", [1, 1])
    c_ones1_f = din("c_ones1_f", [1, 128])
    c_ones8_f = din("c_ones8_f", [8, 1])
    c_evec = din("c_evec", [8, 1])
    c_ones128x8 = din("c_ones128x8", [128, 8])
    c_ident_b = din("c_ident_b", [128, 128], BF16)
    c_tokid = din("c_tokid", [128, 256], U32)
    c_bigidx = din("c_bigidx", [CAP, 4], U32)
    c_zeros2048 = din("c_zeros2048", [128, D], BF16)

    oe_out = dout("oe_out", [CAP, D], BF16)
    flat_out = dout("flat_out", [1, TALL])
    gates_out = dout("gates_out", [128, 64])
    keep_out = dout("keep_out", [1, TALL])
    idxmap_dram = nc.dram_tensor("idxmap", [CAP, 4], U32, kind="Internal").ap()

    with es:
        ones1_f = sb(nc, es, "ones1_f", [1, 128], F32)
        ones8_f = sb(nc, es, "ones8_f", [8, 1], F32)
        evec = sb(nc, es, "evec", [8, 1], F32)
        ones128x8 = sb(nc, es, "ones128x8", [128, 8], F32)
        ident_b = sb(nc, es, "ident_b", [128, 128], BF16)
        tokidT = sb(nc, es, "tokidT", [128, 256], U32)
        zeros2048 = sb(nc, es, "zeros2048", [128, D], BF16)
        eoff_sb = sb(nc, es, "eoff_sb", [1, 1], F32)
        xd_fm = sb(nc, es, "xd_fm", [128, NCH, NTOK], BF16)
        bstg2 = [sb(nc, es, f"bstg2_{i}", [128, HALF], BF16) for i in range(2)]
        sil_t = sb(nc, es, "sil_t", [128, HALF], F32)
        sil_t2 = sb(nc, es, "sil_t2", [128, HALF], F32)
        pA = ps(nc, es, "pA", [128, HALF])
        pB = ps(nc, es, "pB", [128, HALF])
        pC = ps(nc, es, "pC", [128, HALF])
        pE = ps(nc, es, "pE", [1, HALF])
        pD4 = ps(nc, es, "pD4", [128, HALF])
        pT = ps(nc, es, "pT", [128, 128], BF16)
        pT4 = [ps(nc, es, f"pT4_{i}", [128, 512], BF16) for i in range(2)]
        tstg2 = [sb(nc, es, f"tstg2_{i}", [128, 512], BF16) for i in range(2)]

        P = Prog(nc, es)
        ld_f = emit_dma_batch(P, "sync", [
            (ones1_f[:], c_ones1_f[:, :]),
            (ones8_f[:], c_ones8_f[:, :]),
            (evec[:], c_evec[:, :]),
            (ones128x8[:], c_ones128x8[:, :]),
            (ident_b[:], c_ident_b[:, :]),
            (tokidT[:], c_tokid[:, :]),
            (zeros2048[:], c_zeros2048[:, :]),
            (eoff_sb[:], eoff[:, :]),
        ])
        regs = {}
        for e in ("scalar", "tensor", "vector"):
            @P.step(e, waits={"sync": ld_f}, incs=0)
            def _(h, sem):
                pass

        @P.step("gpsimd", waits={"sync": ld_f}, incs=0)
        def _(h, sem):
            regs["b_cap"] = h.to_reg(CAP - 1)
            regs["b_tall"] = h.to_reg(B * S - 1)

        es2 = contextlib.ExitStack()
        lg_t = sb(nc, es2, "lg_t", [128, 8], F32)
        mx8 = sb(nc, es2, "mx8", [128, 1], F32)
        sh8 = sb(nc, es2, "sh8", [128, 8], F32)
        expd = sb(nc, es2, "expd", [128, 8], F32)
        se8 = sb(nc, es2, "se8", [128, 1], F32)
        rc8 = sb(nc, es2, "rc8", [128, 1], F32)
        ohd = sb(nc, es2, "ohd", [128, 8], F32)
        cs8 = sb(nc, es2, "cs8", [128, 8], F32)
        ohf = sb(nc, es2, "ohf", [128, 8], BF16)
        g_all = sb(nc, es2, "g_all", [128, 64], F32)
        ohT_seg = sb(nc, es2, "ohT_seg", [8, NTOK], F32)
        cs_seg = sb(nc, es2, "cs_seg", [8, NTOK], F32)
        ones_seg = sb(nc, es2, "ones_seg", [8, NTOK], F32)
        tmp_seg = sb(nc, es2, "tmp_seg", [8, NTOK], F32)
        tmp2_seg = sb(nc, es2, "tmp2_seg", [8, NTOK], F32)
        kp_seg = sb(nc, es2, "kp_seg", [8, NTOK], F32)
        kpo_seg = sb(nc, es2, "kpo_seg", [8, NTOK], F32)
        cs_init = sb(nc, es2, "cs_init", [8, 1], F32)
        flat0_r = sb(nc, es2, "flat0_r", [1, NTOK], F32)
        ktok_r = sb(nc, es2, "ktok_r", [1, NTOK], F32)
        t1_r = sb(nc, es2, "t1_r", [1, NTOK], F32)
        fsend_r = sb(nc, es2, "fsend_r", [1, NTOK], F32)
        myidx_r = sb(nc, es2, "myidx_r", [1, NTOK], F32)
        negm_r = sb(nc, es2, "negm_r", [1, NTOK], F32)
        myidx2_r = sb(nc, es2, "myidx2_r", [1, NTOK], F32)
        myidxT = sb(nc, es2, "myidxT", [128, 64], U32)
        idx_sb = sb(nc, es2, "idx_sb", [128, 8, 4], U32)
        xd_tm = sb(nc, es2, "xd_tm", [128, 8, D], BF16)

        @P.step("vector", waits={}, incs=1)
        def _(h, sem):
            h.memset(cs_init[:], 0.0)
            h.drain().then_inc(sem, 1)

        for seg in range(8):
            for tt in range(8):
                gt = seg * 8 + tt
                ll = emit_dma_batch(P, "sync", [
                    (lg_t[:], logits_all[gt*128:(gt+1)*128, :]),
                ], waits={"scalar": P.now("scalar"), "vector": P.now("vector")})

                @P.step("vector", waits={"sync": ll}, incs=1)
                def _(h, sem):
                    h.reduce_max(mx8[:], lg_t[:], axis=AX.X).then_inc(sem, 1)

                @P.step("vector", waits={"vector": P.now("vector")}, incs=1)
                def _(h, sem):
                    h.tensor_scalar(ohd[:], lg_t[:], mx8[:], None, ALU.is_ge)
                    h.tensor_scalar(sh8[:], lg_t[:], mx8[:], None, ALU.subtract)
                    h.drain().then_inc(sem, 1)

                @P.step("scalar", waits={"vector": P.now("vector")}, incs=1)
                def _(h, sem):
                    h.activation(expd[:], sh8[:], AF.Exp, bias=0.0,
                                 scale=1.0).then_inc(sem, 1)

                @P.step("vector", waits={"scalar": P.now("scalar")}, incs=1)
                def _(h, sem):
                    h.reduce_sum(se8[:], expd[:], axis=AX.X).then_inc(sem, 1)

                @P.step("vector", waits={"vector": P.now("vector")}, incs=1)
                def _(h, sem, gt=gt):
                    h.reciprocal(rc8[:], se8[:])
                    h.tensor_tensor_scan(cs8[:], ones128x8[:], ohd[:], 0.0,
                                         ALU.mult, ALU.add)
                    h.drain().then_inc(sem, 1)

                @P.step("vector", waits={"vector": P.now("vector")}, incs=1)
                def _(h, sem, gt=gt):
                    h.tensor_copy(g_all[:, gt:gt+1], rc8[:])
                    h.scalar_tensor_tensor(ohf[:], cs8[:], 1.0, ohd[:],
                                           ALU.is_equal, ALU.mult)
                    h.drain().then_inc(sem, 1)

                @P.step("tensor", waits={"vector": P.now("vector")}, incs=1)
                def _(h, sem):
                    h.transpose(pT[0:8, :], ohf[:], ident_b[:]).then_inc(sem, 1)

                @P.step("vector", waits={"tensor": P.now("tensor")}, incs=1)
                def _(h, sem, tt=tt):
                    h.tensor_copy(ohT_seg[:, tt*128:(tt+1)*128], pT[0:8, :]).then_inc(sem, 1)

            @P.step("vector", waits={"vector": P.now("vector")}, incs=1)
            def _(h, sem, seg=seg):
                h.tensor_scalar(ones_seg[:], ohT_seg[:], 0.0, 1.0, ALU.mult, ALU.add)
                h.drain()
                h.tensor_tensor_scan(cs_seg[:], ones_seg[:], ohT_seg[:], cs_init[:],
                                     ALU.mult, ALU.add)
                h.drain()
                h.tensor_copy(cs_init[:], cs_seg[:, NTOK-1:NTOK])
                h.tensor_scalar(tmp_seg[:], cs_seg[:], evec[:], None, ALU.add)
                h.tensor_scalar(kp_seg[:], cs_seg[:], float(CAP) + 0.5, None, ALU.is_le)
                h.drain()
                h.tensor_tensor(tmp2_seg[:], tmp_seg[:], ohT_seg[:], ALU.mult)
                h.tensor_tensor(kpo_seg[:], kp_seg[:], ohT_seg[:], ALU.mult)
                h.drain().then_inc(sem, 1)

            for half in range(2):
                @P.step("tensor", waits={"vector": P.now("vector")}, incs=1)
                def _(h, sem, half=half):
                    h.matmul(pE[:], ones8_f[:], tmp2_seg[:, half*HALF:(half+1)*HALF],
                             start=True, stop=True).then_inc(sem, 1)

                @P.step("vector", waits={"tensor": P.now("tensor")}, incs=1)
                def _(h, sem, half=half):
                    h.tensor_copy(flat0_r[0:1, half*HALF:(half+1)*HALF], pE[:]).then_inc(sem, 1)

                @P.step("tensor", waits={"vector": P.now("vector")}, incs=1)
                def _(h, sem, half=half):
                    h.matmul(pE[:], ones8_f[:], kpo_seg[:, half*HALF:(half+1)*HALF],
                             start=True, stop=True).then_inc(sem, 1)

                @P.step("vector", waits={"tensor": P.now("tensor")}, incs=1)
                def _(h, sem, half=half):
                    h.tensor_copy(ktok_r[0:1, half*HALF:(half+1)*HALF], pE[:]).then_inc(sem, 1)

            @P.step("vector", waits={"vector": P.now("vector")}, incs=1)
            def _(h, sem):
                h.tensor_scalar(t1_r[:], ktok_r[:], -BIG, BIG, ALU.mult, ALU.add)
                h.drain()
                h.tensor_tensor(fsend_r[:], flat0_r[:], t1_r[:], ALU.add)
                h.drain()
                h.tensor_scalar(myidx_r[:], fsend_r[:], eoff_sb[:], None, ALU.subtract)
                h.drain()
                h.tensor_scalar(negm_r[:], myidx_r[:], 0.0, None, ALU.is_lt)
                h.drain()
                h.scalar_tensor_tensor(myidx2_r[:], negm_r[:], 1.0e6, myidx_r[:],
                                       ALU.mult, ALU.add)
                h.drain().then_inc(sem, 1)

            st_f = emit_dma_batch(P, "sync", [
                (flat_out[0:1, seg*NTOK:(seg+1)*NTOK], fsend_r[:]),
                (keep_out[0:1, seg*NTOK:(seg+1)*NTOK], ktok_r[:]),
            ], waits={"vector": P.now("vector")})

            for g in range(8):
                @P.step("tensor", waits={"vector": P.now("vector")}, incs=1)
                def _(h, sem, g=g):
                    h.transpose(pC[:, 0:1], myidx2_r[0:1, g*128:(g+1)*128],
                                ones1_f[0:1, 0:1]).then_inc(sem, 1)

                @P.step("vector", waits={"tensor": P.now("tensor")}, incs=1)
                def _(h, sem, g=g, seg=seg):
                    h.tensor_copy(myidxT[:, seg*8+g:seg*8+g+1], pC[:, 0:1]).then_inc(sem, 1)

        st_g = emit_dma_batch(P, "sync", [
            (gates_out[:, :], g_all[:]),
        ], waits={"vector": P.now("vector")})

        ini = emit_dma_batch(P, "gpsimd", [
            (idxmap_dram[:, :], c_bigidx[:, :]),
        ], waits={})

        @P.step("gpsimd", waits={"gpsimd": ini, "vector": P.now("vector")}, incs=16*64)
        def _(h, sem):
            for g in range(64):
                h.indirect_dma_start(
                    out=idxmap_dram[:, :], out_offset=bass.IndirectOffsetOnAxis(
                        ap=myidxT[:, g:g+1], axis=0),
                    in_=tokidT[:, 4*g:4*g+4], in_offset=None,
                    bounds_check=regs["b_cap"], oob_is_err=False).then_inc(sem, 16)

        ld_idx = emit_dma_batch(P, "gpsimd", [
            (idx_sb[:, g, :], idxmap_dram[g*128:(g+1)*128, :]) for g in range(8)
        ], waits={"gpsimd": P.now("gpsimd")})
        zr = emit_dma_batch(P, "gpsimd", [
            (xd_tm[:, g, :], zeros2048[:, :]) for g in range(8)
        ], waits={})

        @P.step("gpsimd", waits={"gpsimd": P.now("gpsimd")}, incs=16*8)
        def _(h, sem):
            for g in range(8):
                h.indirect_dma_start(
                    out=xd_tm[:, g, :], out_offset=None, in_=xn4_all[:, :],
                    in_offset=bass.IndirectOffsetOnAxis(ap=idx_sb[:, g, 0:1], axis=0),
                    bounds_check=regs["b_tall"], oob_is_err=False).then_inc(sem, 16)
        g_xd = P.now("gpsimd")

        lagTt, lagVt = {}, {}
        for g in range(8):
            for kg in range(4):
                u = (g * 4 + kg) % 2
                w0 = {"gpsimd": g_xd} if (g == 0 and kg == 0) else {}
                w0["vector"] = lagVt.get(u, 0)
                @P.step("tensor", waits=w0, incs=1)
                def _(h, sem, g=g, kg=kg, u=u):
                    for j in range(4):
                        ins = h.transpose(pT4[u][:, j*128:(j+1)*128],
                                          xd_tm[:, g, (kg*4+j)*128:(kg*4+j+1)*128],
                                          ident_b[:])
                    ins.then_inc(sem, 1)
                lagTt[u] = P.now("tensor")

                @P.step("vector", waits={"tensor": lagTt[u]}, incs=1)
                def _(h, sem, g=g, kg=kg, u=u):
                    for j in range(4):
                        ins = h.tensor_copy(xd_fm[:, kg*4+j, g*128:(g+1)*128],
                                            pT4[u][:, j*128:(j+1)*128])
                    ins.then_inc(sem, 1)
                lagVt[u] = P.now("vector")

        es2.close()
        hb = sb(nc, es, "hb", [128, 22, NTOK], BF16)
        oe_fm = sb(nc, es, "oe_fm", [128, NCH, NTOK], BF16)
        wqs2 = [sb(nc, es, f"wqs{i}", [128, NCH, 128], BF16) for i in range(2)]
        ws3 = [sb(nc, es, f"ws3{i}", [128, NCH, 128], BF16) for i in range(2)]
        wqs2f = [sb(nc, es, f"wqsf{i}", [128, NCH, 128], F32) for i in range(2)]
        ws3f = [sb(nc, es, f"ws3f{i}", [128, NCH, 128], F32) for i in range(2)]
        w1_src = w1e[:, :].rearrange("(k p) o -> p k o", p=128)
        w3_src = w3e[:, :].rearrange("(k p) o -> p k o", p=128)
        w2_src = w2e[:, :].rearrange("(k p) o -> p k o", p=128)
        lagT2, lagV2, lagSc2, lagC2 = {}, {}, {}, {}
        for hc in range(22):
            wb = hc % 2
            lw0 = emit_dma_batch(P, "sync", [
                (wqs2f[wb][:], w1_src[:, :, hc*128:(hc+1)*128]),
                (ws3f[wb][:], w3_src[:, :, hc*128:(hc+1)*128]),
            ], waits={"vector": lagC2.get(wb, P.now("vector"))})

            @P.step("vector", waits={"sync": lw0,
                                     "tensor": lagT2.get(wb, P.now("tensor"))}, incs=1)
            def _(h, sem, wb=wb):
                h.tensor_copy(wqs2[wb][:], wqs2f[wb][:])
                h.tensor_copy(ws3[wb][:], ws3f[wb][:]).then_inc(sem, 1)
            lagC2[wb] = P.now("vector")
            for half in range(2):
                cols = slice(half * HALF, (half + 1) * HALF)
                i = hc * 2 + half
                pX = pA if i % 2 == 0 else pC
                pY = pB if i % 2 == 0 else pD4
                sl = sil_t if i % 2 == 0 else sil_t2
                @P.step("tensor", waits={"vector": max(lagC2[wb], lagV2.get(i % 2, P.now("vector"))),
                                         "scalar": lagSc2.get(i % 2, 0)}, incs=1)
                def _(h, sem, wb=wb, cols=cols, pX=pX, pY=pY):
                    for k in range(NCH):
                        h.matmul(pX[:], wqs2[wb][:, k, :], xd_fm[:, k, cols],
                                 start=(k == 0), stop=(k == NCH - 1))
                    for k in range(NCH):
                        ins = h.matmul(pY[:], ws3[wb][:, k, :], xd_fm[:, k, cols],
                                       start=(k == 0), stop=(k == NCH - 1))
                    ins.then_inc(sem, 1)
                lagT2[wb] = P.now("tensor")

                @P.step("scalar", waits={"tensor": lagT2[wb]}, incs=1)
                def _(h, sem, pX=pX, sl=sl):
                    h.activation(sl[:], pX[:], AF.Silu, bias=0.0,
                                 scale=1.0).then_inc(sem, 1)
                lagSc2[i % 2] = P.now("scalar")

                @P.step("vector", waits={"scalar": lagSc2[i % 2]}, incs=1)
                def _(h, sem, hc=hc, cols=cols, pY=pY, sl=sl):
                    h.tensor_tensor(hb[:, hc, cols], sl[:], pY[:],
                                    ALU.mult).then_inc(sem, 1)
                lagV2[i % 2] = P.now("vector")

        lagT3, lagV3, lagC3 = {}, {}, {}
        for half in range(2):
            cols = slice(half * HALF, (half + 1) * HALF)
            for oc in range(NCH):
                i = half * NCH + oc
                pb = i % 2
                @P.step("sync", waits={"vector": lagC3.get(pb, P.now("vector"))}, incs=32)
                def _(h, sem, pb=pb, oc=oc):
                    h.dma_start(out=wqs2f[pb][:, 0:NCH, :],
                                in_=w2_src[:, 0:NCH, oc*128:(oc+1)*128]).then_inc(sem, 16)
                    h.dma_start(out=ws3f[pb][:, 0:6, :],
                                in_=w2_src[:, NCH:22, oc*128:(oc+1)*128]).then_inc(sem, 16)
                myld0 = P.now("sync")

                @P.step("vector", waits={"sync": myld0,
                                         "tensor": lagT3.get(pb, P.now("tensor"))}, incs=1)
                def _(h, sem, pb=pb):
                    h.tensor_copy(wqs2[pb][:, 0:NCH, :], wqs2f[pb][:, 0:NCH, :])
                    h.tensor_copy(ws3[pb][:, 0:6, :], ws3f[pb][:, 0:6, :]).then_inc(sem, 1)
                lagC3[pb] = P.now("vector")

                @P.step("tensor", waits={"vector": max(lagC3[pb], lagV3.get(pb, P.now("vector")))}, incs=1)
                def _(h, sem, pb=pb, oc=oc, cols=cols):
                    for k in range(22):
                        wt = wqs2[pb][:, k, :] if k < NCH else ws3[pb][:, k - NCH, :]
                        ins = h.matmul(pA[:] if pb == 0 else pB[:], wt, hb[:, k, cols],
                                       start=(k == 0), stop=(k == 21))
                    ins.then_inc(sem, 1)
                lagT3[pb] = P.now("tensor")

                @P.step("vector", waits={"tensor": lagT3[pb]}, incs=1)
                def _(h, sem, pb=pb, oc=oc, cols=cols):
                    h.tensor_copy(oe_fm[:, oc, cols], pA[:] if pb == 0 else pB[:]).then_inc(sem, 1)
                lagV3[pb] = P.now("vector")

        oe_dst = oe_out[:, :].rearrange("(g p) dd -> p g dd", p=128)
        lagTt2, lagVt2, lagSt2 = {}, {}, {}
        for g in range(8):
            for kg in range(4):
                u = (g * 4 + kg) % 2
                @P.step("tensor", waits={"vector": lagVt2.get(u, P.now("vector"))}, incs=1)
                def _(h, sem, g=g, kg=kg, u=u):
                    for j in range(4):
                        ins = h.transpose(pT4[u][:, j*128:(j+1)*128],
                                          oe_fm[:, kg*4+j, g*128:(g+1)*128],
                                          ident_b[:])
                    ins.then_inc(sem, 1)
                lagTt2[u] = P.now("tensor")

                @P.step("vector", waits={"tensor": lagTt2[u],
                                         "sync": lagSt2.get(u, 0)}, incs=1)
                def _(h, sem, u=u):
                    h.tensor_copy(tstg2[u][:], pT4[u][:]).then_inc(sem, 1)
                lagVt2[u] = P.now("vector")

                @P.step("sync", waits={"vector": lagVt2[u]}, incs=16)
                def _(h, sem, g=g, kg=kg, u=u):
                    h.dma_start(out=oe_dst[:, g:g+1, kg*512:(kg+1)*512],
                                in_=tstg2[u][:].rearrange("p (o dd) -> p o dd", o=1)
                                ).then_inc(sem, 16)
                lagSt2[u] = P.now("sync")

        @P.step("gpsimd", waits={"sync": P.now("sync"), "gpsimd": P.now("gpsimd"),
                                 "vector": P.now("vector")}, incs=0)
        def _(h, sem):
            pass

        P.run()
    return nc


def build_p5():
    nc = bass.Bass("TRN2", target_bir_lowering=False)
    es = contextlib.ExitStack()
    def din(name, shape, dtype=F32):
        return nc.dram_tensor(name, shape, dtype, kind="ExternalInput").ap()
    def dout(name, shape, dtype=F32):
        return nc.dram_tensor(name, shape, dtype, kind="ExternalOutput").ap()

    TALL = B * S
    x3b_t = din("x3b_t", [D, NTOK])
    oe_all = din("oe_all", [TALL, D], BF16)
    flat_tm = din("flat_tm", [128, 8])
    gates_tm = din("gates_tm", [128, 8])
    keep_tm = din("keep_tm", [128, 8])
    c_ident_b = din("c_ident_b", [128, 128], BF16)
    x4_out = dout("x4_out", [D, NTOK])

    with es:
        ident_b = sb(nc, es, "ident_b", [128, 128], BF16)
        flat_sb = sb(nc, es, "flat_sb", [128, 8], F32)
        flat_cl = sb(nc, es, "flat_cl", [128, 8], F32)
        gates_sb = sb(nc, es, "gates_sb", [128, 8], F32)
        keep_sb = sb(nc, es, "keep_sb", [128, 8], F32)
        gk_sb = sb(nc, es, "gk_sb", [128, 8], F32)
        idx_u = sb(nc, es, "idx_u", [128, 8], U32)
        og_tm = sb(nc, es, "og_tm", [128, 8, D], BF16)
        og_gk = sb(nc, es, "og_gk", [128, 8, D], BF16)
        ld45 = [sb(nc, es, f"ld45_{i}", [128, 4, 128], F32) for i in range(2)]
        stg45 = [sb(nc, es, f"stg45_{i}", [128, 512], F32) for i in range(2)]
        pT4 = [ps(nc, es, f"pT4_{i}", [128, 512], BF16) for i in range(2)]

        P = Prog(nc, es)
        ld_f = emit_dma_batch(P, "sync", [
            (ident_b[:], c_ident_b[:, :]),
            (flat_sb[:], flat_tm[:, :]),
            (gates_sb[:], gates_tm[:, :]),
            (keep_sb[:], keep_tm[:, :]),
        ])
        @P.step("vector", waits={"sync": ld_f}, incs=1)
        def _(h, sem):
            h.tensor_tensor(gk_sb[:], gates_sb[:], keep_sb[:], ALU.mult)
            h.tensor_scalar(flat_cl[:], flat_sb[:], float(TALL - 1), None, ALU.min)
            h.drain()
            h.tensor_copy(idx_u[:], flat_cl[:])
            h.drain().then_inc(sem, 1)

        @P.step("gpsimd", waits={"vector": P.now("vector")}, incs=16*8)
        def _(h, sem):
            for g in range(8):
                h.indirect_dma_start(
                    out=og_tm[:, g, :], out_offset=None, in_=oe_all[:, :],
                    in_offset=bass.IndirectOffsetOnAxis(ap=idx_u[:, g:g+1], axis=0),
                    bounds_check=TALL - 1, oob_is_err=True).then_inc(sem, 16)
        g_og = P.now("gpsimd")

        for g in range(8):
            @P.step("vector", waits={"gpsimd": g_og} if g == 0 else
                    {"vector": P.now("vector")}, incs=1)
            def _(h, sem, g=g):
                h.tensor_scalar(og_gk[:, g, :], og_tm[:, g, :],
                                gk_sb[:, g:g+1], None, ALU.mult).then_inc(sem, 1)
        v_gk = P.now("vector")

        x3_src = x3b_t[:, :].rearrange("(k p) t -> p k t", p=128)
        x4_dst = x4_out[:, :].rearrange("(k p) t -> p k t", p=128)
        lagV5, lagS5, lagT5 = {}, {}, {}
        for g in range(8):
            for kg in range(4):
                i = g * 4 + kg
                u = i % 2
                @P.step("sync", waits={"vector": lagV5.get(u, v_gk)}, incs=16)
                def _(h, sem, u=u, g=g, kg=kg):
                    h.dma_start(out=ld45[u][:], in_=x3_src[:, kg*4:(kg+1)*4, g*128:(g+1)*128]
                                ).then_inc(sem, 16)
                mysync = P.now("sync")

                @P.step("tensor", waits={"vector": lagV5.get(u, v_gk)}, incs=1)
                def _(h, sem, g=g, kg=kg, u=u):
                    for j in range(4):
                        ins = h.transpose(pT4[u][:, j*128:(j+1)*128],
                                          og_gk[:, g, (kg*4+j)*128:(kg*4+j+1)*128],
                                          ident_b[:])
                    ins.then_inc(sem, 1)
                lagT5[u] = P.now("tensor")

                @P.step("vector", waits={"tensor": lagT5[u],
                                         "sync": max(mysync, lagS5.get(u, 0))}, incs=1)
                def _(h, sem, u=u):
                    h.tensor_tensor(stg45[u][:], pT4[u][:],
                                    ld45[u][:].rearrange("p a t -> p (a t)"),
                                    ALU.add).then_inc(sem, 1)
                lagV5[u] = P.now("vector")

                @P.step("sync", waits={"vector": lagV5[u]}, incs=16)
                def _(h, sem, u=u, g=g, kg=kg):
                    h.dma_start(out=x4_dst[:, kg*4:(kg+1)*4, g*128:(g+1)*128],
                                in_=stg45[u][:].rearrange("p (a t) -> p a t", a=4)
                                ).then_inc(sem, 16)
                lagS5[u] = P.now("sync")

        @P.step("gpsimd", waits={"sync": P.now("sync")}, incs=0)
        def _(h, sem):
            pass

        P.run()
    return nc



def core_chunks(core):
    cc = core % 4
    return [4 * m + cc for m in range(8)]


def core_token_ids(core):
    idx = []
    for c in core_chunks(core):
        idx.extend(range(128 * c, 128 * c + 128))
    return np.asarray(idx)


def host_slices(x, core):
    b = core // 4
    return x[b][core_token_ids(core)]


def bf16_np():
    import ml_dtypes
    return ml_dtypes.bfloat16


def rope_tables(core):
    pos = core_token_ids(core).astype(np.float64)
    inv = 1.0 / 10000 ** (np.arange(0, HD, 2) / HD)
    invd = np.concatenate([inv, inv])
    ang = invd[:, None] * pos[None, :]
    return (np.cos(ang).astype(np.float32),
            np.sin(ang).astype(np.float32))


def p2_inputs(z, core, res1):
    bf = bf16_np()
    b = core // 4
    cc = core % 4
    u_glob = np.zeros((8, S), np.float32)
    for c2 in range(4 * b, 4 * b + 4):
        u_glob[:, core_token_ids(c2)] = res1[c2]["u_out"]
    cos, sin = rope_tables(core)
    ind4 = np.zeros((128, 512), np.float32)
    ind4[:, cc * 128:(cc + 1) * 128] = np.eye(128)
    msel = np.zeros((8, 512), np.float32)
    for er in range(512):
        msel[er // 64, er] = 1.0
    hgsel = np.zeros((128, 4), np.float32)
    for j in range(4):
        hgsel[32 * j:32 * (j + 1), j] = 1.0
    return {
        "x_t": feat_major(host_slices(z["L0_attn_x"], core)).astype(np.float32),
        "xln1_in": res1[core]["xln1_out"],
        "xn_in": res1[core]["xn_out"],
        "u_all": u_glob,
        "dlogit": np.ascontiguousarray(z["L0_attn_decay_logit"][:, None]),
        "w_out": np.ascontiguousarray(z["L0_attn_W_out"]),
        "b_out_t": vec_col(z["L0_attn_b_out"]),
        "d_skip_t": vec_col(z["L0_attn_D_skip"]),
        "gamma_t": vec_col(z["L0_attn_ln_gamma"]),
        "beta_t": vec_col(z["L0_attn_ln_beta"]),
        "w_down": np.ascontiguousarray(z["L0_attn_W_down"]),
        "b_down": np.ascontiguousarray(z["L0_attn_b_down"][:, None]),
        "w_up": np.ascontiguousarray(z["L0_attn_W_up"]),
        "b_up_t": vec_col(z["L0_attn_b_up"]),
        "a_t": np.ascontiguousarray(z["L0_ffn_A"].reshape(512, D).T),
        "bw_t": np.ascontiguousarray(z["L0_ffn_Bw"].transpose(0, 2, 1).reshape(512, D)),
        "wg0_t": np.ascontiguousarray(z["L0_ffn_Wg"].T),
        "wq_t": np.ascontiguousarray(z["L1_attn_Wq"].T),
        "wk_t": np.ascontiguousarray(z["L1_attn_Wk"].T),
        "wv_t": np.ascontiguousarray(z["L1_attn_Wv"].T),
        "wrq_t": np.ascontiguousarray(z["L1_attn_Wrq"].T),
        "wrk_t": np.ascontiguousarray(z["L1_attn_Wrk"].T),
        "qnw": np.ascontiguousarray(z["L1_attn_q_norm_w"][:, None]),
        "knw": np.ascontiguousarray(z["L1_attn_k_norm_w"][:, None]),
        "cosq": cos, "sinq": sin,
        "c_ones128_f": np.ones((128, 1), np.float32),
        "c_ones1_f": np.ones((1, 128), np.float32),
        "c_ones8v": np.ones((8, S), np.float32),
        "c_ones128x8": np.ones((128, 8), np.float32),
        "c_ident_b": np.eye(128).astype(bf),
        "c_ind4": ind4.astype(bf),
        "c_msel": msel.astype(bf),
        "c_hgsel": hgsel,
    }


def p3_inputs(z, core, res2):
    bf = bf16_np()
    b = core // 4
    cc = core % 4
    cores_b = list(range(4 * b, 4 * b + 4))
    K_full = np.zeros((512, S), bf)
    V_full = np.zeros((S, 512), bf)
    Rq_full = np.zeros((384, S), bf)
    Rk_full = np.zeros((384, S), bf)
    kn2_full = np.zeros((KVH, S), np.float32)
    rqn2_full = np.zeros((12, S), np.float32)
    rkn2_full = np.zeros((12, S), np.float32)
    for c2 in cores_b:
        ids = core_token_ids(c2)
        K_full[:, ids] = res2[c2]["k_out"]
        V_full[ids, :] = res2[c2]["v_out"]
        Rq_full[:, ids] = res2[c2]["rq_out"]
        Rk_full[:, ids] = res2[c2]["rk_out"]
        kn2_full[:, ids] = res2[c2]["kn2_out"]
        rqn2_full[:, ids] = res2[c2]["rqn2_out"]
        rkn2_full[:, ids] = res2[c2]["rkn2_out"]
    masks = np.zeros((8, 128, 4, 512), np.float32)
    for i in range(8):
        qg = 128 * (4 * i + cc) + np.arange(128)
        for kc in range(4):
            kg = 512 * i + 128 * kc + np.arange(128)
            m = (kg[:, None] <= qg[None, :]).astype(np.float32)
            masks[i, :, kc, :] = np.tile(m, (1, 4))
    gsel = np.zeros((12, 512), np.float32)
    for hh in range(12):
        gsel[hh, 32*hh:32*(hh+1)] = 0.5 if hh < 4 else 1.0
    return {
        "x2_t": res2[core]["x2_out"],
        "q_in": res2[core]["q_out"],
        "k_full": K_full, "v_full": V_full,
        "rq_full": Rq_full, "rk_full": Rk_full,
        "rq_own": res2[core]["rq_out"], "rk_own": res2[core]["rk_out"],
        "qn2_own": res2[core]["qn2_out"],
        "kn2_full": kn2_full,
        "rqn2_own": res2[core]["rqn2_out"], "rkn2_own": res2[core]["rkn2_out"],
        "rqn2_full": rqn2_full, "rkn2_full": rkn2_full,
        "gs12": np.ascontiguousarray(z["L1_attn_graph_scale"][:, None]),
        "masks": masks.astype(bf),
        "wo_t": np.ascontiguousarray(z["L1_attn_Wo"].T),
        "wg1": np.ascontiguousarray(z["L1_ffn_Wg"]),
        "sw1": np.ascontiguousarray(z["L1_ffn_sw1"]),
        "sw3": np.ascontiguousarray(z["L1_ffn_sw3"]),
        "sw2": np.ascontiguousarray(z["L1_ffn_sw2"]),
        "c_ones128_f": np.ones((128, 1), np.float32),
        "c_ones1_f": np.ones((1, 128), np.float32),
        "c_ones128_b": np.ones((128, 1), bf),
        "c_ones1_b": np.ones((1, 2048), bf),
        "c_ident_b": np.eye(128).astype(bf),
        "c_gsel": gsel,
    }


def p4_inputs(z, core, res3):
    bf = bf16_np()
    TALL = B * S
    xn4_all = np.zeros((TALL, D), bf)
    logits_all = np.zeros((TALL, 8), np.float32)
    for c2 in range(8):
        gids = 4096 * (c2 // 4) + core_token_ids(c2)
        xn4_all[gids] = res3[c2]["xn4tm_out"]
        logits_all[gids] = res3[c2]["logits_out"]
    tokid = (np.arange(64)[None, :] * 128 + np.arange(128)[:, None]).astype(np.uint32)
    tokid = np.repeat(tokid, 4, axis=1)
    return {
        "xn4_all": xn4_all,
        "logits_all": logits_all,
        "w1e": np.ascontiguousarray(z["L1_ffn_W1"][core]),
        "w3e": np.ascontiguousarray(z["L1_ffn_W3"][core]),
        "w2e": np.ascontiguousarray(z["L1_ffn_W2"][core]),
        "eoff": np.full((1, 1), 1024.0 * core, np.float32),
        "c_ones1_f": np.ones((1, 128), np.float32),
        "c_ones8_f": np.ones((8, 1), np.float32),
        "c_evec": (1024.0 * np.arange(8) - 1.0).astype(np.float32)[:, None],
        "c_ones128x8": np.ones((128, 8), np.float32),
        "c_ident_b": np.eye(128).astype(bf),
        "c_tokid": tokid,
        "c_bigidx": np.full((CAP, 4), 1_000_000_000, np.uint32),
        "c_zeros2048": np.zeros((128, D), bf),
    }


def p5_inputs(z, core, res3, res4):
    bf = bf16_np()
    TALL = B * S
    oe_all = np.zeros((TALL, D), bf)
    for e in range(8):
        oe_all[1024*e:1024*(e+1)] = res4[e]["oe_out"]
    gids = 4096 * (core // 4) + core_token_ids(core)
    flat_own = res4[0]["flat_out"][0, gids]
    keep_own = res4[0]["keep_out"][0, gids]
    cc = core % 4
    own_cols = [32 * (core // 4) + 4 * m + cc for m in range(8)]
    gates_tm = np.ascontiguousarray(res4[0]["gates_out"][:, own_cols])
    return {
        "x3b_t": res3[core]["x3b_out"],
        "oe_all": oe_all,
        "flat_tm": np.ascontiguousarray(flat_own.reshape(8, 128).T),
        "gates_tm": gates_tm,
        "keep_tm": np.ascontiguousarray(keep_own.reshape(8, 128).T),
        "c_ident_b": np.eye(128).astype(bf),
    }


def kernel(**inputs):
    z = inputs
    nc1 = build_p1()
    res1 = run_phase(nc1, [p1_inputs(z, c) for c in range(NCORE)])
    nc2 = build_p2()
    res2 = run_phase(nc2, [p2_inputs(z, c, res1) for c in range(NCORE)])
    nc3 = build_p3()
    res3 = run_phase(nc3, [p3_inputs(z, c, res2) for c in range(NCORE)])
    nc4 = build_p4()
    res4 = run_phase(nc4, [p4_inputs(z, c, res3) for c in range(NCORE)])
    nc5 = build_p5()
    res5 = run_phase(nc5, [p5_inputs(z, c, res3, res4) for c in range(NCORE)])
    out = np.zeros((B, S, D), np.float32)
    for c in range(NCORE):
        out[c // 4][core_token_ids(c)] = res5[c]["x4_out"].T
    return out


def p1_inputs(z, core):
    return {
        "x_t": feat_major(host_slices(z["L0_attn_x"], core)).astype(np.float32),
        "gamma_t": vec_col(z["L0_attn_ln_gamma"]),
        "beta_t": vec_col(z["L0_attn_ln_beta"]),
        "w_in": np.ascontiguousarray(z["L0_attn_W_in"]),
        "b_in": np.ascontiguousarray(z["L0_attn_b_in"][:, None]),
        "c_ones128_f": np.ones((128, 1), np.float32),
        "c_ones1_f": np.ones((1, 128), np.float32),
    }


def feat_major(a):
    return np.ascontiguousarray(a.T)


def vec_col(v):
    return np.ascontiguousarray(v.reshape(NCH, 128).T)


def run_phase(nc, in_maps):
    res = run_bass_kernel_spmd(nc, in_maps, core_ids=list(range(NCORE)))
    return res.results


if __name__ == "__main__":
    pass
